# version 16
# baseline (speedup 1.0000x reference)
import contextlib
import numpy as np
import concourse.bass as bass
import concourse.mybir as mybir

F32 = mybir.dt.float32
BF16 = mybir.dt.bfloat16
I32 = mybir.dt.int32
AF = mybir.ActivationFunctionType
OP = mybir.AluOpType
AX = mybir.AxisListType

T = 2048
D = 1024
HID = 4096
NT = T // 128
NK = D // 128
SCALE_A = 1.0 / 32.0
NEG = -1e30


class Rot:

    def __init__(self, mk, sem, n):
        self.mk, self.sem, self.n = mk, sem, n
        self.rel = []
        self.i = 0

    def acquire(self, eng):
        if self.i >= self.n:
            ev = self.rel[self.i - self.n]
            self.mk.w(eng, ev[0], ev[1])
        s = self.i % self.n
        self.i += 1
        return s

    def release(self, ev=None):
        if ev is None:
            ev = (self.sem, self.mk.cnt[self.sem])
        self.rel.append(ev)


class EngW:
    _PASS = {"wait_ge", "drain", "nop", "sem_inc", "sem_clear"}

    def __init__(self, eng):
        object.__setattr__(self, "_e", eng)
        object.__setattr__(self, "last", None)

    def __getattr__(self, n):
        f = getattr(self._e, n)
        if n in EngW._PASS or not callable(f):
            return f
        def call(*a, **k):
            r = f(*a, **k)
            object.__setattr__(self, "last", r)
            return r
        return call


class MK:
    def __init__(self, phases=("A", "B", "C", "D"), a_stop=99, c_stop=99):
        self.a_stop = a_stop
        self.c_stop = c_stop
        self.phases = phases
        nc = bass.Bass("TRN2", target_bir_lowering=False)
        self.nc = nc
        self.st = contextlib.ExitStack()
        self.cnt = {}
        self.sem = {}
        for name in ["dS", "dA", "dG", "sT", "sV", "sS",
                     "X0", "X1", "XR0", "XR1", "W0", "W1", "W2", "W3",
                     "SH", "CST", "SO0", "SO1", "SO2", "SO3"]:
            self.sem[name] = self.st.enter_context(nc.semaphore(name))
            self.cnt[name] = 0
        self.V, self.S, self.TE = EngW(nc.vector), EngW(nc.scalar), EngW(nc.tensor)
        self.SY, self.G = EngW(nc.sync), EngW(nc.gpsimd)
        self._declare_io()

    def inc(self, res, name, n=1):
        res.then_inc(self.sem[name], n)
        self.cnt[name] += n
        return self.cnt[name]

    def bump(self, name, n):
        self.cnt[name] += n
        return self.cnt[name]

    def w(self, eng, name, v):
        if v > 0:
            eng.wait_ge(self.sem[name], v)

    def dma(self, eng, out, in_, slot):
        r = eng.dma_start(out=out, in_=in_)
        r.then_inc(self.sem[slot], 16)
        self.cnt[slot] += 16
        return (slot, self.cnt[slot])

    def _declare_io(self):
        nc = self.nc
        dt_ = lambda n, shp, dt=F32, kind="ExternalInput": nc.dram_tensor(n, shp, dt, kind=kind).ap()
        self.d_x = dt_("x", [T, D])
        self.d_y = dt_("y", [T, D], kind="ExternalOutput")
        self.d_wqk = dt_("wqk", [D, D])
        self.d_wva = dt_("wva", [D, D])
        self.d_alibi = dt_("alibi", [1, 1])
        self.d_wq = dt_("wq", [D, D])
        self.d_wk = dt_("wk", [D, D])
        self.d_wv = dt_("wv", [D, D])
        self.d_wo = dt_("wo", [D, D])
        self.d_bqF = dt_("bqF", [128, 8])
        self.d_bkF = dt_("bkF", [128, 8])
        self.d_bvrow = dt_("bvrow", [1, D])
        self.d_borow = dt_("borow", [1, D])
        self.d_beta = dt_("beta", [1, 1])
        self.d_w1t = dt_("w1t", [D, HID])
        self.d_w2t = dt_("w2t", [HID, D])
        self.d_identf = dt_("identf", [128, 128])
        self.d_identb = dt_("identb", [128, 128], BF16)
        self.d_iota128 = dt_("iota128", [128, T])
        self.d_iotacol = dt_("iotacol", [128, 1])
        self.d_amask = dt_("amask", [128, 128])
        self.d_triu4 = dt_("triu4", [128, 512])
        self.d_bd8 = dt_("bd8", [128, 8])
        self.d_bd8t = dt_("bd8t", [8, 128])
        self.d_ones1 = dt_("ones1", [1, 128])
        self.d_r1 = dt_("r1", [T, D], kind="Internal")
        self.d_r2 = dt_("r2", [T, D], kind="Internal")
        self.d_r3 = dt_("r3", [T, D], kind="Internal")
        self.d_w1q = dt_("w1q", [D, HID], BF16, kind="Internal")
        self.d_w2q = dt_("w2q", [HID, D], BF16, kind="Internal")
        self.d_s12 = dt_("s12", [1, 2], kind="Internal")
        self.dbg_xq = dt_("dbg_xq", [T, D], BF16, kind="Internal")
        self.dbg_h = dt_("dbg_h", [T, HID], kind="Internal")
        self.dbg_hq = dt_("dbg_hq", [T, HID], BF16, kind="Internal")
        self.dbg_am = dt_("dbg_am", [T, 4], kind="Internal")
        self.dbg_yq = dt_("dbg_yq", [T, D], kind="Internal")
        self.dbg_i1 = dt_("dbg_i1", [T, D], I32, kind="Internal")
        self.dbg_dl = dt_("dbg_dl", [T, D], kind="Internal")

    def sb(self, name, shape, dt=F32):
        return self.st.enter_context(self.nc.sbuf_tensor(name, shape, dt))

    def setup_consts(self):
        sb = self.sb
        self.identf = sb("c_identf", [128, 128])
        self.identb = sb("c_identb", [128, 128], BF16)
        self.iotacol = sb("c_iotacol", [128, 1])
        self.bd8 = sb("c_bd8", [128, 8])
        self.bd8t = sb("c_bd8t", [8, 128])
        self.ones1 = sb("c_ones1", [1, 128])
        self.slope_b = sb("c_slope_b", [128, 1])
        self.be_b = sb("c_be_b", [128, 1])
        self.bebq = sb("c_bebq", [128, 8])
        self.bebk = sb("c_bebk", [128, 8])
        self.alibi_sb = sb("c_alibi", [1, 1])
        self.beta_sb = sb("c_beta", [1, 1])
        self.bqF_sb = sb("c_bqF", [128, 8])
        self.bkF_sb = sb("c_bkF", [128, 8])
        self.s1b = sb("c_s1b", [128, 1])
        self.s2b = sb("c_s2b", [128, 1])
        self.is1b = sb("c_is1b", [128, 1])
        self.is2b = sb("c_is2b", [128, 1])
        self.epsb = sb("c_epsb", [128, 1])
        self.s12row = sb("c_s12row", [1, 2])

        with contextlib.ExitStack() as ph:
            ps_bc = ph.enter_context(self.nc.psum_tensor("c_ps", [128, 2], F32))
            with self.nc.Block() as blk:
                def body(_):
                    V, S, TE, SY = self.V, self.S, self.TE, self.SY
                    V.memset(self.epsb[:], 1e-5)
                    V.drain()
                    for dst, src in [(self.identf, self.d_identf), (self.identb, self.d_identb),
                                     (self.iotacol, self.d_iotacol), (self.bd8, self.d_bd8),
                                     (self.bd8t, self.d_bd8t), (self.ones1, self.d_ones1),
                                     (self.alibi_sb, self.d_alibi), (self.beta_sb, self.d_beta),
                                     (self.bqF_sb, self.d_bqF), (self.bkF_sb, self.d_bkF)]:
                        SY.dma_start(out=dst[:], in_=src[:]).then_inc(self.sem["dS"], 16)
                        self.bump("dS", 16)
                    t0 = self.cnt["dS"]
                    self.w(S, "dS", t0)
                    S.activation(self.be_b[0:1, :], self.beta_sb[:], AF.Exp)
                    s0 = self.inc(S.last, "sS")
                    self.w(TE, "dS", t0)
                    self.w(TE, "sS", s0)
                    TE.matmul(ps_bc[:, 0:1], self.ones1[:], self.alibi_sb[:], start=True, stop=True)
                    TE.matmul(ps_bc[:, 1:2], self.ones1[:], self.be_b[0:1, :], start=True, stop=True)
                    tt = self.inc(TE.last, "sT")
                    self.w(V, "sT", tt)
                    V.tensor_copy(self.slope_b[:], ps_bc[:, 0:1])
                    V.tensor_copy(self.be_b[:], ps_bc[:, 1:2])
                    V.drain()
                    V.tensor_scalar(self.bebq[:], self.bqF_sb[:], self.be_b[:], None, OP.mult)
                    V.tensor_scalar(self.bebk[:], self.bkF_sb[:], self.be_b[:], None, OP.mult)
                    self.inc(V.last, "sV")
                blk.vector(body)

    def emit_ln(self, xt, sc, quant=False, pre_wait=(), s_pre_wait=()):
        V, S = self.V, self.S
        for s_, v_ in pre_wait:
            self.w(V, s_, v_)
        V.reduce_sum(sc["s1"], xt, axis=AX.X)
        V.drain()
        V.tensor_scalar(sc["negm"], sc["s1"], -1.0 / D, None, OP.mult)
        V.drain()
        v1 = self.inc(V.last, "sV")
        for s_, v_ in s_pre_wait:
            self.w(S, s_, v_)
        self.w(S, "sV", v1)
        S.activation(sc["xc"], xt, AF.Identity, bias=sc["negm"], scale=1.0)
        S.drain()
        S.activation(sc["sqd"], sc["xc"], AF.Square, accum_out=sc["ssq"])
        S.drain()
        S.activation(sc["sdt"], sc["ssq"], AF.Sqrt, bias=self.epsb[:], scale=1.0 / D)
        S.drain()
        s1 = self.inc(S.last, "sS")
        self.w(V, "sS", s1)
        V.reciprocal(sc["rstd"], sc["sdt"])
        V.drain()
        if quant:
            V.reduce_max(sc["axc"], sc["xc"], axis=AX.X, apply_absolute_value=True)
            V.drain()
            V.tensor_scalar(sc["mxn"], sc["axc"], sc["rstd"], None, OP.mult)
            V.drain()
            V.tensor_scalar(sc["mxn"], sc["mxn"], 1e-5, None, OP.max)
            V.drain()
            V.reciprocal(sc["iam"], sc["mxn"])
            V.drain()
            V.tensor_scalar(sc["scq"], sc["iam"], 127.0, None, OP.mult)
            V.drain()
            V.tensor_scalar(sc["comb"], sc["scq"], sc["rstd"], None, OP.mult)
            V.drain()
        v2 = self.inc(V.last, "sV")
        return v2, s1

    def ln_scratch(self, ph, pfx, quant=False, sqd_ap=None, xc_ap=None):
        sbl = lambda n, shp, dt=F32: ph.enter_context(self.nc.sbuf_tensor(n, shp, dt))[:]
        sc = {"s1": sbl(pfx + "_s1", [128, 1]), "negm": sbl(pfx + "_negm", [128, 1]),
              "ssq": sbl(pfx + "_ssq", [128, 1]), "sdt": sbl(pfx + "_sdt", [128, 1]),
              "rstd": sbl(pfx + "_rstd", [128, 1])}
        sc["xc"] = xc_ap if xc_ap is not None else sbl(pfx + "_xc", [128, D])
        sc["sqd"] = sqd_ap if sqd_ap is not None else sbl(pfx + "_sqd", [128, D])
        if quant:
            for k in ["axc", "mxn", "iam", "scq", "comb"]:
                sc[k] = sbl(pfx + "_" + k, [128, 1])
        return sc

    def phase_A(self):
        nc = self.nc
        V, S, TE, SY, G = self.V, self.S, self.TE, self.SY, self.G
        with contextlib.ExitStack() as ph:
            sbl = lambda n, shp, dt=F32: ph.enter_context(nc.sbuf_tensor(n, shp, dt))
            psl = lambda n, shp, dt=F32: ph.enter_context(nc.psum_tensor(n, shp, dt))
            XnF = sbl("A_XnF", [128, NK, T])
            Vtm = sbl("A_Vtm", [128, NT, D])
            wbuf = sbl("A_w", [128, NK, 512])
            qF = sbl("A_qF", [128, NK, 512])
            Ssb = sbl("A_S", [128, T])
            Ssb2 = sbl("A_S2", [128, T])
            scr = sbl("A_scr", [128, T])
            PT = sbl("A_PT", [128, 2, 128])
            xt = sbl("A_xt", [128, 2, D])
            B0 = sbl("A_B0", [128, T])
            amask = sbl("A_amask", [128, 128])
            lnsc = self.ln_scratch(ph, "A_ln", sqd_ap=scr[:, 0:D], xc_ap=Ssb[:, 0:D])
            xn = scr[:, D:2*D]
            mx = sbl("A_mx", [128, 1])
            nmx = sbl("A_nmx", [128, 1])
            sume = sbl("A_sume", [128, 1])
            rs = sbl("A_rs", [128, 1])
            qbv = sbl("A_qbv", [128, 1])
            ps_tr = psl("A_ps_tr", [128, 2, 512])
            ps_v = psl("A_ps_v", [128, 2, 512])
            ps_s = psl("A_ps_s", [128, 2, 512])
            ps_o = psl("A_ps_o", [128, 2, 512])
            wflat = wbuf[:].rearrange("p k n -> p (k n)")

            with nc.Block() as blk:
                def body(_):
                    rot_xt = Rot(self, "sS", 2)
                    rot_tr = Rot(self, "sV", 2)
                    rot_pv = Rot(self, "sV", 2)
                    rot_ps = Rot(self, "sV", 2)
                    so_tgt = {0: 0, 1: 0}
                    rot_wo = Rot(self, "sT", 4)
                    ot_i = [0]

                    self.dma(SY, amask[:], self.d_amask[:], "CST")
                    cst_t = self.dma(SY, B0[:], self.d_iota128[:], "CST")
                    last_s_done = 0
                    xn_free = 0
                    for t in range(NT):
                        sl = rot_xt.acquire(SY)
                        xw = self.dma(SY, xt[:, sl, :], self.d_x[t*128:(t+1)*128, :],
                                      "X%d" % sl)
                        if t == 0:
                            self.w(V, *cst_t)
                            V.tensor_scalar(B0[:], B0[:], self.slope_b[:], None, OP.mult)
                            V.drain()
                            self.inc(V.last, "sV")
                        v_done, s_done = self.emit_ln(
                            xt[:, sl, :], lnsc,
                            pre_wait=[xw, ("sS", last_s_done)])
                        self.w(S, "sV", v_done)
                        self.w(S, "sT", xn_free)
                        S.activation(xn, lnsc["xc"], AF.Identity, scale=lnsc["rstd"])
                        S.drain()
                        s_xn = self.inc(S.last, "sS")
                        last_s_done = s_xn
                        rot_xt.release()
                        for k in range(NK):
                            trs = rot_tr.acquire(TE)
                            self.w(TE, "sS", s_xn)
                            TE.transpose(ps_tr[:, trs, 0:128], xn[:, k*128:(k+1)*128], self.identf[:])
                            tt_ = self.inc(TE.last, "sT")
                            self.w(V, "sT", tt_)
                            V.tensor_copy(XnF[:, k, t*128:(t+1)*128], ps_tr[:, trs, 0:128])
                            self.inc(V.last, "sV")
                            rot_tr.release()
                        xn_free = self.cnt["sT"]
                    xnf_all = self.cnt["sV"]
                    if self.a_stop <= 1:
                        self.w(G, "sV", self.cnt["sV"])
                        return

                    for half in range(2):
                        self.w(SY, "sT", self.cnt["sT"])
                        ww = self.dma(SY, wbuf[:],
                                      self.d_wva[:, half*512:(half+1)*512].rearrange(
                                          "(k p) n -> p k n", p=128), "W0")
                        for t in range(NT):
                            pvs = rot_pv.acquire(TE)
                            self.w(TE, *ww)
                            self.w(TE, "sV", xnf_all)
                            for k in range(NK):
                                TE.matmul(ps_v[:, pvs, :], XnF[:, k, t*128:(t+1)*128],
                                          wbuf[:, k, :], start=(k == 0), stop=(k == NK - 1))
                            tt_ = self.inc(TE.last, "sT")
                            self.w(V, "sT", tt_)
                            V.tensor_copy(Vtm[:, t, half*512:(half+1)*512], ps_v[:, pvs, :])
                            self.inc(V.last, "sV")
                            rot_pv.release()
                    a1b_T = self.cnt["sT"]
                    if self.a_stop <= 2:
                        self.w(G, "sV", self.cnt["sV"])
                        return

                    ssb_free = {0: 0, 1: 0}
                    prev_sm_s = 0
                    prev_ot_v = 0
                    ot2 = [scr[:, 0:D], scr[:, D:2*D]]
                    self.w(SY, "sT", a1b_T)
                    for qc in range(4):
                        for o in range(NK):
                            wos = rot_wo.acquire(SY)
                            ow = self.dma(SY, wflat[:, wos*1024:(wos+1)*1024].rearrange(
                                              "p (k n) -> p k n", n=128),
                                          self.d_wqk[:, o*128:(o+1)*128].rearrange(
                                              "(k p) n -> p k n", p=128), "W%d" % wos)
                            pvs = rot_pv.acquire(TE)
                            self.w(TE, *ow)
                            for k in range(NK):
                                TE.matmul(ps_v[:, pvs, :],
                                          wflat[:, wos*1024 + k*128: wos*1024 + (k+1)*128],
                                          XnF[:, k, qc*512:(qc+1)*512],
                                          start=(k == 0), stop=(k == NK - 1))
                            tt_ = self.inc(TE.last, "sT")
                            rot_wo.release()
                            self.w(V, "sT", tt_)
                            V.tensor_copy(qF[:, o, :], ps_v[:, pvs, :])
                            self.inc(V.last, "sV")
                            rot_pv.release()
                        qf_done = self.cnt["sV"]
                        if self.a_stop <= 3:
                            continue

                        for qt in range(4):
                            q0 = qc * 512 + qt * 128
                            qi = q0 // 128
                            Sbuf = [Ssb, Ssb2][qi % 2]
                            xrs = rot_xt.acquire(S)
                            rw = self.dma(S, xt[:, xrs, :], self.d_x[q0:q0+128, :],
                                          "XR%d" % xrs)
                            V.tensor_scalar(qbv[:], self.iotacol[:], float(q0),
                                            self.slope_b[:], OP.add, OP.mult)
                            V.drain()
                            qb_v = self.inc(V.last, "sV")
                            kmax = q0 + 128
                            nkb = (kmax + 511) // 512
                            for kb in range(nkb):
                                kw = min(512, kmax - kb * 512)
                                pss = rot_ps.acquire(TE)
                                self.w(TE, "sV", qf_done)
                                for dch in range(NK):
                                    TE.matmul(ps_s[:, pss, 0:kw],
                                              qF[:, dch, qt*128:(qt+1)*128],
                                              XnF[:, dch, kb*512:kb*512+kw],
                                              start=(dch == 0), stop=(dch == NK - 1))
                                tt_ = self.inc(TE.last, "sT")
                                self.w(V, "sT", tt_)
                                self.w(V, "sV", qb_v)
                                self.w(V, "sT", ssb_free[qi % 2])
                                V.tensor_scalar(Sbuf[:, kb*512:kb*512+kw], ps_s[:, pss, 0:kw],
                                                SCALE_A, qbv[:], OP.mult, OP.subtract)
                                V.drain()
                                V.tensor_tensor(Sbuf[:, kb*512:kb*512+kw],
                                                Sbuf[:, kb*512:kb*512+kw],
                                                B0[:, kb*512:kb*512+kw], OP.add)
                                V.drain()
                                self.inc(V.last, "sV")
                                rot_ps.release()
                            V.tensor_tensor(Sbuf[:, q0:q0+128], Sbuf[:, q0:q0+128],
                                            amask[:], OP.add)
                            V.drain()
                            V.reduce_max(mx[:], Sbuf[:, 0:kmax], axis=AX.X)
                            V.drain()
                            self.w(V, "sS", prev_sm_s)
                            V.tensor_scalar(nmx[:], mx[:], -1.0, None, OP.mult)
                            V.drain()
                            sm_v = self.inc(V.last, "sV")
                            self.w(S, "sV", sm_v)
                            S.activation(Sbuf[:, 0:kmax], Sbuf[:, 0:kmax], AF.Exp,
                                         bias=nmx[:], scale=1.0, accum_out=sume[:])
                            S.drain()
                            sm_s = self.inc(S.last, "sS")
                            prev_sm_s = sm_s
                            self.w(V, "sS", sm_s)
                            V.reciprocal(rs[:], sume[:])
                            V.drain()
                            rs_v = self.inc(V.last, "sV")
                            if self.a_stop <= 4:
                                self.w(V, "sV", rs_v)
                                V.tensor_copy(lnsc["negm"], rs[:])
                                self.inc(V.last, "sV")
                                rot_xt.release()
                                continue
                            if self.a_stop == 5:
                                ssb_free_T = self.cnt["sT"]
                            nb = q0 // 128 + 1
                            for kb2 in range(nb):
                                trs = rot_tr.acquire(TE)
                                self.w(TE, "sS", sm_s)
                                if kb2 == 0:
                                    self.w(TE, "sV", prev_ot_v)
                                TE.transpose(ps_tr[:, trs, 0:128], Sbuf[:, kb2*128:(kb2+1)*128],
                                             self.identf[:])
                                tt_ = self.inc(TE.last, "sT")
                                self.w(V, "sT", tt_)
                                V.tensor_copy(PT[:, trs, :], ps_tr[:, trs, 0:128])
                                ev = self.inc(V.last, "sV")
                                rot_tr.release()
                                if self.a_stop <= 5:
                                    if kb2 == nb - 1:
                                        pass
                                    continue
                                self.w(TE, "sV", ev)
                                for half in range(2):
                                    TE.matmul(ps_o[:, half, :], PT[:, trs, :],
                                              Vtm[:, kb2, half*512:(half+1)*512],
                                              start=(kb2 == 0), stop=(kb2 == nb - 1))
                            if self.a_stop <= 5:
                                ssb_free_T = self.cnt["sT"]
                                self.w(V, "sV", self.cnt["sV"])
                                V.tensor_copy(lnsc["negm"], rs[:])
                                self.inc(V.last, "sV")
                                rot_xt.release()
                                continue
                            ssb_free_T = self.inc(TE.last, "sT")
                            if self.a_stop <= 6:
                                self.w(V, "sT", ssb_free_T)
                                V.tensor_copy(lnsc["negm"], rs[:])
                                self.inc(V.last, "sV")
                                rot_xt.release()
                                continue
                            ots = ot_i[0] % 2
                            ot_i[0] += 1
                            self.w(V, "SO%d" % ots, so_tgt[ots])
                            self.w(V, "sT", ssb_free_T)
                            self.w(V, "sV", rs_v)
                            self.w(V, *rw)
                            for half in range(2):
                                V.tensor_scalar(ot2[ots][:, half*512:(half+1)*512],
                                                ps_o[:, half, :], rs[:], None, OP.mult)
                            V.drain()
                            V.tensor_tensor(ot2[ots], ot2[ots], xt[:, xrs, :], OP.add)
                            V.drain()
                            prev_ot_v = self.inc(V.last, "sV")
                            rot_xt.release(("sV", prev_ot_v))
                            if self.a_stop <= 7:
                                continue
                            self.w(SY, "sV", prev_ot_v)
                            _, so_tgt[ots] = self.dma(SY, self.d_r1[q0:q0+128, :], ot2[ots],
                                                      "SO%d" % ots)
                    self.w(G, "SO0", self.cnt["SO0"])
                    self.w(G, "SO1", self.cnt["SO1"])
                blk.vector(body)

    def phase_B(self, src, dst, first, tag):
        nc = self.nc
        V, S, TE, SY, G = self.V, self.S, self.TE, self.SY, self.G
        with contextlib.ExitStack() as ph:
            sbl = lambda n, shp, dt=F32: ph.enter_context(nc.sbuf_tensor(tag + n, shp, dt))
            psl = lambda n, shp, dt=F32: ph.enter_context(nc.psum_tensor(tag + n, shp, dt))
            W1 = sbl("_W1", [128, NK, HID], BF16)
            W2 = sbl("_W2", [128, HID // 128, D], BF16)
            if first:
                self.cache_stores = []
                self._ternarize(W1, W2, tag)
            else:
                with nc.Block() as blk0:
                    def body0(_):
                        self.w(SY, "dG", self.cnt["dG"])
                        self.dma(SY, W1[:],
                                 self.d_w1q[:].rearrange("(k p) n -> p k n", p=128), "CST")
                        lw = self.dma(SY, W2[:],
                                      self.d_w2q[:].rearrange("(k p) n -> p k n", p=128),
                                      "CST")
                        self.w(SY, *lw)
                    blk0.vector(body0)

            rt = sbl("_rt", [128, 4, D])
            lnsc = self.ln_scratch(ph, tag + "_ln", quant=True)
            qs = sbl("_qs", [128, 1024])
            qi = psl("_qi", [128, 1024], I32)
            qd = sbl("_qd", [128, 1024])
            qb = sbl("_qb", [128, 1024])
            xqm = sbl("_xqm", [128, D], BF16)
            XqF = sbl("_XqF", [128, NK, 128], BF16)
            hbuf = sbl("_h", [128, HID])
            am2 = sbl("_am2", [128, 1])
            am2c = sbl("_am2c", [128, 1])
            iam2 = sbl("_iam2", [128, 1])
            scq2 = sbl("_scq2", [128, 1])
            hqm = sbl("_hqm", [128, HID], BF16)
            HqF = sbl("_HqF", [128, HID // 128, 128], BF16)
            dq1 = sbl("_dq1", [128, 2])
            dq2 = sbl("_dq2", [128, 2])
            ot = sbl("_ot", [128, 1, D])
            ps_h = psl("_ps_h", [128, 2, 512])
            ps_o = psl("_ps_o", [128, 2, 512])
            ps_tb = psl("_ps_tb", [128, 2, 1024], BF16)

            def round_slices(src_ap, dst_ap, width, v_wait, s_wait):
                for j in range(width // 512):
                    ssl = src_ap[:, j*512:(j+1)*512]
                    self.w(S, "sV", s_wait)
                    S.copy(qi[:], ssl)
                    sq = self.inc(S.last, "sS")
                    self.w(V, "sS", sq)
                    V.tensor_tensor(qd[:], ssl, qi[:], OP.subtract)
                    V.drain()
                    V.scalar_tensor_tensor(qb[:], qd[:], 0.5, qi[:], OP.is_ge, OP.add)
                    V.drain()
                    V.scalar_tensor_tensor(dst_ap[:, j*512:(j+1)*512], qd[:], -0.5,
                                           qb[:], OP.is_lt, OP.subtract)
                    V.drain()
                    s_wait = self.inc(V.last, "sV")
                return self.cnt["sV"]

            levels = 1 if first else 3
            levels2 = 1
            with nc.Block() as blk:
                def body(_):
                    rot_rt = Rot(self, "sV", 4)
                    so_tgt = {0: 0, 1: 0}
                    ot_i = [0]
                    rot_h = Rot(self, "sS", 2)
                    rot_ov = Rot(self, "sV", 2)
                    rot_tb = Rot(self, "sV", 2)
                    st = {"pipe_s": 0, "pipe_v": 0, "xq_tr_T": 0, "hq_tr_T": 0,
                          "q2_free_v": 0, "v_q2": 0}
                    pend = []
                    rt_rel_idx = {}
                    s1 = {}

                    def stage1a(t):
                        sl = rot_rt.acquire(SY)
                        xw = self.dma(SY, rt[:, sl, :], src[t*128:(t+1)*128, :],
                                      ["X0", "X1", "XR0", "XR1"][sl])
                        self.emit_ln(
                            rt[:, sl, :], lnsc, quant=True,
                            pre_wait=[xw],
                            s_pre_wait=[("sS", st["pipe_s"]), ("sV", st["pipe_v"])])
                        V.tensor_scalar(dq1[:, t % 2:t % 2 + 1], lnsc["mxn"][:],
                                        1.0 / 127.0, self.s1b[:], OP.mult, OP.mult)
                        V.drain()
                        v_dq1 = self.inc(V.last, "sV")
                        self.w(S, "sV", v_dq1)
                        self.w(S, "sV", st["q2_free_v"])
                        S.activation(qs[:], lnsc["xc"], AF.Identity, scale=lnsc["comb"])
                        S.drain()
                        S.copy(qi[:], qs[:])
                        sq = self.inc(S.last, "sS")
                        self.w(V, "sS", sq)
                        V.tensor_tensor(qd[:], qs[:], qi[:], OP.subtract)
                        V.drain()
                        V.scalar_tensor_tensor(qb[:], qd[:], 0.5, qi[:], OP.is_ge, OP.add)
                        V.drain()
                        st["pipe_v"] = v_dq1
                        st["pipe_s"] = sq
                        s1[t] = {"sl": sl, "v_dq1": v_dq1}

                    def stage1f(t):
                        self.w(V, "sT", st["xq_tr_T"])
                        if levels == 1:
                            V.scalar_tensor_tensor(xqm[:], qd[:], -0.5,
                                                   qb[:], OP.is_lt, OP.subtract)
                            V.drain()
                        else:
                            V.scalar_tensor_tensor(qs[:], qd[:], 1.5, qb[:], OP.is_ge, OP.add)
                            V.drain()
                            V.scalar_tensor_tensor(qb[:], qd[:], 2.5, qs[:], OP.is_ge, OP.add)
                            V.drain()
                            V.scalar_tensor_tensor(qs[:], qd[:], -0.5, qb[:], OP.is_lt, OP.subtract)
                            V.drain()
                            V.scalar_tensor_tensor(qb[:], qd[:], -1.5, qs[:], OP.is_lt, OP.add)
                            V.drain()
                            V.scalar_tensor_tensor(xqm[:], qd[:], -2.5,
                                                   qb[:], OP.is_lt, OP.add)
                            V.drain()
                        s1[t]["vq"] = self.inc(V.last, "sV")
                        rot_rt.release()
                        rt_rel_idx[t] = len(rot_rt.rel) - 1

                    def xq_transposes(t):
                        for k in range(NK):
                            tbs = rot_tb.acquire(TE)
                            self.w(TE, "sV", s1[t]["vq"])
                            TE.transpose(ps_tb[:, tbs, 0:128], xqm[:, k*128:(k+1)*128],
                                         self.identb[:])
                            tt_ = self.inc(TE.last, "sT")
                            self.w(S, "sT", tt_)
                            S.copy(XqF[:, k, :], ps_tb[:, tbs, 0:128])
                            S.drain()
                            self.inc(S.last, "sS")
                            rot_tb.release(("sS", self.cnt["sS"]))
                        st["xq_tr_T"] = self.cnt["sT"]
                        s1[t]["xqf_done"] = ("sS", self.cnt["sS"])

                    def ffn1_gelu(t, q2=None):
                        for b in range(8):
                            if q2 is not None and b % 2 == 0:
                                q2(b // 2)
                            hs = rot_h.acquire(TE)
                            self.w(TE, *s1[t]["xqf_done"])
                            for k in range(NK):
                                TE.matmul(ps_h[:, hs, :], XqF[:, k, :],
                                          W1[:, k, b*512:(b+1)*512],
                                          start=(k == 0), stop=(k == NK - 1))
                            tt_ = self.inc(TE.last, "sT")
                            self.w(S, "sT", tt_)
                            self.w(S, "sV", s1[t]["v_dq1"])
                            if b == 0:
                                self.w(S, "sV", st["v_q2"])
                            S.activation(hbuf[:, b*512:(b+1)*512], ps_h[:, hs, :],
                                         AF.Gelu, scale=dq1[:, t % 2:t % 2 + 1])
                            self.inc(S.last, "sS")
                            rot_h.release()
                        s1[t]["gelu_done"] = self.cnt["sS"]

                    stage1a(0)
                    stage1f(0)
                    xq_transposes(0)
                    stage1a(1)
                    stage1f(1)
                    ffn1_gelu(0)
                    for t in range(NT):
                        sl = s1[t]["sl"]
                        if pend:
                            pend.pop(0)()
                        self.w(V, "sS", s1[t]["gelu_done"])
                        V.reduce_max(am2[:], hbuf[:], axis=AX.X, apply_absolute_value=True)
                        V.drain()
                        V.tensor_scalar(am2c[:], am2[:], 1e-5, None, OP.max)
                        V.drain()
                        V.reciprocal(iam2[:], am2c[:])
                        V.drain()
                        V.tensor_scalar(scq2[:], iam2[:], 127.0, None, OP.mult)
                        V.drain()
                        V.tensor_scalar(dq2[:, (t % 2):(t % 2)+1], am2c[:], 1.0 / 127.0,
                                        self.s2b[:], OP.mult, OP.mult)
                        V.drain()
                        v_q2 = self.inc(V.last, "sV")
                        st["v_q2"] = v_q2
                        vq_next = s1[t + 1]["vq"] if t + 1 < NT else 0
                        sv2_box = [max(v_q2, vq_next)]

                        def q2slice(j, t=t, sv2_box=sv2_box):
                            self.w(S, "sV", sv2_box[0])
                            S.activation(qs[:], hbuf[:, j*1024:(j+1)*1024],
                                         AF.Identity, scale=scq2[:])
                            S.drain()
                            S.copy(qi[:], qs[:])
                            sq = self.inc(S.last, "sS")
                            self.w(V, "sS", sq)
                            V.tensor_tensor(qd[:], qs[:], qi[:], OP.subtract)
                            V.drain()
                            V.scalar_tensor_tensor(qb[:], qd[:], 0.5, qi[:], OP.is_ge, OP.add)
                            V.drain()
                            if j == 0:
                                self.w(V, "sT", st["hq_tr_T"])
                            V.scalar_tensor_tensor(hqm[:, j*1024:(j+1)*1024], qd[:], -0.5,
                                                   qb[:], OP.is_lt, OP.subtract)
                            V.drain()
                            sv2_box[0] = self.inc(V.last, "sV")

                        if t + 1 < NT:
                            xq_transposes(t + 1)
                            ffn1_gelu(t + 1, q2=q2slice)
                        else:
                            for j in range(4):
                                q2slice(j)
                        st["q2_free_v"] = sv2_box[0]
                        v_hq = self.cnt["sV"]
                        for k in range(HID // 128):
                            tbs = rot_tb.acquire(TE)
                            self.w(TE, "sV", v_hq)
                            TE.transpose(ps_tb[:, tbs, 0:128], hqm[:, k*128:(k+1)*128],
                                         self.identb[:])
                            tt_ = self.inc(TE.last, "sT")
                            self.w(S, "sT", tt_)
                            S.copy(HqF[:, k, :], ps_tb[:, tbs, 0:128])
                            S.drain()
                            self.inc(S.last, "sS")
                            rot_tb.release(("sS", self.cnt["sS"]))
                        st["hq_tr_T"] = self.cnt["sT"]
                        hqf_done = ("sS", self.cnt["sS"])
                        ots = 0
                        ffn2_T = []
                        for half in range(2):
                            ovs = rot_ov.acquire(TE)
                            self.w(TE, *hqf_done)
                            for k in range(HID // 128):
                                TE.matmul(ps_o[:, ovs, :], HqF[:, k, :],
                                          W2[:, k, half*512:(half+1)*512],
                                          start=(k == 0), stop=(k == HID // 128 - 1))
                            ffn2_T.append((ovs, self.inc(TE.last, "sT")))
                        if t + 2 < NT:
                            stage1a(t + 2)
                            stage1f(t + 2)
                        if first:
                            for _ in range(3):
                                if self.cache_stores:
                                    dc, wsrc, ev = self.cache_stores.pop(0)
                                    self.w(SY, "sV", ev)
                                    SY.dma_start(out=dc, in_=wsrc).then_inc(
                                        self.sem["dG"], 16)
                                    self.bump("dG", 16)

                        def stageB(t=t, sl=sl, ots=ots, ffn2_T=ffn2_T):
                            first_ot = True
                            for half, (ovs, tt_) in enumerate(ffn2_T):
                                self.w(V, "sT", tt_)
                                if first_ot:
                                    self.w(V, "SO%d" % ots, so_tgt[ots])
                                    first_ot = False
                                V.tensor_scalar(ot[:, ots, half*512:(half+1)*512],
                                                ps_o[:, ovs, :], dq2[:, (t % 2):(t % 2)+1],
                                                None, OP.mult)
                                self.inc(V.last, "sV")
                                rot_ov.release()
                            V.drain()
                            V.tensor_tensor(ot[:, ots, :], ot[:, ots, :], rt[:, sl, :], OP.add)
                            V.drain()
                            ot_v = self.inc(V.last, "sV")
                            rot_rt.rel[rt_rel_idx[t]] = ("sV", ot_v)
                            self.w(SY, "sV", ot_v)
                            _, so_tgt[ots] = self.dma(SY, dst[t*128:(t+1)*128, :],
                                                      ot[:, ots, :], "SO%d" % ots)
                        pend.append(stageB)
                    while pend:
                        pend.pop(0)()
                    self.w(SY, "SO0", self.cnt["SO0"])
                    self.w(SY, "SO1", self.cnt["SO1"])
                    if first:
                        self.w(SY, "dG", self.cnt["dG"])
                blk.vector(body)

    def _ternarize(self, W1, W2, tag):
        nc = self.nc
        V, S, TE, SY, G = self.V, self.S, self.TE, self.SY, self.G
        with contextlib.ExitStack() as p2:
            sbl = lambda n, shp, dt=F32: p2.enter_context(nc.sbuf_tensor(tag + "T" + n, shp, dt))
            wtmp = sbl("_wtmp", [128, 2, HID])
            acc = sbl("_acc", [128, 1])
            part = sbl("_part", [128, 1])
            zt = sbl("_zt", [128, HID])
            tp = sbl("_tp", [128, HID])
            onec = sbl("_onec", [128, 1])
            psm = p2.enter_context(nc.psum_tensor(tag + "T_psm", [128, 2], F32))

            with nc.Block() as blk:
                def body(_):
                    V.memset(onec[:], 1.0)
                    V.memset(acc[:], 0.0)
                    V.drain()
                    self.inc(V.last, "sV")
                    rot_w = Rot(self, "sV", 2)
                    plans = [(self.d_w1t, HID, 8, 0), (self.d_w2t, D, 32, 1)]
                    for dram, width, ntile, idx in plans:
                        for i in range(ntile):
                            sl = rot_w.acquire(SY)
                            lw = self.dma(SY, wtmp[:, sl, 0:width],
                                          dram[i*128:(i+1)*128, :], "X%d" % sl)
                            self.w(V, *lw)
                            V.reduce_sum(part[:], wtmp[:, sl, 0:width], axis=AX.X,
                                         apply_absolute_value=True)
                            V.drain()
                            V.tensor_tensor(acc[:], acc[:], part[:], OP.add)
                            V.drain()
                            self.inc(V.last, "sV")
                            rot_w.release()
                        av = self.cnt["sV"]
                        self.w(TE, "sV", av)
                        TE.matmul(psm[0:1, idx:idx+1], acc[:], onec[:], start=True, stop=True)
                        tt_ = self.inc(TE.last, "sT")
                        self.w(V, "sT", tt_)
                        V.tensor_scalar(self.s12row[0:1, idx:idx+1], psm[0:1, idx:idx+1],
                                        1.0 / (D * HID), None, OP.mult)
                        V.drain()
                        V.memset(acc[:], 0.0)
                        V.drain()
                        self.inc(V.last, "sV")
                    V.tensor_scalar(self.s12row[:], self.s12row[:], 1e-5, None, OP.max)
                    V.drain()
                    sv = self.inc(V.last, "sV")
                    self.w(S, "sV", sv)
                    S.dma_start(out=self.d_s12[:], in_=self.s12row[:]).then_inc(self.sem["dG"], 16)
                    self.bump("dG", 16)
                    self.w(TE, "sV", sv)
                    TE.matmul(psm[:, 0:1], self.ones1[:], self.s12row[0:1, 0:1], start=True, stop=True)
                    TE.matmul(psm[:, 1:2], self.ones1[:], self.s12row[0:1, 1:2], start=True, stop=True)
                    tt_ = self.inc(TE.last, "sT")
                    self.w(V, "sT", tt_)
                    V.tensor_copy(self.s1b[:], psm[:, 0:1])
                    V.tensor_copy(self.s2b[:], psm[:, 1:2])
                    V.drain()
                    V.reciprocal(self.is1b[:], self.s1b[:])
                    V.reciprocal(self.is2b[:], self.s2b[:])
                    V.drain()
                    bv = self.inc(V.last, "sV")

                    plans2 = [(self.d_w1t, self.d_w1q, W1, HID, 8, self.is1b),
                              (self.d_w2t, self.d_w2q, W2, D, 32, self.is2b)]
                    zt_free = 0
                    for dram, dcache, Wsb, width, ntile, isb in plans2:
                        for i in range(ntile):
                            sl = rot_w.acquire(SY)
                            lw = self.dma(SY, wtmp[:, sl, 0:width],
                                          dram[i*128:(i+1)*128, :], "X%d" % sl)
                            self.w(S, *lw)
                            self.w(S, "sV", max(bv, zt_free))
                            S.activation(zt[:, 0:width], wtmp[:, sl, 0:width],
                                         AF.Identity, scale=isb[:])
                            s_z = self.inc(S.last, "sS")
                            self.w(V, "sS", s_z)
                            V.tensor_scalar(tp[:, 0:width], zt[:, 0:width], 0.5, None, OP.is_gt)
                            V.drain()
                            V.scalar_tensor_tensor(Wsb[:, i, :], zt[:, 0:width], -0.5,
                                                   tp[:, 0:width], OP.is_lt, OP.subtract)
                            V.drain()
                            wv_ = self.inc(V.last, "sV")
                            zt_free = wv_
                            rot_w.release()
                            self.cache_stores.append(
                                (dcache[i*128:(i+1)*128, :], Wsb[:, i, :], wv_))
                blk.vector(body)

    def phase_C(self):
        nc = self.nc
        V, S, TE, SY, G = self.V, self.S, self.TE, self.SY, self.G
        CH = 512
        NCH = T // CH
        NS = CH // 128
        with contextlib.ExitStack() as ph:
            sbl = lambda n, shp, dt=F32: ph.enter_context(nc.sbuf_tensor("C" + n, shp, dt))
            psl = lambda n, shp, dt=F32: ph.enter_context(nc.psum_tensor("C" + n, shp, dt))
            XnFc = sbl("_XnF", [128, NK, CH])
            qFd = sbl("_qF", [128, NK, CH])
            kFd = sbl("_kF", [128, NK, CH])
            qS = sbl("_qS", [128, NK, 128])
            kS = sbl("_kS", [128, NK, 128])
            ktok = sbl("_ktok", [128, NS, D])
            Vp = sbl("_Vp", [128, NS, 64 * 17])
            wbufC = sbl("_w", [128, 2, NK, 512])
            Oc = sbl("_Oc", [128, NS, D])
            OF = sbl("_OF", [128, NK, 128])
            rtC = sbl("_rt", [128, NS, D])
            stateSB = sbl("_st", [128, 16 * 68])
            triu4 = sbl("_triu4", [128, 512])
            PTb = sbl("_PT", [128, 2, 512])
            lnsc = self.ln_scratch(ph, "C_ln")
            xn = sbl("_xn", [128, D])
            rden = sbl("_rden", [8, 512])
            zero512 = sbl("_zero512", [128, 512])
            rd4 = sbl("_rd4", [128, 4])
            bvb = sbl("_bvb", [128, D])
            bob = sbl("_bob", [128, D])
            brow = sbl("_brow", [1, D])
            ps_quad = psl("_ps_quad", [128, 4, 512])
            ps_N = psl("_ps_N", [128, 2, 512])
            ps_st = psl("_ps_state", [128, 512])
            ps_d = psl("_ps_d", [128, 512])


            with nc.Block() as blk:
                def body(_):
                    rot_tr = Rot(self, "sV", 2)
                    rot_w = Rot(self, "sT", 2)
                    rot_p = Rot(self, "sV", 2)
                    rot_N = Rot(self, "sV", 2)
                    rot_PT = Rot(self, "sT", 2)
                    rot_ot = Rot(self, "dG", 2)
                    rot_rt = Rot(self, "sV", 4)

                    V.memset(zero512[:], 0.0)
                    V.drain()
                    V.memset(stateSB[:], 0.0)
                    V.drain()
                    V.memset(qS[96:128, :, :], 0.0)
                    V.drain()
                    V.memset(kS[96:128, :, :], 0.0)
                    V.drain()
                    self.inc(V.last, "sV")
                    self.dma(SY, triu4[:], self.d_triu4[:], "CST")
                    for row_d, dst in [(self.d_bvrow, bvb), (self.d_borow, bob)]:
                        self.w(SY, "sT", self.cnt["sT"])
                        bw = self.dma(SY, brow[:], row_d[:], "CST")
                        self.w(TE, *bw)
                        for half in range(2):
                            TE.matmul(ps_d[:, 0:512], self.ones1[:],
                                      brow[:, half*512:(half+1)*512], start=True, stop=True)
                            tt_ = self.inc(TE.last, "sT")
                            self.w(V, "sT", tt_)
                            V.tensor_copy(dst[:, half*512:(half+1)*512], ps_d[:, 0:512])
                            fv = self.inc(V.last, "sV")
                            self.w(TE, "sV", fv)
                    bias_done = self.cnt["sV"]

                    st_upd_T = 0
                    chunk_T = 0
                    last_s = 0

                    for c in range(NCH):
                        for s_ in range(NS):
                            t0 = c * NS + s_
                            sl = rot_rt.acquire(SY)
                            xw = self.dma(SY, rtC[:, sl, :],
                                          self.d_r2[t0*128:(t0+1)*128, :],
                                          ["X0", "X1", "XR0", "XR1"][sl])
                            v_done, s_done = self.emit_ln(
                                rtC[:, sl, :], lnsc,
                                pre_wait=[xw, ("sS", last_s), ("sT", chunk_T)])
                            self.w(S, "sV", v_done)
                            self.w(S, "sT", self.cnt["sT"])
                            S.activation(xn[:], lnsc["xc"], AF.Identity, scale=lnsc["rstd"])
                            S.drain()
                            s_xn = self.inc(S.last, "sS")
                            last_s = s_xn
                            for k in range(NK):
                                trs = rot_tr.acquire(TE)
                                self.w(TE, "sS", s_xn)
                                TE.transpose(ps_quad[:, trs, 0:128], xn[:, k*128:(k+1)*128],
                                             self.identf[:])
                                tt_ = self.inc(TE.last, "sT")
                                self.w(V, "sT", tt_)
                                V.tensor_copy(XnFc[:, k, s_*128:(s_+1)*128], ps_quad[:, trs, 0:128])
                                self.inc(V.last, "sV")
                                rot_tr.release()
                        xnf_done = self.cnt["sV"]
                        if self.c_stop <= 1:
                            continue

                        for half in range(2):
                            ws = rot_w.acquire(SY)
                            w_ld = self.dma(SY, wbufC[:, ws, :, :],
                                            self.d_wq[:, half*512:(half+1)*512].rearrange(
                                                "(k p) n -> p k n", p=128), "W%d" % ws)
                            for oo in range(4):
                                o = half * 4 + oo
                                pps = rot_p.acquire(TE)
                                self.w(TE, *w_ld)
                                self.w(TE, "sV", xnf_done)
                                for k in range(NK):
                                    TE.matmul(ps_quad[:, 2 + pps, :], wbufC[:, ws, k, oo*128:(oo+1)*128],
                                              XnFc[:, k, :], start=(k == 0), stop=(k == NK - 1))
                                tt_ = self.inc(TE.last, "sT")
                                self.w(S, "sT", tt_)
                                S.activation(qFd[:, o, :], ps_quad[:, 2 + pps, :], AF.Exp,
                                             bias=self.bebq[:, o:o+1], scale=self.be_b[:])
                                S.drain()
                                sev = self.inc(S.last, "sS")
                                rot_p.release(("sS", sev))
                            rot_w.release()
                        q_done_s = self.cnt["sS"]
                        if self.c_stop <= 2:
                            continue

                        knorm_v = 0
                        for half in range(2):
                            ws = rot_w.acquire(SY)
                            w_ld = self.dma(SY, wbufC[:, ws, :, :],
                                            self.d_wk[:, half*512:(half+1)*512].rearrange(
                                                "(k p) n -> p k n", p=128), "W%d" % ws)
                            for oo in range(4):
                                o = half * 4 + oo
                                pps = rot_p.acquire(TE)
                                self.w(TE, *w_ld)
                                self.w(TE, "sV", xnf_done)
                                for k in range(NK):
                                    TE.matmul(ps_quad[:, 2 + pps, :], wbufC[:, ws, k, oo*128:(oo+1)*128],
                                              XnFc[:, k, :], start=(k == 0), stop=(k == NK - 1))
                                tt_ = self.inc(TE.last, "sT")
                                self.w(S, "sT", tt_)
                                S.activation(kFd[:, o, :], ps_quad[:, 2 + pps, :], AF.Exp,
                                             bias=self.bebk[:, o:o+1], scale=self.be_b[:])
                                S.drain()
                                sev = self.inc(S.last, "sS")
                                self.w(TE, "sS", sev)
                                self.w(TE, "sV", knorm_v)
                                TE.matmul(ps_d[0:8, 0:512], self.bd8[:], kFd[:, o, :],
                                          start=True, stop=True)
                                t1 = self.inc(TE.last, "sT")
                                self.w(V, "sT", t1)
                                V.reciprocal(rden[:], ps_d[0:8, 0:512])
                                V.drain()
                                rv = self.inc(V.last, "sV")
                                self.w(TE, "sV", rv)
                                TE.matmul(ps_d[:, 0:512], self.bd8t[:], rden[:],
                                          start=True, stop=True)
                                t2 = self.inc(TE.last, "sT")
                                self.w(V, "sT", t2)
                                V.tensor_tensor(kFd[:, o, :], kFd[:, o, :], ps_d[:, 0:512],
                                                OP.mult)
                                V.drain()
                                knorm_v = self.inc(V.last, "sV")
                                rot_p.release()
                            rot_w.release()
                        k_done_v = self.cnt["sV"]
                        if self.c_stop <= 3:
                            continue

                        for s_ in range(NS):
                            for o in range(NK):
                                trs = rot_tr.acquire(TE)
                                self.w(TE, "sV", k_done_v)
                                TE.transpose(ps_quad[:, trs, 0:128], kFd[:, o, s_*128:(s_+1)*128],
                                             self.identf[:])
                                tt_ = self.inc(TE.last, "sT")
                                self.w(V, "sT", tt_)
                                V.tensor_copy(ktok[:, s_, o*128:(o+1)*128], ps_quad[:, trs, 0:128])
                                self.inc(V.last, "sV")
                                rot_tr.release()
                        ktok_done = self.cnt["sV"]
                        if self.c_stop <= 4:
                            continue

                        self.w(V, "sV", bias_done)
                        V.memset(Vp[:].rearrange("p s (h e) -> p s h e", e=17)[:, :, :, 16:17],
                                 1.0)
                        V.drain()
                        ones_v = self.inc(V.last, "sV")
                        for half in range(2):
                            ws = rot_w.acquire(SY)
                            w_ld = self.dma(SY, wbufC[:, ws, :, :],
                                            self.d_wv[:, half*512:(half+1)*512].rearrange(
                                                "(k p) n -> p k n", p=128), "W%d" % ws)
                            for s_ in range(NS):
                                pps = rot_p.acquire(TE)
                                self.w(TE, *w_ld)
                                self.w(TE, "sV", xnf_done)
                                for k in range(NK):
                                    TE.matmul(ps_quad[:, 2 + pps, :],
                                              XnFc[:, k, s_*128:(s_+1)*128],
                                              wbufC[:, ws, k, :],
                                              start=(k == 0), stop=(k == NK - 1))
                                tt_ = self.inc(TE.last, "sT")
                                self.w(V, "sT", tt_)
                                self.w(V, "sV", ones_v)
                                vview = Vp[:, s_, :].rearrange("p (h e) -> p h e", e=17)
                                pview = ps_quad[:, 2 + pps, :].rearrange("p (h e) -> p h e", e=16)
                                bview = (bvb[:, half*512:(half+1)*512]
                                         .rearrange("p (h e) -> p h e", e=16))
                                V.tensor_tensor(vview[:, half*32:(half+1)*32, 0:16],
                                                pview, bview, OP.add)
                                V.drain()
                                self.inc(V.last, "sV")
                                rot_p.release()
                            rot_w.release()
                        vp_done = self.cnt["sV"]
                        if self.c_stop <= 5:
                            continue
                        prev_pt_v = vp_done

                        scan_sub_T = 0
                        for s_ in range(NS):
                            cg = c * NS + s_
                            self.w(S, "sS", q_done_s)
                            self.w(S, "sV", vp_done)
                            self.w(S, "sT", scan_sub_T)
                            self.dma(S, qS[0:112, :, :],
                                     qFd[16:128, :, s_*128:(s_+1)*128], "SH")
                            shift_ld = self.dma(S, kS[0:112, :, :],
                                                kFd[16:128, :, s_*128:(s_+1)*128], "SH")
                            if cg > 0:
                                self.w(V, "sT", st_upd_T)
                                for a4 in range(4):
                                    V.tensor_copy(
                                        stateSB[32*a4:32*a4+16, :].rearrange(
                                            "p (g e) -> p g e", e=68)[:, :, 17*a4:17*a4+17],
                                        ps_st[32*a4:32*a4+16, 0:272].rearrange(
                                            "p (g e) -> p g e", e=17))
                                V.drain()
                                st_ev = self.inc(V.last, "sV")
                            else:
                                st_ev = self.cnt["sV"]
                            if cg == 0:
                                self.w(TE, "sV", st_ev)
                                TE.matmul(ps_st[0:128, 0:512], self.identf[:],
                                          zero512[:], start=True, stop=True)
                                st_upd_T = self.inc(TE.last, "sT")
                            for o in range(NK):
                                for par in range(2):
                                    g = 2 * o + par
                                    if par == 0:
                                        qT = qFd[:, o, s_*128:(s_+1)*128]
                                        kT = kFd[:, o, s_*128:(s_+1)*128]
                                    else:
                                        qT = qS[:, o, :]
                                        kT = kS[:, o, :]
                                    self.w(TE, "sS", q_done_s)
                                    self.w(TE, "sV", k_done_v)
                                    self.w(TE, *shift_ld)
                                    self.w(TE, "sV", prev_pt_v)
                                    for a in range(4):
                                        bofs = 32 * a
                                        TE.matmul(ps_quad[:, a, 0:128],
                                                  kT[bofs:bofs+16, :],
                                                  qT[bofs:bofs+16, :],
                                                  start=True, stop=True,
                                                  tile_position=(bofs, 0),
                                                  skip_group_check=True)
                                    t_s = self.inc(TE.last, "sT")
                                    pts = rot_PT.acquire(V)
                                    self.w(V, "sT", t_s)
                                    V.tensor_tensor(
                                        PTb[:, pts, :].rearrange("p (a e) -> p a e", e=128),
                                        ps_quad[:, 0:4, 0:128],
                                        triu4[:].rearrange("p (a e) -> p a e", e=128),
                                        OP.mult)
                                    V.drain()
                                    pt_v = self.inc(V.last, "sV")
                                    prev_pt_v = pt_v
                                    Ns = rot_N.acquire(TE)
                                    self.w(TE, "sV", pt_v)
                                    self.w(TE, "sV", vp_done)
                                    self.w(TE, "sV", st_ev)
                                    first = True
                                    if cg > 0:
                                        TE.matmul(ps_N[:, Ns, 0:68], qT,
                                                  stateSB[:, g*68:(g+1)*68],
                                                  start=True, stop=False,
                                                  skip_group_check=True)
                                        first = False
                                    for a in range(4):
                                        h = 8 * o + par + 2 * a
                                        TE.matmul(ps_N[:, Ns, a*17:(a+1)*17],
                                                  PTb[:, pts, a*128:(a+1)*128],
                                                  Vp[:, s_, h*17:(h+1)*17],
                                                  start=first, stop=(a == 3),
                                                  skip_group_check=True)
                                        first = False
                                    t_n = self.inc(TE.last, "sT")
                                    rot_PT.release()
                                    self.w(TE, "sV", ktok_done)
                                    for a in range(4):
                                        h = 8 * o + par + 2 * a
                                        bofs = 32 * a
                                        TE.matmul(ps_st[bofs:bofs+16, g*17:(g+1)*17],
                                                  ktok[:, s_, h*16:(h+1)*16],
                                                  Vp[:, s_, h*17:(h+1)*17],
                                                  start=False, stop=False,
                                                  tile_position=(0, bofs),
                                                  skip_group_check=True)
                                    st_upd_T = self.inc(TE.last, "sT")
                                    self.w(V, "sT", t_n)
                                    if o == 0 and par == 0 and c > 0 and self.c_stop > 7:
                                        self.w(V, "SO%d" % s_, ot_store[s_])
                                    nview = ps_N[:, Ns, 0:68].rearrange("p (h e) -> p h e", e=17)
                                    V.reciprocal(rd4[:], nview[:, 0:4, 16])
                                    V.drain()
                                    h0 = 8 * o + par
                                    ov = Oc[:, s_, :].rearrange(
                                        "p (hh e) -> p hh e", e=16)[:, h0:h0+7:2, :]
                                    rb = rd4[:].unsqueeze(-1).broadcast_to((128, 4, 16))
                                    V.tensor_tensor(ov, nview[:, 0:4, 0:16], rb, OP.mult)
                                    V.drain()
                                    self.inc(V.last, "sV")
                                    rot_N.release()
                            scan_sub_T = self.cnt["sT"]
                        oc_done = self.cnt["sV"]
                        if self.c_stop <= 7:
                            chunk_T = self.cnt["sT"]
                            continue

                        if c == 0:
                            ot_store = {}
                        w_lds = []
                        wss = []
                        for half in range(2):
                            ws = rot_w.acquire(SY)
                            w_lds.append(self.dma(
                                SY, wbufC[:, ws, :, :],
                                self.d_wo[:, half*512:(half+1)*512].rearrange(
                                    "(k p) n -> p k n", p=128), "W%d" % ws))
                            wss.append(ws)
                        for s_ in range(NS):
                            t0 = c * NS + s_
                            for k in range(NK):
                                trs = rot_tr.acquire(TE)
                                self.w(TE, "sV", oc_done)
                                TE.transpose(ps_quad[:, trs, 0:128], Oc[:, s_, k*128:(k+1)*128],
                                             self.identf[:])
                                tt_ = self.inc(TE.last, "sT")
                                self.w(V, "sT", tt_)
                                V.tensor_copy(OF[:, k, :], ps_quad[:, trs, 0:128])
                                self.inc(V.last, "sV")
                                rot_tr.release()
                            of_done = self.cnt["sV"]
                            for half in range(2):
                                pps = rot_p.acquire(TE)
                                self.w(TE, *w_lds[half])
                                self.w(TE, "sV", of_done)
                                for k in range(NK):
                                    TE.matmul(ps_quad[:, 2 + pps, :], OF[:, k, :],
                                              wbufC[:, wss[half], k, :],
                                              start=(k == 0), stop=(k == NK - 1))
                                tt_ = self.inc(TE.last, "sT")
                                self.w(V, "sT", tt_)
                                self.w(V, "sV", bias_done)
                                V.tensor_tensor(Oc[:, s_, half*512:(half+1)*512],
                                                ps_quad[:, 2 + pps, :],
                                                bob[:, half*512:(half+1)*512], OP.add)
                                V.drain()
                                self.inc(V.last, "sV")
                                rot_p.release()
                            V.tensor_tensor(Oc[:, s_, :], Oc[:, s_, :],
                                            rtC[:, s_, :], OP.add)
                            V.drain()
                            ot_v = self.inc(V.last, "sV")
                            rot_rt.release(("sV", ot_v))
                            self.w(SY, "sV", ot_v)
                            _, ot_store[s_] = self.dma(
                                SY, self.d_r3[t0*128:(t0+1)*128, :],
                                Oc[:, s_, :], "SO%d" % s_)
                        for half in range(2):
                            rot_w.release()
                        chunk_T = self.cnt["sT"]
                    for i4 in range(4):
                        self.w(SY, "SO%d" % i4, self.cnt["SO%d" % i4])
                blk.vector(body)


    def copy_dram(self, src, dst, tagn):
        nc = self.nc
        with contextlib.ExitStack() as ph:
            buf = ph.enter_context(nc.sbuf_tensor("cp%s" % tagn, [128, 2, D], F32))
            with nc.Block() as blk:
                def body(_):
                    so = {0: 0, 1: 0}
                    for t in range(NT):
                        sl = t % 2
                        self.w(self.SY, "SO%d" % sl, so[sl])
                        lw = self.dma(self.SY, buf[:, sl, :], src[t*128:(t+1)*128, :],
                                      "X%d" % sl)
                        self.w(self.S, *lw)
                        _, so[sl] = self.dma(self.S, dst[t*128:(t+1)*128, :], buf[:, sl, :],
                                             "SO%d" % sl)
                    self.w(self.S, "SO0", self.cnt["SO0"])
                    self.w(self.S, "SO1", self.cnt["SO1"])
                blk.vector(body)

    def build(self):
        self.setup_consts()
        if "A" in self.phases:
            self.phase_A()
        else:
            self.copy_dram(self.d_x, self.d_r1, "A")
        if "B" in self.phases:
            self.phase_B(self.d_r1, self.d_r2, first=True, tag="B")
        else:
            self.copy_dram(self.d_r1, self.d_r2, "B")
        if "C" in self.phases:
            self.phase_C()
        else:
            self.copy_dram(self.d_r2, self.d_r3, "C")
        if "D" in self.phases:
            self.phase_B(self.d_r3, self.d_y, first=("B" not in self.phases), tag="D")
        else:
            self.copy_dram(self.d_r3, self.d_y, "D")
        self.st.close()
        return self.nc


def host_consts():
    import ml_dtypes
    eye = np.eye(128, dtype=np.float32)
    eyeb = np.eye(128, dtype=ml_dtypes.bfloat16)
    iota128 = np.broadcast_to(np.arange(T, dtype=np.float32)[None, :], (128, T)).copy()
    iotacol = np.arange(128, dtype=np.float32)[:, None].copy()
    q = np.arange(128)[:, None]
    kk = np.arange(128)[None, :]
    amask = np.where(kk <= q, 0.0, NEG).astype(np.float32)
    tri = (np.arange(128)[:, None] <= np.arange(128)[None, :]).astype(np.float32)
    triu4 = np.concatenate([tri] * 4, axis=1).astype(np.float32)
    bd8 = np.zeros((128, 8), dtype=np.float32)
    for h in range(8):
        bd8[h*16:(h+1)*16, h] = 1.0
    bd8t = np.ascontiguousarray(bd8.T)
    ones1 = np.ones((1, 128), dtype=np.float32)
    return dict(identf=eye, identb=eyeb, iota128=iota128, iotacol=iotacol,
                amask=amask, triu4=triu4, bd8=bd8, bd8t=bd8t, ones1=ones1)


def stage_inputs(inputs, core):
    c = host_consts()
    f32 = np.float32
    m = dict(c)
    m["x"] = np.ascontiguousarray(np.asarray(inputs["L0_attn_a_x"], dtype=f32)[core])
    m["wqk"] = np.ascontiguousarray(np.asarray(inputs["L0_attn_a_W_qk"], dtype=f32))
    m["wva"] = np.ascontiguousarray(np.asarray(inputs["L0_attn_a_W_v"], dtype=f32).T)
    m["alibi"] = np.asarray(inputs["L0_attn_a_alibi_slope"], dtype=f32).reshape(1, 1).copy()
    m["wq"] = np.ascontiguousarray(np.asarray(inputs["L0_attn_b_Wq"], dtype=f32).T)
    m["wk"] = np.ascontiguousarray(np.asarray(inputs["L0_attn_b_Wk"], dtype=f32).T)
    m["wv"] = np.ascontiguousarray(np.asarray(inputs["L0_attn_b_Wv"], dtype=f32).T)
    m["wo"] = np.ascontiguousarray(np.asarray(inputs["L0_attn_b_Wo"], dtype=f32).T)
    m["bqF"] = np.ascontiguousarray(np.asarray(inputs["L0_attn_b_bq"], dtype=f32).reshape(8, 128).T)
    m["bkF"] = np.ascontiguousarray(np.asarray(inputs["L0_attn_b_bk"], dtype=f32).reshape(8, 128).T)
    m["bvrow"] = np.asarray(inputs["L0_attn_b_bv"], dtype=f32).reshape(1, D).copy()
    m["borow"] = np.asarray(inputs["L0_attn_b_bo"], dtype=f32).reshape(1, D).copy()
    m["beta"] = np.asarray(inputs["L0_attn_b_beta"], dtype=f32).reshape(1, 1).copy()
    m["w1t"] = np.ascontiguousarray(np.asarray(inputs["L0_ffn_w1"], dtype=f32).T)
    m["w2t"] = np.ascontiguousarray(np.asarray(inputs["L0_ffn_w2"], dtype=f32).T)
    return m


def kernel(**inputs):
    import numpy as np
    from concourse.bass_utils import run_bass_kernel_spmd
    n = 8
    nc = MK(phases=("A", "B", "C", "D")).build()
    in_maps = [stage_inputs(inputs, c) for c in range(n)]
    res = run_bass_kernel_spmd(nc, in_maps, core_ids=list(range(n)))
    return np.stack([np.asarray(res.results[c]["y"]) for c in range(n)], axis=0)


# version 17
# speedup vs baseline: 1.0288x; 1.0288x over previous
import contextlib
import numpy as np
import concourse.bass as bass
import concourse.mybir as mybir

F32 = mybir.dt.float32
BF16 = mybir.dt.bfloat16
I32 = mybir.dt.int32
AF = mybir.ActivationFunctionType
OP = mybir.AluOpType
AX = mybir.AxisListType

T = 2048
D = 1024
HID = 4096
NT = T // 128
NK = D // 128
SCALE_A = 1.0 / 32.0
NEG = -1e30


class Rot:

    def __init__(self, mk, sem, n):
        self.mk, self.sem, self.n = mk, sem, n
        self.rel = []
        self.i = 0

    def acquire(self, eng):
        if self.i >= self.n:
            ev = self.rel[self.i - self.n]
            self.mk.w(eng, ev[0], ev[1])
        s = self.i % self.n
        self.i += 1
        return s

    def release(self, ev=None):
        if ev is None:
            ev = (self.sem, self.mk.cnt[self.sem])
        self.rel.append(ev)


class EngW:
    _PASS = {"wait_ge", "drain", "nop", "sem_inc", "sem_clear"}

    def __init__(self, eng):
        object.__setattr__(self, "_e", eng)
        object.__setattr__(self, "last", None)

    def __getattr__(self, n):
        f = getattr(self._e, n)
        if n in EngW._PASS or not callable(f):
            return f
        def call(*a, **k):
            r = f(*a, **k)
            object.__setattr__(self, "last", r)
            return r
        return call


class MK:
    def __init__(self, phases=("A", "B", "C", "D"), a_stop=99, c_stop=99):
        self.a_stop = a_stop
        self.c_stop = c_stop
        self.phases = phases
        nc = bass.Bass("TRN2", target_bir_lowering=False)
        self.nc = nc
        self.st = contextlib.ExitStack()
        self.cnt = {}
        self.sem = {}
        for name in ["dS", "dA", "dG", "sT", "sV", "sS",
                     "X0", "X1", "XR0", "XR1", "W0", "W1", "W2", "W3",
                     "SH", "CST", "SO0", "SO1", "SO2", "SO3"]:
            self.sem[name] = self.st.enter_context(nc.semaphore(name))
            self.cnt[name] = 0
        self.V, self.S, self.TE = EngW(nc.vector), EngW(nc.scalar), EngW(nc.tensor)
        self.SY, self.G = EngW(nc.sync), EngW(nc.gpsimd)
        self._declare_io()

    def inc(self, res, name, n=1):
        res.then_inc(self.sem[name], n)
        self.cnt[name] += n
        return self.cnt[name]

    def bump(self, name, n):
        self.cnt[name] += n
        return self.cnt[name]

    def w(self, eng, name, v):
        if v > 0:
            eng.wait_ge(self.sem[name], v)

    def dma(self, eng, out, in_, slot):
        r = eng.dma_start(out=out, in_=in_)
        r.then_inc(self.sem[slot], 16)
        self.cnt[slot] += 16
        return (slot, self.cnt[slot])

    def _declare_io(self):
        nc = self.nc
        dt_ = lambda n, shp, dt=F32, kind="ExternalInput": nc.dram_tensor(n, shp, dt, kind=kind).ap()
        self.d_x = dt_("x", [T, D])
        self.d_y = dt_("y", [T, D], kind="ExternalOutput")
        self.d_wqk = dt_("wqk", [D, D])
        self.d_wva = dt_("wva", [D, D])
        self.d_alibi = dt_("alibi", [1, 1])
        self.d_wq = dt_("wq", [D, D])
        self.d_wk = dt_("wk", [D, D])
        self.d_wv = dt_("wv", [D, D])
        self.d_wo = dt_("wo", [D, D])
        self.d_bqF = dt_("bqF", [128, 8])
        self.d_bkF = dt_("bkF", [128, 8])
        self.d_bvrow = dt_("bvrow", [1, D])
        self.d_borow = dt_("borow", [1, D])
        self.d_beta = dt_("beta", [1, 1])
        self.d_w1t = dt_("w1t", [D, HID])
        self.d_w2t = dt_("w2t", [HID, D])
        self.d_identf = dt_("identf", [128, 128])
        self.d_identb = dt_("identb", [128, 128], BF16)
        self.d_iota128 = dt_("iota128", [128, T])
        self.d_iotacol = dt_("iotacol", [128, 1])
        self.d_amask = dt_("amask", [128, 128])
        self.d_triu4 = dt_("triu4", [128, 512])
        self.d_bd8 = dt_("bd8", [128, 8])
        self.d_bd8t = dt_("bd8t", [8, 128])
        self.d_ones1 = dt_("ones1", [1, 128])
        self.d_r1 = dt_("r1", [T, D], kind="Internal")
        self.d_r2 = dt_("r2", [T, D], kind="Internal")
        self.d_r3 = dt_("r3", [T, D], kind="Internal")
        self.d_w1q = dt_("w1q", [D, HID], BF16, kind="Internal")
        self.d_w2q = dt_("w2q", [HID, D], BF16, kind="Internal")
        self.d_s12 = dt_("s12", [1, 2], kind="Internal")
        self.dbg_xq = dt_("dbg_xq", [T, D], BF16, kind="Internal")
        self.dbg_h = dt_("dbg_h", [T, HID], kind="Internal")
        self.dbg_hq = dt_("dbg_hq", [T, HID], BF16, kind="Internal")
        self.dbg_am = dt_("dbg_am", [T, 4], kind="Internal")
        self.dbg_yq = dt_("dbg_yq", [T, D], kind="Internal")
        self.dbg_i1 = dt_("dbg_i1", [T, D], I32, kind="Internal")
        self.dbg_dl = dt_("dbg_dl", [T, D], kind="Internal")

    def sb(self, name, shape, dt=F32):
        return self.st.enter_context(self.nc.sbuf_tensor(name, shape, dt))

    def setup_consts(self):
        sb = self.sb
        self.identf = sb("c_identf", [128, 128])
        self.identb = sb("c_identb", [128, 128], BF16)
        self.iotacol = sb("c_iotacol", [128, 1])
        self.bd8 = sb("c_bd8", [128, 8])
        self.bd8t = sb("c_bd8t", [8, 128])
        self.ones1 = sb("c_ones1", [1, 128])
        self.slope_b = sb("c_slope_b", [128, 1])
        self.be_b = sb("c_be_b", [128, 1])
        self.bebq = sb("c_bebq", [128, 8])
        self.bebk = sb("c_bebk", [128, 8])
        self.alibi_sb = sb("c_alibi", [1, 1])
        self.beta_sb = sb("c_beta", [1, 1])
        self.bqF_sb = sb("c_bqF", [128, 8])
        self.bkF_sb = sb("c_bkF", [128, 8])
        self.s1b = sb("c_s1b", [128, 1])
        self.s2b = sb("c_s2b", [128, 1])
        self.is1b = sb("c_is1b", [128, 1])
        self.is2b = sb("c_is2b", [128, 1])
        self.epsb = sb("c_epsb", [128, 1])
        self.s12row = sb("c_s12row", [1, 2])

        with contextlib.ExitStack() as ph:
            ps_bc = ph.enter_context(self.nc.psum_tensor("c_ps", [128, 2], F32))
            with self.nc.Block() as blk:
                def body(_):
                    V, S, TE, SY = self.V, self.S, self.TE, self.SY
                    V.memset(self.epsb[:], 1e-5)
                    V.drain()
                    for dst, src in [(self.identf, self.d_identf), (self.identb, self.d_identb),
                                     (self.iotacol, self.d_iotacol), (self.bd8, self.d_bd8),
                                     (self.bd8t, self.d_bd8t), (self.ones1, self.d_ones1),
                                     (self.alibi_sb, self.d_alibi), (self.beta_sb, self.d_beta),
                                     (self.bqF_sb, self.d_bqF), (self.bkF_sb, self.d_bkF)]:
                        SY.dma_start(out=dst[:], in_=src[:]).then_inc(self.sem["dS"], 16)
                        self.bump("dS", 16)
                    t0 = self.cnt["dS"]
                    self.w(S, "dS", t0)
                    S.activation(self.be_b[0:1, :], self.beta_sb[:], AF.Exp)
                    s0 = self.inc(S.last, "sS")
                    self.w(TE, "dS", t0)
                    self.w(TE, "sS", s0)
                    TE.matmul(ps_bc[:, 0:1], self.ones1[:], self.alibi_sb[:], start=True, stop=True)
                    TE.matmul(ps_bc[:, 1:2], self.ones1[:], self.be_b[0:1, :], start=True, stop=True)
                    tt = self.inc(TE.last, "sT")
                    self.w(V, "sT", tt)
                    V.tensor_copy(self.slope_b[:], ps_bc[:, 0:1])
                    V.tensor_copy(self.be_b[:], ps_bc[:, 1:2])
                    V.drain()
                    V.tensor_scalar(self.bebq[:], self.bqF_sb[:], self.be_b[:], None, OP.mult)
                    V.tensor_scalar(self.bebk[:], self.bkF_sb[:], self.be_b[:], None, OP.mult)
                    self.inc(V.last, "sV")
                blk.vector(body)

    def emit_ln(self, xt, sc, quant=False, pre_wait=(), s_pre_wait=()):
        V, S = self.V, self.S
        for s_, v_ in pre_wait:
            self.w(V, s_, v_)
        V.reduce_sum(sc["s1"], xt, axis=AX.X)
        V.drain()
        V.tensor_scalar(sc["negm"], sc["s1"], -1.0 / D, None, OP.mult)
        V.drain()
        v1 = self.inc(V.last, "sV")
        for s_, v_ in s_pre_wait:
            self.w(S, s_, v_)
        self.w(S, "sV", v1)
        S.activation(sc["xc"], xt, AF.Identity, bias=sc["negm"], scale=1.0)
        S.drain()
        S.activation(sc["sqd"], sc["xc"], AF.Square, accum_out=sc["ssq"])
        S.drain()
        S.activation(sc["sdt"], sc["ssq"], AF.Sqrt, bias=self.epsb[:], scale=1.0 / D)
        S.drain()
        s1 = self.inc(S.last, "sS")
        self.w(V, "sS", s1)
        V.reciprocal(sc["rstd"], sc["sdt"])
        V.drain()
        if quant:
            V.reduce_max(sc["axc"], sc["xc"], axis=AX.X, apply_absolute_value=True)
            V.drain()
            V.tensor_scalar(sc["mxn"], sc["axc"], sc["rstd"], None, OP.mult)
            V.drain()
            V.tensor_scalar(sc["mxn"], sc["mxn"], 1e-5, None, OP.max)
            V.drain()
            V.reciprocal(sc["iam"], sc["mxn"])
            V.drain()
            V.tensor_scalar(sc["scq"], sc["iam"], 127.0, None, OP.mult)
            V.drain()
            V.tensor_scalar(sc["comb"], sc["scq"], sc["rstd"], None, OP.mult)
            V.drain()
        v2 = self.inc(V.last, "sV")
        return v2, s1

    def ln_scratch(self, ph, pfx, quant=False, sqd_ap=None, xc_ap=None):
        sbl = lambda n, shp, dt=F32: ph.enter_context(self.nc.sbuf_tensor(n, shp, dt))[:]
        sc = {"s1": sbl(pfx + "_s1", [128, 1]), "negm": sbl(pfx + "_negm", [128, 1]),
              "ssq": sbl(pfx + "_ssq", [128, 1]), "sdt": sbl(pfx + "_sdt", [128, 1]),
              "rstd": sbl(pfx + "_rstd", [128, 1])}
        sc["xc"] = xc_ap if xc_ap is not None else sbl(pfx + "_xc", [128, D])
        sc["sqd"] = sqd_ap if sqd_ap is not None else sbl(pfx + "_sqd", [128, D])
        if quant:
            for k in ["axc", "mxn", "iam", "scq", "comb"]:
                sc[k] = sbl(pfx + "_" + k, [128, 1])
        return sc

    def phase_A(self):
        nc = self.nc
        V, S, TE, SY, G = self.V, self.S, self.TE, self.SY, self.G
        with contextlib.ExitStack() as ph:
            sbl = lambda n, shp, dt=F32: ph.enter_context(nc.sbuf_tensor(n, shp, dt))
            psl = lambda n, shp, dt=F32: ph.enter_context(nc.psum_tensor(n, shp, dt))
            XnF = sbl("A_XnF", [128, NK, T])
            Vtm = sbl("A_Vtm", [128, NT, D])
            wbuf = sbl("A_w", [128, NK, 512])
            qF = sbl("A_qF", [128, NK, 512])
            Ssb = sbl("A_S", [128, T])
            Ssb2 = sbl("A_S2", [128, T])
            scr = sbl("A_scr", [128, T])
            PT = sbl("A_PT", [128, 2, 128])
            xt = sbl("A_xt", [128, 2, D])
            B0 = sbl("A_B0", [128, T])
            amask = sbl("A_amask", [128, 128])
            lnsc = self.ln_scratch(ph, "A_ln", sqd_ap=scr[:, 0:D], xc_ap=Ssb[:, 0:D])
            xn = scr[:, D:2*D]
            mx = sbl("A_mx", [128, 1])
            nmx = sbl("A_nmx", [128, 1])
            sume = sbl("A_sume", [128, 1])
            rs = sbl("A_rs", [128, 1])
            qbv = sbl("A_qbv", [128, 1])
            ps_tr = psl("A_ps_tr", [128, 2, 512])
            ps_v = psl("A_ps_v", [128, 2, 512])
            ps_s = psl("A_ps_s", [128, 2, 512])
            ps_o = psl("A_ps_o", [128, 2, 512])
            wflat = wbuf[:].rearrange("p k n -> p (k n)")

            with nc.Block() as blk:
                def body(_):
                    rot_xt = Rot(self, "sS", 2)
                    rot_tr = Rot(self, "sV", 2)
                    rot_pv = Rot(self, "sV", 2)
                    rot_ps = Rot(self, "sV", 2)
                    so_tgt = {0: 0, 1: 0}
                    rot_wo = Rot(self, "sT", 4)
                    ot_i = [0]

                    self.dma(SY, amask[:], self.d_amask[:], "CST")
                    cst_t = self.dma(SY, B0[:], self.d_iota128[:], "CST")
                    last_s_done = 0
                    xn_free = 0
                    for t in range(NT):
                        sl = rot_xt.acquire(SY)
                        xw = self.dma(SY, xt[:, sl, :], self.d_x[t*128:(t+1)*128, :],
                                      "X%d" % sl)
                        if t == 0:
                            self.w(V, *cst_t)
                            V.tensor_scalar(B0[:], B0[:], self.slope_b[:], None, OP.mult)
                            V.drain()
                            self.inc(V.last, "sV")
                        v_done, s_done = self.emit_ln(
                            xt[:, sl, :], lnsc,
                            pre_wait=[xw, ("sS", last_s_done)])
                        self.w(S, "sV", v_done)
                        self.w(S, "sT", xn_free)
                        S.activation(xn, lnsc["xc"], AF.Identity, scale=lnsc["rstd"])
                        S.drain()
                        s_xn = self.inc(S.last, "sS")
                        last_s_done = s_xn
                        rot_xt.release()
                        for k in range(NK):
                            trs = rot_tr.acquire(TE)
                            self.w(TE, "sS", s_xn)
                            TE.transpose(ps_tr[:, trs, 0:128], xn[:, k*128:(k+1)*128], self.identf[:])
                            tt_ = self.inc(TE.last, "sT")
                            self.w(V, "sT", tt_)
                            V.tensor_copy(XnF[:, k, t*128:(t+1)*128], ps_tr[:, trs, 0:128])
                            self.inc(V.last, "sV")
                            rot_tr.release()
                        xn_free = self.cnt["sT"]
                    xnf_all = self.cnt["sV"]
                    if self.a_stop <= 1:
                        self.w(G, "sV", self.cnt["sV"])
                        return

                    for half in range(2):
                        self.w(SY, "sT", self.cnt["sT"])
                        ww = self.dma(SY, wbuf[:],
                                      self.d_wva[:, half*512:(half+1)*512].rearrange(
                                          "(k p) n -> p k n", p=128), "W0")
                        for t in range(NT):
                            pvs = rot_pv.acquire(TE)
                            self.w(TE, *ww)
                            self.w(TE, "sV", xnf_all)
                            for k in range(NK):
                                TE.matmul(ps_v[:, pvs, :], XnF[:, k, t*128:(t+1)*128],
                                          wbuf[:, k, :], start=(k == 0), stop=(k == NK - 1))
                            tt_ = self.inc(TE.last, "sT")
                            self.w(V, "sT", tt_)
                            V.tensor_copy(Vtm[:, t, half*512:(half+1)*512], ps_v[:, pvs, :])
                            self.inc(V.last, "sV")
                            rot_pv.release()
                    a1b_T = self.cnt["sT"]
                    if self.a_stop <= 2:
                        self.w(G, "sV", self.cnt["sV"])
                        return

                    ssb_free = {0: 0, 1: 0}
                    prev_sm_s = 0
                    prev_ot_v = 0
                    ot2 = [scr[:, 0:D], scr[:, D:2*D]]
                    self.w(SY, "sT", a1b_T)
                    for qc in range(4):
                        for o in range(NK):
                            wos = rot_wo.acquire(SY)
                            ow = self.dma(SY, wflat[:, wos*1024:(wos+1)*1024].rearrange(
                                              "p (k n) -> p k n", n=128),
                                          self.d_wqk[:, o*128:(o+1)*128].rearrange(
                                              "(k p) n -> p k n", p=128), "W%d" % wos)
                            pvs = rot_pv.acquire(TE)
                            self.w(TE, *ow)
                            for k in range(NK):
                                TE.matmul(ps_v[:, pvs, :],
                                          wflat[:, wos*1024 + k*128: wos*1024 + (k+1)*128],
                                          XnF[:, k, qc*512:(qc+1)*512],
                                          start=(k == 0), stop=(k == NK - 1))
                            tt_ = self.inc(TE.last, "sT")
                            rot_wo.release()
                            self.w(V, "sT", tt_)
                            V.tensor_copy(qF[:, o, :], ps_v[:, pvs, :])
                            self.inc(V.last, "sV")
                            rot_pv.release()
                        qf_done = self.cnt["sV"]
                        if self.a_stop <= 3:
                            continue

                        for qt in range(4):
                            q0 = qc * 512 + qt * 128
                            qi = q0 // 128
                            Sbuf = [Ssb, Ssb2][qi % 2]
                            xrs = rot_xt.acquire(S)
                            rw = self.dma(S, xt[:, xrs, :], self.d_x[q0:q0+128, :],
                                          "XR%d" % xrs)
                            V.tensor_scalar(qbv[:], self.iotacol[:], float(q0),
                                            self.slope_b[:], OP.add, OP.mult)
                            V.drain()
                            qb_v = self.inc(V.last, "sV")
                            kmax = q0 + 128
                            nkb = (kmax + 511) // 512
                            for kb in range(nkb):
                                kw = min(512, kmax - kb * 512)
                                pss = rot_ps.acquire(TE)
                                self.w(TE, "sV", qf_done)
                                for dch in range(NK):
                                    TE.matmul(ps_s[:, pss, 0:kw],
                                              qF[:, dch, qt*128:(qt+1)*128],
                                              XnF[:, dch, kb*512:kb*512+kw],
                                              start=(dch == 0), stop=(dch == NK - 1))
                                tt_ = self.inc(TE.last, "sT")
                                self.w(V, "sT", tt_)
                                self.w(V, "sV", qb_v)
                                self.w(V, "sT", ssb_free[qi % 2])
                                V.tensor_scalar(Sbuf[:, kb*512:kb*512+kw], ps_s[:, pss, 0:kw],
                                                SCALE_A, qbv[:], OP.mult, OP.subtract)
                                V.drain()
                                V.tensor_tensor(Sbuf[:, kb*512:kb*512+kw],
                                                Sbuf[:, kb*512:kb*512+kw],
                                                B0[:, kb*512:kb*512+kw], OP.add)
                                V.drain()
                                self.inc(V.last, "sV")
                                rot_ps.release()
                            V.tensor_tensor(Sbuf[:, q0:q0+128], Sbuf[:, q0:q0+128],
                                            amask[:], OP.add)
                            V.drain()
                            V.reduce_max(mx[:], Sbuf[:, 0:kmax], axis=AX.X)
                            V.drain()
                            self.w(V, "sS", prev_sm_s)
                            V.tensor_scalar(nmx[:], mx[:], -1.0, None, OP.mult)
                            V.drain()
                            sm_v = self.inc(V.last, "sV")
                            self.w(S, "sV", sm_v)
                            S.activation(Sbuf[:, 0:kmax], Sbuf[:, 0:kmax], AF.Exp,
                                         bias=nmx[:], scale=1.0, accum_out=sume[:])
                            S.drain()
                            sm_s = self.inc(S.last, "sS")
                            prev_sm_s = sm_s
                            self.w(V, "sS", sm_s)
                            V.reciprocal(rs[:], sume[:])
                            V.drain()
                            rs_v = self.inc(V.last, "sV")
                            if self.a_stop <= 4:
                                self.w(V, "sV", rs_v)
                                V.tensor_copy(lnsc["negm"], rs[:])
                                self.inc(V.last, "sV")
                                rot_xt.release()
                                continue
                            if self.a_stop == 5:
                                ssb_free_T = self.cnt["sT"]
                            nb = q0 // 128 + 1
                            for kb2 in range(nb):
                                trs = rot_tr.acquire(TE)
                                self.w(TE, "sS", sm_s)
                                if kb2 == 0:
                                    self.w(TE, "sV", prev_ot_v)
                                TE.transpose(ps_tr[:, trs, 0:128], Sbuf[:, kb2*128:(kb2+1)*128],
                                             self.identf[:])
                                tt_ = self.inc(TE.last, "sT")
                                self.w(V, "sT", tt_)
                                V.tensor_copy(PT[:, trs, :], ps_tr[:, trs, 0:128])
                                ev = self.inc(V.last, "sV")
                                rot_tr.release()
                                if self.a_stop <= 5:
                                    if kb2 == nb - 1:
                                        pass
                                    continue
                                self.w(TE, "sV", ev)
                                for half in range(2):
                                    TE.matmul(ps_o[:, half, :], PT[:, trs, :],
                                              Vtm[:, kb2, half*512:(half+1)*512],
                                              start=(kb2 == 0), stop=(kb2 == nb - 1))
                            if self.a_stop <= 5:
                                ssb_free_T = self.cnt["sT"]
                                self.w(V, "sV", self.cnt["sV"])
                                V.tensor_copy(lnsc["negm"], rs[:])
                                self.inc(V.last, "sV")
                                rot_xt.release()
                                continue
                            ssb_free_T = self.inc(TE.last, "sT")
                            if self.a_stop <= 6:
                                self.w(V, "sT", ssb_free_T)
                                V.tensor_copy(lnsc["negm"], rs[:])
                                self.inc(V.last, "sV")
                                rot_xt.release()
                                continue
                            ots = ot_i[0] % 2
                            ot_i[0] += 1
                            self.w(V, "SO%d" % ots, so_tgt[ots])
                            self.w(V, "sT", ssb_free_T)
                            self.w(V, "sV", rs_v)
                            self.w(V, *rw)
                            for half in range(2):
                                V.tensor_scalar(ot2[ots][:, half*512:(half+1)*512],
                                                ps_o[:, half, :], rs[:], None, OP.mult)
                            V.drain()
                            V.tensor_tensor(ot2[ots], ot2[ots], xt[:, xrs, :], OP.add)
                            V.drain()
                            prev_ot_v = self.inc(V.last, "sV")
                            rot_xt.release(("sV", prev_ot_v))
                            if self.a_stop <= 7:
                                continue
                            self.w(SY, "sV", prev_ot_v)
                            _, so_tgt[ots] = self.dma(SY, self.d_r1[q0:q0+128, :], ot2[ots],
                                                      "SO%d" % ots)
                    self.w(G, "SO0", self.cnt["SO0"])
                    self.w(G, "SO1", self.cnt["SO1"])
                blk.vector(body)

    def phase_B(self, src, dst, first, tag):
        nc = self.nc
        V, S, TE, SY, G = self.V, self.S, self.TE, self.SY, self.G
        with contextlib.ExitStack() as ph:
            sbl = lambda n, shp, dt=F32: ph.enter_context(nc.sbuf_tensor(tag + n, shp, dt))
            psl = lambda n, shp, dt=F32: ph.enter_context(nc.psum_tensor(tag + n, shp, dt))
            W1 = sbl("_W1", [128, NK, HID], BF16)
            W2 = sbl("_W2", [128, HID // 128, D], BF16)
            if first:
                self.cache_stores = []
                self._ternarize(W1, W2, tag)
            else:
                with nc.Block() as blk0:
                    def body0(_):
                        self.w(SY, "dG", self.cnt["dG"])
                        self.dma(SY, W1[:],
                                 self.d_w1q[:].rearrange("(k p) n -> p k n", p=128), "CST")
                        lw = self.dma(SY, W2[:],
                                      self.d_w2q[:].rearrange("(k p) n -> p k n", p=128),
                                      "CST")
                        self.w(SY, *lw)
                    blk0.vector(body0)

            rt = sbl("_rt", [128, 4, D])
            lnsc = self.ln_scratch(ph, tag + "_ln", quant=True)
            qs = sbl("_qs", [128, 1024])
            qi = psl("_qi", [128, 1024], I32)
            qd = sbl("_qd", [128, 1024])
            qb = sbl("_qb", [128, 1024])
            xqm = sbl("_xqm", [128, D], BF16)
            XqF = sbl("_XqF", [128, NK, 128], BF16)
            hbuf = sbl("_h", [128, HID])
            am2 = sbl("_am2", [128, 1])
            am2c = sbl("_am2c", [128, 1])
            iam2 = sbl("_iam2", [128, 1])
            scq2 = sbl("_scq2", [128, 1])
            hqm = sbl("_hqm", [128, HID], BF16)
            HqF = sbl("_HqF", [128, HID // 128, 128], BF16)
            dq1 = sbl("_dq1", [128, 2])
            dq2 = sbl("_dq2", [128, 2])
            ot = sbl("_ot", [128, 1, D])
            ps_h = psl("_ps_h", [128, 2, 512])
            ps_o = psl("_ps_o", [128, 2, 512])
            ps_tb = psl("_ps_tb", [128, 2, 1024], BF16)

            def round_slices(src_ap, dst_ap, width, v_wait, s_wait):
                for j in range(width // 512):
                    ssl = src_ap[:, j*512:(j+1)*512]
                    self.w(S, "sV", s_wait)
                    S.copy(qi[:], ssl)
                    sq = self.inc(S.last, "sS")
                    self.w(V, "sS", sq)
                    V.tensor_tensor(qd[:], ssl, qi[:], OP.subtract)
                    V.drain()
                    V.scalar_tensor_tensor(qb[:], qd[:], 0.5, qi[:], OP.is_ge, OP.add)
                    V.drain()
                    V.scalar_tensor_tensor(dst_ap[:, j*512:(j+1)*512], qd[:], -0.5,
                                           qb[:], OP.is_lt, OP.subtract)
                    V.drain()
                    s_wait = self.inc(V.last, "sV")
                return self.cnt["sV"]

            levels = 1 if first else 3
            levels2 = 1
            with nc.Block() as blk:
                def body(_):
                    rot_rt = Rot(self, "sV", 4)
                    so_tgt = {0: 0, 1: 0}
                    ot_i = [0]
                    rot_h = Rot(self, "sS", 2)
                    rot_ov = Rot(self, "sV", 2)
                    rot_tb = Rot(self, "sV", 2)
                    st = {"pipe_s": 0, "pipe_v": 0, "xq_tr_T": 0, "hq_tr_T": 0,
                          "q2_free_v": 0, "v_q2": 0}
                    pend = []
                    rt_rel_idx = {}
                    s1 = {}

                    def stage1a(t):
                        sl = rot_rt.acquire(SY)
                        xw = self.dma(SY, rt[:, sl, :], src[t*128:(t+1)*128, :],
                                      ["X0", "X1", "XR0", "XR1"][sl])
                        self.emit_ln(
                            rt[:, sl, :], lnsc, quant=True,
                            pre_wait=[xw],
                            s_pre_wait=[("sS", st["pipe_s"]), ("sV", st["pipe_v"])])
                        V.tensor_scalar(dq1[:, t % 2:t % 2 + 1], lnsc["mxn"][:],
                                        1.0 / 127.0, self.s1b[:], OP.mult, OP.mult)
                        V.drain()
                        v_dq1 = self.inc(V.last, "sV")
                        self.w(S, "sV", v_dq1)
                        self.w(S, "sV", st["q2_free_v"])
                        S.activation(qs[:], lnsc["xc"], AF.Identity, scale=lnsc["comb"])
                        S.drain()
                        S.copy(qi[:], qs[:])
                        sq = self.inc(S.last, "sS")
                        self.w(V, "sS", sq)
                        V.tensor_tensor(qd[:], qs[:], qi[:], OP.subtract)
                        V.drain()
                        V.scalar_tensor_tensor(qb[:], qd[:], 0.5, qi[:], OP.is_ge, OP.add)
                        V.drain()
                        st["pipe_v"] = v_dq1
                        st["pipe_s"] = sq
                        s1[t] = {"sl": sl, "v_dq1": v_dq1}

                    def stage1f(t):
                        self.w(V, "sT", st["xq_tr_T"])
                        if levels == 1:
                            V.scalar_tensor_tensor(xqm[:], qd[:], -0.5,
                                                   qb[:], OP.is_lt, OP.subtract)
                            V.drain()
                        else:
                            V.scalar_tensor_tensor(qs[:], qd[:], 1.5, qb[:], OP.is_ge, OP.add)
                            V.drain()
                            V.scalar_tensor_tensor(qb[:], qd[:], 2.5, qs[:], OP.is_ge, OP.add)
                            V.drain()
                            V.scalar_tensor_tensor(qs[:], qd[:], -0.5, qb[:], OP.is_lt, OP.subtract)
                            V.drain()
                            V.scalar_tensor_tensor(qb[:], qd[:], -1.5, qs[:], OP.is_lt, OP.add)
                            V.drain()
                            V.scalar_tensor_tensor(xqm[:], qd[:], -2.5,
                                                   qb[:], OP.is_lt, OP.add)
                            V.drain()
                        s1[t]["vq"] = self.inc(V.last, "sV")
                        rot_rt.release()
                        rt_rel_idx[t] = len(rot_rt.rel) - 1

                    def xq_transposes(t):
                        for k in range(NK):
                            tbs = rot_tb.acquire(TE)
                            self.w(TE, "sV", s1[t]["vq"])
                            TE.transpose(ps_tb[:, tbs, 0:128], xqm[:, k*128:(k+1)*128],
                                         self.identb[:])
                            tt_ = self.inc(TE.last, "sT")
                            self.w(S, "sT", tt_)
                            S.copy(XqF[:, k, :], ps_tb[:, tbs, 0:128])
                            S.drain()
                            self.inc(S.last, "sS")
                            rot_tb.release(("sS", self.cnt["sS"]))
                        st["xq_tr_T"] = self.cnt["sT"]
                        s1[t]["xqf_done"] = ("sS", self.cnt["sS"])

                    def ffn1_gelu(t, q2=None):
                        for b in range(8):
                            if q2 is not None and b % 2 == 0:
                                q2(b // 2)
                            hs = rot_h.acquire(TE)
                            self.w(TE, *s1[t]["xqf_done"])
                            for k in range(NK):
                                TE.matmul(ps_h[:, hs, :], XqF[:, k, :],
                                          W1[:, k, b*512:(b+1)*512],
                                          start=(k == 0), stop=(k == NK - 1))
                            tt_ = self.inc(TE.last, "sT")
                            self.w(S, "sT", tt_)
                            self.w(S, "sV", s1[t]["v_dq1"])
                            if b == 0:
                                self.w(S, "sV", st["v_q2"])
                            S.activation(hbuf[:, b*512:(b+1)*512], ps_h[:, hs, :],
                                         AF.Gelu, scale=dq1[:, t % 2:t % 2 + 1])
                            self.inc(S.last, "sS")
                            rot_h.release()
                        s1[t]["gelu_done"] = self.cnt["sS"]

                    stage1a(0)
                    stage1f(0)
                    xq_transposes(0)
                    stage1a(1)
                    stage1f(1)
                    ffn1_gelu(0)
                    for t in range(NT):
                        sl = s1[t]["sl"]
                        if pend:
                            pend.pop(0)()
                        self.w(V, "sS", s1[t]["gelu_done"])
                        V.reduce_max(am2[:], hbuf[:], axis=AX.X, apply_absolute_value=True)
                        V.drain()
                        V.tensor_scalar(am2c[:], am2[:], 1e-5, None, OP.max)
                        V.drain()
                        V.reciprocal(iam2[:], am2c[:])
                        V.drain()
                        V.tensor_scalar(scq2[:], iam2[:], 127.0, None, OP.mult)
                        V.drain()
                        V.tensor_scalar(dq2[:, (t % 2):(t % 2)+1], am2c[:], 1.0 / 127.0,
                                        self.s2b[:], OP.mult, OP.mult)
                        V.drain()
                        v_q2 = self.inc(V.last, "sV")
                        st["v_q2"] = v_q2
                        vq_next = s1[t + 1]["vq"] if t + 1 < NT else 0
                        sv2_box = [max(v_q2, vq_next)]

                        def q2slice(j, t=t, sv2_box=sv2_box):
                            self.w(S, "sV", sv2_box[0])
                            S.activation(qs[:], hbuf[:, j*1024:(j+1)*1024],
                                         AF.Identity, scale=scq2[:])
                            S.drain()
                            S.copy(qi[:], qs[:])
                            sq = self.inc(S.last, "sS")
                            self.w(V, "sS", sq)
                            V.tensor_tensor(qd[:], qs[:], qi[:], OP.subtract)
                            V.drain()
                            V.scalar_tensor_tensor(qb[:], qd[:], 0.5, qi[:], OP.is_ge, OP.add)
                            V.drain()
                            if j == 0:
                                self.w(V, "sT", st["hq_tr_T"])
                            V.scalar_tensor_tensor(hqm[:, j*1024:(j+1)*1024], qd[:], -0.5,
                                                   qb[:], OP.is_lt, OP.subtract)
                            V.drain()
                            sv2_box[0] = self.inc(V.last, "sV")

                        if t + 1 < NT:
                            xq_transposes(t + 1)
                            ffn1_gelu(t + 1, q2=q2slice)
                        else:
                            for j in range(4):
                                q2slice(j)
                        st["q2_free_v"] = sv2_box[0]
                        v_hq = self.cnt["sV"]
                        for k in range(HID // 128):
                            tbs = rot_tb.acquire(TE)
                            self.w(TE, "sV", v_hq)
                            TE.transpose(ps_tb[:, tbs, 0:128], hqm[:, k*128:(k+1)*128],
                                         self.identb[:])
                            tt_ = self.inc(TE.last, "sT")
                            self.w(S, "sT", tt_)
                            S.copy(HqF[:, k, :], ps_tb[:, tbs, 0:128])
                            S.drain()
                            self.inc(S.last, "sS")
                            rot_tb.release(("sS", self.cnt["sS"]))
                        st["hq_tr_T"] = self.cnt["sT"]
                        hqf_done = ("sS", self.cnt["sS"])
                        ots = 0
                        ffn2_T = []
                        for half in range(2):
                            ovs = rot_ov.acquire(TE)
                            self.w(TE, *hqf_done)
                            for k in range(HID // 128):
                                TE.matmul(ps_o[:, ovs, :], HqF[:, k, :],
                                          W2[:, k, half*512:(half+1)*512],
                                          start=(k == 0), stop=(k == HID // 128 - 1))
                            ffn2_T.append((ovs, self.inc(TE.last, "sT")))
                        if t + 2 < NT:
                            stage1a(t + 2)
                            stage1f(t + 2)
                        if first:
                            for _ in range(3):
                                if self.cache_stores:
                                    dc, wsrc, ev = self.cache_stores.pop(0)
                                    self.w(SY, "sV", ev)
                                    SY.dma_start(out=dc, in_=wsrc).then_inc(
                                        self.sem["dG"], 16)
                                    self.bump("dG", 16)

                        def stageB(t=t, sl=sl, ots=ots, ffn2_T=ffn2_T):
                            first_ot = True
                            for half, (ovs, tt_) in enumerate(ffn2_T):
                                self.w(V, "sT", tt_)
                                if first_ot:
                                    self.w(V, "SO%d" % ots, so_tgt[ots])
                                    first_ot = False
                                V.tensor_scalar(ot[:, ots, half*512:(half+1)*512],
                                                ps_o[:, ovs, :], dq2[:, (t % 2):(t % 2)+1],
                                                None, OP.mult)
                                self.inc(V.last, "sV")
                                rot_ov.release()
                            V.drain()
                            V.tensor_tensor(ot[:, ots, :], ot[:, ots, :], rt[:, sl, :], OP.add)
                            V.drain()
                            ot_v = self.inc(V.last, "sV")
                            rot_rt.rel[rt_rel_idx[t]] = ("sV", ot_v)
                            self.w(SY, "sV", ot_v)
                            _, so_tgt[ots] = self.dma(SY, dst[t*128:(t+1)*128, :],
                                                      ot[:, ots, :], "SO%d" % ots)
                        pend.append(stageB)
                    while pend:
                        pend.pop(0)()
                    self.w(SY, "SO0", self.cnt["SO0"])
                    self.w(SY, "SO1", self.cnt["SO1"])
                    if first:
                        self.w(SY, "dG", self.cnt["dG"])
                blk.vector(body)

    def _ternarize(self, W1, W2, tag):
        nc = self.nc
        V, S, TE, SY, G = self.V, self.S, self.TE, self.SY, self.G
        with contextlib.ExitStack() as p2:
            sbl = lambda n, shp, dt=F32: p2.enter_context(nc.sbuf_tensor(tag + "T" + n, shp, dt))
            wtmp = sbl("_wtmp", [128, 2, HID])
            acc = sbl("_acc", [128, 1])
            part = sbl("_part", [128, 1])
            zt = sbl("_zt", [128, HID])
            tp = sbl("_tp", [128, HID])
            onec = sbl("_onec", [128, 1])
            psm = p2.enter_context(nc.psum_tensor(tag + "T_psm", [128, 2], F32))

            with nc.Block() as blk:
                def body(_):
                    V.memset(onec[:], 1.0)
                    V.memset(acc[:], 0.0)
                    V.drain()
                    self.inc(V.last, "sV")
                    rot_w = Rot(self, "sV", 2)
                    plans = [(self.d_w1t, HID, 8, 0), (self.d_w2t, D, 32, 1)]
                    for dram, width, ntile, idx in plans:
                        for i in range(ntile):
                            sl = rot_w.acquire(SY)
                            lw = self.dma(SY, wtmp[:, sl, 0:width],
                                          dram[i*128:(i+1)*128, :], "X%d" % sl)
                            self.w(V, *lw)
                            V.reduce_sum(part[:], wtmp[:, sl, 0:width], axis=AX.X,
                                         apply_absolute_value=True)
                            V.drain()
                            V.tensor_tensor(acc[:], acc[:], part[:], OP.add)
                            V.drain()
                            self.inc(V.last, "sV")
                            rot_w.release()
                        av = self.cnt["sV"]
                        self.w(TE, "sV", av)
                        TE.matmul(psm[0:1, idx:idx+1], acc[:], onec[:], start=True, stop=True)
                        tt_ = self.inc(TE.last, "sT")
                        self.w(V, "sT", tt_)
                        V.tensor_scalar(self.s12row[0:1, idx:idx+1], psm[0:1, idx:idx+1],
                                        1.0 / (D * HID), None, OP.mult)
                        V.drain()
                        V.memset(acc[:], 0.0)
                        V.drain()
                        self.inc(V.last, "sV")
                    V.tensor_scalar(self.s12row[:], self.s12row[:], 1e-5, None, OP.max)
                    V.drain()
                    sv = self.inc(V.last, "sV")
                    self.w(S, "sV", sv)
                    S.dma_start(out=self.d_s12[:], in_=self.s12row[:]).then_inc(self.sem["dG"], 16)
                    self.bump("dG", 16)
                    self.w(TE, "sV", sv)
                    TE.matmul(psm[:, 0:1], self.ones1[:], self.s12row[0:1, 0:1], start=True, stop=True)
                    TE.matmul(psm[:, 1:2], self.ones1[:], self.s12row[0:1, 1:2], start=True, stop=True)
                    tt_ = self.inc(TE.last, "sT")
                    self.w(V, "sT", tt_)
                    V.tensor_copy(self.s1b[:], psm[:, 0:1])
                    V.tensor_copy(self.s2b[:], psm[:, 1:2])
                    V.drain()
                    V.reciprocal(self.is1b[:], self.s1b[:])
                    V.reciprocal(self.is2b[:], self.s2b[:])
                    V.drain()
                    bv = self.inc(V.last, "sV")

                    plans2 = [(self.d_w1t, self.d_w1q, W1, HID, 8, self.is1b),
                              (self.d_w2t, self.d_w2q, W2, D, 32, self.is2b)]
                    zt_free = 0
                    for dram, dcache, Wsb, width, ntile, isb in plans2:
                        for i in range(ntile):
                            sl = rot_w.acquire(SY)
                            lw = self.dma(SY, wtmp[:, sl, 0:width],
                                          dram[i*128:(i+1)*128, :], "X%d" % sl)
                            self.w(S, *lw)
                            self.w(S, "sV", max(bv, zt_free))
                            S.activation(zt[:, 0:width], wtmp[:, sl, 0:width],
                                         AF.Identity, scale=isb[:])
                            s_z = self.inc(S.last, "sS")
                            self.w(V, "sS", s_z)
                            V.tensor_scalar(tp[:, 0:width], zt[:, 0:width], 0.5, None, OP.is_gt)
                            V.drain()
                            V.scalar_tensor_tensor(Wsb[:, i, :], zt[:, 0:width], -0.5,
                                                   tp[:, 0:width], OP.is_lt, OP.subtract)
                            V.drain()
                            wv_ = self.inc(V.last, "sV")
                            zt_free = wv_
                            rot_w.release()
                            self.cache_stores.append(
                                (dcache[i*128:(i+1)*128, :], Wsb[:, i, :], wv_))
                blk.vector(body)

    def phase_C(self):
        nc = self.nc
        V, S, TE, SY, G = self.V, self.S, self.TE, self.SY, self.G
        CH = 512
        NCH = T // CH
        NS = CH // 128
        with contextlib.ExitStack() as ph:
            sbl = lambda n, shp, dt=F32: ph.enter_context(nc.sbuf_tensor("C" + n, shp, dt))
            psl = lambda n, shp, dt=F32: ph.enter_context(nc.psum_tensor("C" + n, shp, dt))
            XnFc = sbl("_XnF", [128, NK, CH])
            qFd = sbl("_qF", [128, NK, CH])
            kFd = sbl("_kF", [128, NK, CH])
            qS = sbl("_qS", [128, NK, 128])
            kS = sbl("_kS", [128, NK, 128])
            ktok = sbl("_ktok", [128, NS, D])
            Vp = sbl("_Vp", [128, NS, 64 * 17])
            wbufC = sbl("_w", [128, 2, NK, 512])
            Oc = sbl("_Oc", [128, NS, D])
            OF = sbl("_OF", [128, NK, 128])
            rtC = sbl("_rt", [128, NS, D])
            stateSB = sbl("_st", [128, 16 * 68])
            triu4 = sbl("_triu4", [128, 512])
            PTb = sbl("_PT", [128, 2, 512])
            lnsc = self.ln_scratch(ph, "C_ln")
            xn = sbl("_xn", [128, D])
            rden = sbl("_rden", [8, 512])
            zero512 = sbl("_zero512", [128, 512])
            rd4 = sbl("_rd4", [128, 4])
            bvb = sbl("_bvb", [128, D])
            bob = sbl("_bob", [128, D])
            brow = sbl("_brow", [1, D])
            ps_quad = psl("_ps_quad", [128, 4, 512])
            ps_N = psl("_ps_N", [128, 2, 512])
            ps_st = psl("_ps_state", [128, 512])
            ps_d = psl("_ps_d", [128, 512])


            with nc.Block() as blk:
                def body(_):
                    rot_tr = Rot(self, "sV", 2)
                    rot_w = Rot(self, "sT", 2)
                    rot_p = Rot(self, "sV", 2)
                    rot_N = Rot(self, "sV", 2)
                    rot_PT = Rot(self, "sT", 2)
                    rot_ot = Rot(self, "dG", 2)
                    rot_rt = Rot(self, "sV", 4)

                    V.memset(zero512[:], 0.0)
                    V.drain()
                    V.memset(stateSB[:], 0.0)
                    V.drain()
                    V.memset(qS[96:128, :, :], 0.0)
                    V.drain()
                    V.memset(kS[96:128, :, :], 0.0)
                    V.drain()
                    self.inc(V.last, "sV")
                    self.dma(SY, triu4[:], self.d_triu4[:], "CST")
                    for row_d, dst in [(self.d_bvrow, bvb), (self.d_borow, bob)]:
                        self.w(SY, "sT", self.cnt["sT"])
                        bw = self.dma(SY, brow[:], row_d[:], "CST")
                        self.w(TE, *bw)
                        for half in range(2):
                            TE.matmul(ps_d[:, 0:512], self.ones1[:],
                                      brow[:, half*512:(half+1)*512], start=True, stop=True)
                            tt_ = self.inc(TE.last, "sT")
                            self.w(V, "sT", tt_)
                            V.tensor_copy(dst[:, half*512:(half+1)*512], ps_d[:, 0:512])
                            fv = self.inc(V.last, "sV")
                            self.w(TE, "sV", fv)
                    bias_done = self.cnt["sV"]

                    st_upd_T = 0
                    chunk_T = 0
                    last_s = 0
                    xn_tr = 0
                    pt_guard = 0

                    for c in range(NCH):
                        for s_ in range(NS):
                            t0 = c * NS + s_
                            sl = rot_rt.acquire(SY)
                            xw = self.dma(SY, rtC[:, sl, :],
                                          self.d_r2[t0*128:(t0+1)*128, :],
                                          ["X0", "X1", "XR0", "XR1"][sl])
                            v_done, s_done = self.emit_ln(
                                rtC[:, sl, :], lnsc,
                                pre_wait=[xw, ("sS", last_s)])
                            self.w(S, "sV", v_done)
                            self.w(S, "sT", xn_tr)
                            S.activation(xn[:], lnsc["xc"], AF.Identity, scale=lnsc["rstd"])
                            S.drain()
                            s_xn = self.inc(S.last, "sS")
                            last_s = s_xn
                            for k in range(NK):
                                trs = rot_tr.acquire(TE)
                                self.w(TE, "sS", s_xn)
                                if s_ == 0 and k == 0:
                                    self.w(TE, "sV", pt_guard)
                                TE.transpose(ps_quad[:, trs, 0:128], xn[:, k*128:(k+1)*128],
                                             self.identf[:])
                                tt_ = self.inc(TE.last, "sT")
                                self.w(V, "sT", tt_)
                                V.tensor_copy(XnFc[:, k, s_*128:(s_+1)*128], ps_quad[:, trs, 0:128])
                                self.inc(V.last, "sV")
                                rot_tr.release()
                            xn_tr = self.cnt["sT"]
                        xnf_done = self.cnt["sV"]
                        if self.c_stop <= 1:
                            continue

                        for half in range(2):
                            ws = rot_w.acquire(SY)
                            w_ld = self.dma(SY, wbufC[:, ws, :, :],
                                            self.d_wq[:, half*512:(half+1)*512].rearrange(
                                                "(k p) n -> p k n", p=128), "W%d" % ws)
                            for oo in range(4):
                                o = half * 4 + oo
                                pps = rot_p.acquire(TE)
                                self.w(TE, *w_ld)
                                self.w(TE, "sV", xnf_done)
                                for k in range(NK):
                                    TE.matmul(ps_quad[:, 2 + pps, :], wbufC[:, ws, k, oo*128:(oo+1)*128],
                                              XnFc[:, k, :], start=(k == 0), stop=(k == NK - 1))
                                tt_ = self.inc(TE.last, "sT")
                                self.w(S, "sT", tt_)
                                S.activation(qFd[:, o, :], ps_quad[:, 2 + pps, :], AF.Exp,
                                             bias=self.bebq[:, o:o+1], scale=self.be_b[:])
                                S.drain()
                                sev = self.inc(S.last, "sS")
                                rot_p.release(("sS", sev))
                            rot_w.release()
                        q_done_s = self.cnt["sS"]
                        if self.c_stop <= 2:
                            continue

                        knorm_v = 0
                        for half in range(2):
                            ws = rot_w.acquire(SY)
                            w_ld = self.dma(SY, wbufC[:, ws, :, :],
                                            self.d_wk[:, half*512:(half+1)*512].rearrange(
                                                "(k p) n -> p k n", p=128), "W%d" % ws)
                            for oo in range(4):
                                o = half * 4 + oo
                                pps = rot_p.acquire(TE)
                                self.w(TE, *w_ld)
                                self.w(TE, "sV", xnf_done)
                                for k in range(NK):
                                    TE.matmul(ps_quad[:, 2 + pps, :], wbufC[:, ws, k, oo*128:(oo+1)*128],
                                              XnFc[:, k, :], start=(k == 0), stop=(k == NK - 1))
                                tt_ = self.inc(TE.last, "sT")
                                self.w(S, "sT", tt_)
                                S.activation(kFd[:, o, :], ps_quad[:, 2 + pps, :], AF.Exp,
                                             bias=self.bebk[:, o:o+1], scale=self.be_b[:])
                                S.drain()
                                sev = self.inc(S.last, "sS")
                                self.w(TE, "sS", sev)
                                self.w(TE, "sV", knorm_v)
                                TE.matmul(ps_d[0:8, 0:512], self.bd8[:], kFd[:, o, :],
                                          start=True, stop=True)
                                t1 = self.inc(TE.last, "sT")
                                self.w(V, "sT", t1)
                                V.reciprocal(rden[:], ps_d[0:8, 0:512])
                                V.drain()
                                rv = self.inc(V.last, "sV")
                                self.w(TE, "sV", rv)
                                TE.matmul(ps_d[:, 0:512], self.bd8t[:], rden[:],
                                          start=True, stop=True)
                                t2 = self.inc(TE.last, "sT")
                                self.w(V, "sT", t2)
                                V.tensor_tensor(kFd[:, o, :], kFd[:, o, :], ps_d[:, 0:512],
                                                OP.mult)
                                V.drain()
                                knorm_v = self.inc(V.last, "sV")
                                rot_p.release()
                            rot_w.release()
                        k_done_v = self.cnt["sV"]
                        if self.c_stop <= 3:
                            continue

                        for s_ in range(NS):
                            for o in range(NK):
                                trs = rot_tr.acquire(TE)
                                self.w(TE, "sV", k_done_v)
                                TE.transpose(ps_quad[:, trs, 0:128], kFd[:, o, s_*128:(s_+1)*128],
                                             self.identf[:])
                                tt_ = self.inc(TE.last, "sT")
                                self.w(V, "sT", tt_)
                                V.tensor_copy(ktok[:, s_, o*128:(o+1)*128], ps_quad[:, trs, 0:128])
                                self.inc(V.last, "sV")
                                rot_tr.release()
                        ktok_done = self.cnt["sV"]
                        if self.c_stop <= 4:
                            continue

                        self.w(V, "sV", bias_done)
                        V.memset(Vp[:].rearrange("p s (h e) -> p s h e", e=17)[:, :, :, 16:17],
                                 1.0)
                        V.drain()
                        ones_v = self.inc(V.last, "sV")
                        for half in range(2):
                            ws = rot_w.acquire(SY)
                            w_ld = self.dma(SY, wbufC[:, ws, :, :],
                                            self.d_wv[:, half*512:(half+1)*512].rearrange(
                                                "(k p) n -> p k n", p=128), "W%d" % ws)
                            for s_ in range(NS):
                                pps = rot_p.acquire(TE)
                                self.w(TE, *w_ld)
                                self.w(TE, "sV", xnf_done)
                                for k in range(NK):
                                    TE.matmul(ps_quad[:, 2 + pps, :],
                                              XnFc[:, k, s_*128:(s_+1)*128],
                                              wbufC[:, ws, k, :],
                                              start=(k == 0), stop=(k == NK - 1))
                                tt_ = self.inc(TE.last, "sT")
                                self.w(V, "sT", tt_)
                                self.w(V, "sV", ones_v)
                                vview = Vp[:, s_, :].rearrange("p (h e) -> p h e", e=17)
                                pview = ps_quad[:, 2 + pps, :].rearrange("p (h e) -> p h e", e=16)
                                bview = (bvb[:, half*512:(half+1)*512]
                                         .rearrange("p (h e) -> p h e", e=16))
                                V.tensor_tensor(vview[:, half*32:(half+1)*32, 0:16],
                                                pview, bview, OP.add)
                                V.drain()
                                self.inc(V.last, "sV")
                                rot_p.release()
                            rot_w.release()
                        vp_done = self.cnt["sV"]
                        if self.c_stop <= 5:
                            continue
                        prev_pt_v = vp_done

                        scan_sub_T = 0
                        for s_ in range(NS):
                            cg = c * NS + s_
                            self.w(S, "sS", q_done_s)
                            self.w(S, "sV", vp_done)
                            self.w(S, "sT", scan_sub_T)
                            self.dma(S, qS[0:112, :, :],
                                     qFd[16:128, :, s_*128:(s_+1)*128], "SH")
                            shift_ld = self.dma(S, kS[0:112, :, :],
                                                kFd[16:128, :, s_*128:(s_+1)*128], "SH")
                            if cg > 0:
                                self.w(V, "sT", st_upd_T)
                                for a4 in range(4):
                                    V.tensor_copy(
                                        stateSB[32*a4:32*a4+16, :].rearrange(
                                            "p (g e) -> p g e", e=68)[:, :, 17*a4:17*a4+17],
                                        ps_st[32*a4:32*a4+16, 0:272].rearrange(
                                            "p (g e) -> p g e", e=17))
                                V.drain()
                                st_ev = self.inc(V.last, "sV")
                            else:
                                st_ev = self.cnt["sV"]
                            if cg == 0:
                                self.w(TE, "sV", st_ev)
                                TE.matmul(ps_st[0:128, 0:512], self.identf[:],
                                          zero512[:], start=True, stop=True)
                                st_upd_T = self.inc(TE.last, "sT")
                            for o in range(NK):
                                for par in range(2):
                                    g = 2 * o + par
                                    if par == 0:
                                        qT = qFd[:, o, s_*128:(s_+1)*128]
                                        kT = kFd[:, o, s_*128:(s_+1)*128]
                                    else:
                                        qT = qS[:, o, :]
                                        kT = kS[:, o, :]
                                    self.w(TE, "sS", q_done_s)
                                    self.w(TE, "sV", k_done_v)
                                    self.w(TE, *shift_ld)
                                    self.w(TE, "sV", prev_pt_v)
                                    for a in range(4):
                                        bofs = 32 * a
                                        TE.matmul(ps_quad[:, a, 0:128],
                                                  kT[bofs:bofs+16, :],
                                                  qT[bofs:bofs+16, :],
                                                  start=True, stop=True,
                                                  tile_position=(bofs, 0),
                                                  skip_group_check=True)
                                    t_s = self.inc(TE.last, "sT")
                                    pts = rot_PT.acquire(V)
                                    self.w(V, "sT", t_s)
                                    V.tensor_tensor(
                                        PTb[:, pts, :].rearrange("p (a e) -> p a e", e=128),
                                        ps_quad[:, 0:4, 0:128],
                                        triu4[:].rearrange("p (a e) -> p a e", e=128),
                                        OP.mult)
                                    V.drain()
                                    pt_v = self.inc(V.last, "sV")
                                    prev_pt_v = pt_v
                                    Ns = rot_N.acquire(TE)
                                    self.w(TE, "sV", pt_v)
                                    self.w(TE, "sV", vp_done)
                                    self.w(TE, "sV", st_ev)
                                    first = True
                                    if cg > 0:
                                        TE.matmul(ps_N[:, Ns, 0:68], qT,
                                                  stateSB[:, g*68:(g+1)*68],
                                                  start=True, stop=False,
                                                  skip_group_check=True)
                                        first = False
                                    for a in range(4):
                                        h = 8 * o + par + 2 * a
                                        TE.matmul(ps_N[:, Ns, a*17:(a+1)*17],
                                                  PTb[:, pts, a*128:(a+1)*128],
                                                  Vp[:, s_, h*17:(h+1)*17],
                                                  start=first, stop=(a == 3),
                                                  skip_group_check=True)
                                        first = False
                                    t_n = self.inc(TE.last, "sT")
                                    rot_PT.release()
                                    self.w(TE, "sV", ktok_done)
                                    for a in range(4):
                                        h = 8 * o + par + 2 * a
                                        bofs = 32 * a
                                        TE.matmul(ps_st[bofs:bofs+16, g*17:(g+1)*17],
                                                  ktok[:, s_, h*16:(h+1)*16],
                                                  Vp[:, s_, h*17:(h+1)*17],
                                                  start=False, stop=False,
                                                  tile_position=(0, bofs),
                                                  skip_group_check=True)
                                    st_upd_T = self.inc(TE.last, "sT")
                                    self.w(V, "sT", t_n)
                                    if o == 0 and par == 0 and c > 0 and self.c_stop > 7:
                                        self.w(V, "SO%d" % s_, ot_store[s_])
                                    nview = ps_N[:, Ns, 0:68].rearrange("p (h e) -> p h e", e=17)
                                    V.reciprocal(rd4[:], nview[:, 0:4, 16])
                                    V.drain()
                                    h0 = 8 * o + par
                                    ov = Oc[:, s_, :].rearrange(
                                        "p (hh e) -> p hh e", e=16)[:, h0:h0+7:2, :]
                                    rb = rd4[:].unsqueeze(-1).broadcast_to((128, 4, 16))
                                    V.tensor_tensor(ov, nview[:, 0:4, 0:16], rb, OP.mult)
                                    V.drain()
                                    self.inc(V.last, "sV")
                                    rot_N.release()
                            scan_sub_T = self.cnt["sT"]
                            pt_guard = prev_pt_v
                        oc_done = self.cnt["sV"]
                        if self.c_stop <= 7:
                            chunk_T = self.cnt["sT"]
                            continue

                        if c == 0:
                            ot_store = {}
                        w_lds = []
                        wss = []
                        for half in range(2):
                            ws = rot_w.acquire(SY)
                            w_lds.append(self.dma(
                                SY, wbufC[:, ws, :, :],
                                self.d_wo[:, half*512:(half+1)*512].rearrange(
                                    "(k p) n -> p k n", p=128), "W%d" % ws))
                            wss.append(ws)
                        for s_ in range(NS):
                            t0 = c * NS + s_
                            for k in range(NK):
                                trs = rot_tr.acquire(TE)
                                self.w(TE, "sV", oc_done)
                                TE.transpose(ps_quad[:, trs, 0:128], Oc[:, s_, k*128:(k+1)*128],
                                             self.identf[:])
                                tt_ = self.inc(TE.last, "sT")
                                self.w(V, "sT", tt_)
                                V.tensor_copy(OF[:, k, :], ps_quad[:, trs, 0:128])
                                self.inc(V.last, "sV")
                                rot_tr.release()
                            of_done = self.cnt["sV"]
                            for half in range(2):
                                pps = rot_p.acquire(TE)
                                self.w(TE, *w_lds[half])
                                self.w(TE, "sV", of_done)
                                for k in range(NK):
                                    TE.matmul(ps_quad[:, 2 + pps, :], OF[:, k, :],
                                              wbufC[:, wss[half], k, :],
                                              start=(k == 0), stop=(k == NK - 1))
                                tt_ = self.inc(TE.last, "sT")
                                self.w(V, "sT", tt_)
                                self.w(V, "sV", bias_done)
                                V.tensor_tensor(Oc[:, s_, half*512:(half+1)*512],
                                                ps_quad[:, 2 + pps, :],
                                                bob[:, half*512:(half+1)*512], OP.add)
                                V.drain()
                                self.inc(V.last, "sV")
                                rot_p.release()
                            V.tensor_tensor(Oc[:, s_, :], Oc[:, s_, :],
                                            rtC[:, s_, :], OP.add)
                            V.drain()
                            ot_v = self.inc(V.last, "sV")
                            rot_rt.release(("sV", ot_v))
                            self.w(SY, "sV", ot_v)
                            _, ot_store[s_] = self.dma(
                                SY, self.d_r3[t0*128:(t0+1)*128, :],
                                Oc[:, s_, :], "SO%d" % s_)
                        for half in range(2):
                            rot_w.release()
                        chunk_T = self.cnt["sT"]
                    for i4 in range(4):
                        self.w(SY, "SO%d" % i4, self.cnt["SO%d" % i4])
                blk.vector(body)


    def copy_dram(self, src, dst, tagn):
        nc = self.nc
        with contextlib.ExitStack() as ph:
            buf = ph.enter_context(nc.sbuf_tensor("cp%s" % tagn, [128, 2, D], F32))
            with nc.Block() as blk:
                def body(_):
                    so = {0: 0, 1: 0}
                    for t in range(NT):
                        sl = t % 2
                        self.w(self.SY, "SO%d" % sl, so[sl])
                        lw = self.dma(self.SY, buf[:, sl, :], src[t*128:(t+1)*128, :],
                                      "X%d" % sl)
                        self.w(self.S, *lw)
                        _, so[sl] = self.dma(self.S, dst[t*128:(t+1)*128, :], buf[:, sl, :],
                                             "SO%d" % sl)
                    self.w(self.S, "SO0", self.cnt["SO0"])
                    self.w(self.S, "SO1", self.cnt["SO1"])
                blk.vector(body)

    def build(self):
        self.setup_consts()
        if "A" in self.phases:
            self.phase_A()
        else:
            self.copy_dram(self.d_x, self.d_r1, "A")
        if "B" in self.phases:
            self.phase_B(self.d_r1, self.d_r2, first=True, tag="B")
        else:
            self.copy_dram(self.d_r1, self.d_r2, "B")
        if "C" in self.phases:
            self.phase_C()
        else:
            self.copy_dram(self.d_r2, self.d_r3, "C")
        if "D" in self.phases:
            self.phase_B(self.d_r3, self.d_y, first=("B" not in self.phases), tag="D")
        else:
            self.copy_dram(self.d_r3, self.d_y, "D")
        self.st.close()
        return self.nc


def host_consts():
    import ml_dtypes
    eye = np.eye(128, dtype=np.float32)
    eyeb = np.eye(128, dtype=ml_dtypes.bfloat16)
    iota128 = np.broadcast_to(np.arange(T, dtype=np.float32)[None, :], (128, T)).copy()
    iotacol = np.arange(128, dtype=np.float32)[:, None].copy()
    q = np.arange(128)[:, None]
    kk = np.arange(128)[None, :]
    amask = np.where(kk <= q, 0.0, NEG).astype(np.float32)
    tri = (np.arange(128)[:, None] <= np.arange(128)[None, :]).astype(np.float32)
    triu4 = np.concatenate([tri] * 4, axis=1).astype(np.float32)
    bd8 = np.zeros((128, 8), dtype=np.float32)
    for h in range(8):
        bd8[h*16:(h+1)*16, h] = 1.0
    bd8t = np.ascontiguousarray(bd8.T)
    ones1 = np.ones((1, 128), dtype=np.float32)
    return dict(identf=eye, identb=eyeb, iota128=iota128, iotacol=iotacol,
                amask=amask, triu4=triu4, bd8=bd8, bd8t=bd8t, ones1=ones1)


def stage_inputs(inputs, core):
    c = host_consts()
    f32 = np.float32
    m = dict(c)
    m["x"] = np.ascontiguousarray(np.asarray(inputs["L0_attn_a_x"], dtype=f32)[core])
    m["wqk"] = np.ascontiguousarray(np.asarray(inputs["L0_attn_a_W_qk"], dtype=f32))
    m["wva"] = np.ascontiguousarray(np.asarray(inputs["L0_attn_a_W_v"], dtype=f32).T)
    m["alibi"] = np.asarray(inputs["L0_attn_a_alibi_slope"], dtype=f32).reshape(1, 1).copy()
    m["wq"] = np.ascontiguousarray(np.asarray(inputs["L0_attn_b_Wq"], dtype=f32).T)
    m["wk"] = np.ascontiguousarray(np.asarray(inputs["L0_attn_b_Wk"], dtype=f32).T)
    m["wv"] = np.ascontiguousarray(np.asarray(inputs["L0_attn_b_Wv"], dtype=f32).T)
    m["wo"] = np.ascontiguousarray(np.asarray(inputs["L0_attn_b_Wo"], dtype=f32).T)
    m["bqF"] = np.ascontiguousarray(np.asarray(inputs["L0_attn_b_bq"], dtype=f32).reshape(8, 128).T)
    m["bkF"] = np.ascontiguousarray(np.asarray(inputs["L0_attn_b_bk"], dtype=f32).reshape(8, 128).T)
    m["bvrow"] = np.asarray(inputs["L0_attn_b_bv"], dtype=f32).reshape(1, D).copy()
    m["borow"] = np.asarray(inputs["L0_attn_b_bo"], dtype=f32).reshape(1, D).copy()
    m["beta"] = np.asarray(inputs["L0_attn_b_beta"], dtype=f32).reshape(1, 1).copy()
    m["w1t"] = np.ascontiguousarray(np.asarray(inputs["L0_ffn_w1"], dtype=f32).T)
    m["w2t"] = np.ascontiguousarray(np.asarray(inputs["L0_ffn_w2"], dtype=f32).T)
    return m


def kernel(**inputs):
    import numpy as np
    from concourse.bass_utils import run_bass_kernel_spmd
    n = 8
    nc = MK(phases=("A", "B", "C", "D")).build()
    in_maps = [stage_inputs(inputs, c) for c in range(n)]
    res = run_bass_kernel_spmd(nc, in_maps, core_ids=list(range(n)))
    return np.stack([np.asarray(res.results[c]["y"]) for c in range(n)], axis=0)


# version 18
# speedup vs baseline: 1.0457x; 1.0164x over previous
import contextlib
import numpy as np
import concourse.bass as bass
import concourse.mybir as mybir

F32 = mybir.dt.float32
BF16 = mybir.dt.bfloat16
I32 = mybir.dt.int32
AF = mybir.ActivationFunctionType
OP = mybir.AluOpType
AX = mybir.AxisListType

T = 2048
D = 1024
HID = 4096
NT = T // 128
NK = D // 128
SCALE_A = 1.0 / 32.0
NEG = -1e30


class Rot:

    def __init__(self, mk, sem, n):
        self.mk, self.sem, self.n = mk, sem, n
        self.rel = []
        self.i = 0

    def acquire(self, eng):
        if self.i >= self.n:
            ev = self.rel[self.i - self.n]
            self.mk.w(eng, ev[0], ev[1])
        s = self.i % self.n
        self.i += 1
        return s

    def release(self, ev=None):
        if ev is None:
            ev = (self.sem, self.mk.cnt[self.sem])
        self.rel.append(ev)


class EngW:
    _PASS = {"wait_ge", "drain", "nop", "sem_inc", "sem_clear"}

    def __init__(self, eng):
        object.__setattr__(self, "_e", eng)
        object.__setattr__(self, "last", None)

    def __getattr__(self, n):
        f = getattr(self._e, n)
        if n in EngW._PASS or not callable(f):
            return f
        def call(*a, **k):
            r = f(*a, **k)
            object.__setattr__(self, "last", r)
            return r
        return call


class MK:
    def __init__(self, phases=("A", "B", "C", "D"), a_stop=99, c_stop=99):
        self.a_stop = a_stop
        self.c_stop = c_stop
        self.phases = phases
        nc = bass.Bass("TRN2", target_bir_lowering=False)
        self.nc = nc
        self.st = contextlib.ExitStack()
        self.cnt = {}
        self.sem = {}
        for name in ["dS", "dA", "dG", "sT", "sV", "sS",
                     "X0", "X1", "XR0", "XR1", "W0", "W1", "W2", "W3",
                     "SH", "CST", "SO0", "SO1", "SO2", "SO3"]:
            self.sem[name] = self.st.enter_context(nc.semaphore(name))
            self.cnt[name] = 0
        self.V, self.S, self.TE = EngW(nc.vector), EngW(nc.scalar), EngW(nc.tensor)
        self.SY, self.G = EngW(nc.sync), EngW(nc.gpsimd)
        self._declare_io()

    def inc(self, res, name, n=1):
        res.then_inc(self.sem[name], n)
        self.cnt[name] += n
        return self.cnt[name]

    def bump(self, name, n):
        self.cnt[name] += n
        return self.cnt[name]

    def w(self, eng, name, v):
        if v > 0:
            eng.wait_ge(self.sem[name], v)

    def dma(self, eng, out, in_, slot):
        r = eng.dma_start(out=out, in_=in_)
        r.then_inc(self.sem[slot], 16)
        self.cnt[slot] += 16
        return (slot, self.cnt[slot])

    def _declare_io(self):
        nc = self.nc
        dt_ = lambda n, shp, dt=F32, kind="ExternalInput": nc.dram_tensor(n, shp, dt, kind=kind).ap()
        self.d_x = dt_("x", [T, D])
        self.d_y = dt_("y", [T, D], kind="ExternalOutput")
        self.d_wqk = dt_("wqk", [D, D])
        self.d_wva = dt_("wva", [D, D])
        self.d_alibi = dt_("alibi", [1, 1])
        self.d_wq = dt_("wq", [D, D])
        self.d_wk = dt_("wk", [D, D])
        self.d_wv = dt_("wv", [D, D])
        self.d_wo = dt_("wo", [D, D])
        self.d_bqF = dt_("bqF", [128, 8])
        self.d_bkF = dt_("bkF", [128, 8])
        self.d_bvrow = dt_("bvrow", [1, D])
        self.d_borow = dt_("borow", [1, D])
        self.d_beta = dt_("beta", [1, 1])
        self.d_w1t = dt_("w1t", [D, HID])
        self.d_w2t = dt_("w2t", [HID, D])
        self.d_identf = dt_("identf", [128, 128])
        self.d_identb = dt_("identb", [128, 128], BF16)
        self.d_iota128 = dt_("iota128", [128, T])
        self.d_iotacol = dt_("iotacol", [128, 1])
        self.d_amask = dt_("amask", [128, 128])
        self.d_triu4 = dt_("triu4", [128, 512])
        self.d_bd8 = dt_("bd8", [128, 8])
        self.d_bd8t = dt_("bd8t", [8, 128])
        self.d_ones1 = dt_("ones1", [1, 128])
        self.d_r1 = dt_("r1", [T, D], kind="Internal")
        self.d_r2 = dt_("r2", [T, D], kind="Internal")
        self.d_r3 = dt_("r3", [T, D], kind="Internal")
        self.d_w1q = dt_("w1q", [D, HID], BF16, kind="Internal")
        self.d_w2q = dt_("w2q", [HID, D], BF16, kind="Internal")
        self.d_s12 = dt_("s12", [1, 2], kind="Internal")
        self.dbg_xq = dt_("dbg_xq", [T, D], BF16, kind="Internal")
        self.dbg_h = dt_("dbg_h", [T, HID], kind="Internal")
        self.dbg_hq = dt_("dbg_hq", [T, HID], BF16, kind="Internal")
        self.dbg_am = dt_("dbg_am", [T, 4], kind="Internal")
        self.dbg_yq = dt_("dbg_yq", [T, D], kind="Internal")
        self.dbg_i1 = dt_("dbg_i1", [T, D], I32, kind="Internal")
        self.dbg_dl = dt_("dbg_dl", [T, D], kind="Internal")

    def sb(self, name, shape, dt=F32):
        return self.st.enter_context(self.nc.sbuf_tensor(name, shape, dt))

    def setup_consts(self):
        sb = self.sb
        self.identf = sb("c_identf", [128, 128])
        self.identb = sb("c_identb", [128, 128], BF16)
        self.iotacol = sb("c_iotacol", [128, 1])
        self.bd8 = sb("c_bd8", [128, 8])
        self.bd8t = sb("c_bd8t", [8, 128])
        self.ones1 = sb("c_ones1", [1, 128])
        self.slope_b = sb("c_slope_b", [128, 1])
        self.be_b = sb("c_be_b", [128, 1])
        self.bebq = sb("c_bebq", [128, 8])
        self.bebk = sb("c_bebk", [128, 8])
        self.alibi_sb = sb("c_alibi", [1, 1])
        self.beta_sb = sb("c_beta", [1, 1])
        self.bqF_sb = sb("c_bqF", [128, 8])
        self.bkF_sb = sb("c_bkF", [128, 8])
        self.s1b = sb("c_s1b", [128, 1])
        self.s2b = sb("c_s2b", [128, 1])
        self.is1b = sb("c_is1b", [128, 1])
        self.is2b = sb("c_is2b", [128, 1])
        self.epsb = sb("c_epsb", [128, 1])
        self.s12row = sb("c_s12row", [1, 2])

        with contextlib.ExitStack() as ph:
            ps_bc = ph.enter_context(self.nc.psum_tensor("c_ps", [128, 2], F32))
            with self.nc.Block() as blk:
                def body(_):
                    V, S, TE, SY = self.V, self.S, self.TE, self.SY
                    V.memset(self.epsb[:], 1e-5)
                    V.drain()
                    for dst, src in [(self.identf, self.d_identf), (self.identb, self.d_identb),
                                     (self.iotacol, self.d_iotacol), (self.bd8, self.d_bd8),
                                     (self.bd8t, self.d_bd8t), (self.ones1, self.d_ones1),
                                     (self.alibi_sb, self.d_alibi), (self.beta_sb, self.d_beta),
                                     (self.bqF_sb, self.d_bqF), (self.bkF_sb, self.d_bkF)]:
                        SY.dma_start(out=dst[:], in_=src[:]).then_inc(self.sem["dS"], 16)
                        self.bump("dS", 16)
                    t0 = self.cnt["dS"]
                    self.w(S, "dS", t0)
                    S.activation(self.be_b[0:1, :], self.beta_sb[:], AF.Exp)
                    s0 = self.inc(S.last, "sS")
                    self.w(TE, "dS", t0)
                    self.w(TE, "sS", s0)
                    TE.matmul(ps_bc[:, 0:1], self.ones1[:], self.alibi_sb[:], start=True, stop=True)
                    TE.matmul(ps_bc[:, 1:2], self.ones1[:], self.be_b[0:1, :], start=True, stop=True)
                    tt = self.inc(TE.last, "sT")
                    self.w(V, "sT", tt)
                    V.tensor_copy(self.slope_b[:], ps_bc[:, 0:1])
                    V.tensor_copy(self.be_b[:], ps_bc[:, 1:2])
                    V.drain()
                    V.tensor_scalar(self.bebq[:], self.bqF_sb[:], self.be_b[:], None, OP.mult)
                    V.tensor_scalar(self.bebk[:], self.bkF_sb[:], self.be_b[:], None, OP.mult)
                    self.inc(V.last, "sV")
                blk.vector(body)

    def emit_ln(self, xt, sc, quant=False, pre_wait=(), s_pre_wait=()):
        V, S = self.V, self.S
        for s_, v_ in pre_wait:
            self.w(V, s_, v_)
        V.reduce_sum(sc["s1"], xt, axis=AX.X)
        V.drain()
        V.tensor_scalar(sc["negm"], sc["s1"], -1.0 / D, None, OP.mult)
        V.drain()
        v1 = self.inc(V.last, "sV")
        for s_, v_ in s_pre_wait:
            self.w(S, s_, v_)
        self.w(S, "sV", v1)
        S.activation(sc["xc"], xt, AF.Identity, bias=sc["negm"], scale=1.0)
        S.drain()
        S.activation(sc["sqd"], sc["xc"], AF.Square, accum_out=sc["ssq"])
        S.drain()
        S.activation(sc["sdt"], sc["ssq"], AF.Sqrt, bias=self.epsb[:], scale=1.0 / D)
        S.drain()
        s1 = self.inc(S.last, "sS")
        self.w(V, "sS", s1)
        V.reciprocal(sc["rstd"], sc["sdt"])
        V.drain()
        if quant:
            V.reduce_max(sc["axc"], sc["xc"], axis=AX.X, apply_absolute_value=True)
            V.drain()
            V.tensor_scalar(sc["mxn"], sc["axc"], sc["rstd"], None, OP.mult)
            V.drain()
            V.tensor_scalar(sc["mxn"], sc["mxn"], 1e-5, None, OP.max)
            V.drain()
            V.reciprocal(sc["iam"], sc["mxn"])
            V.drain()
            V.tensor_scalar(sc["scq"], sc["iam"], 127.0, None, OP.mult)
            V.drain()
            V.tensor_scalar(sc["comb"], sc["scq"], sc["rstd"], None, OP.mult)
            V.drain()
        v2 = self.inc(V.last, "sV")
        return v2, s1

    def ln_scratch(self, ph, pfx, quant=False, sqd_ap=None, xc_ap=None):
        sbl = lambda n, shp, dt=F32: ph.enter_context(self.nc.sbuf_tensor(n, shp, dt))[:]
        sc = {"s1": sbl(pfx + "_s1", [128, 1]), "negm": sbl(pfx + "_negm", [128, 1]),
              "ssq": sbl(pfx + "_ssq", [128, 1]), "sdt": sbl(pfx + "_sdt", [128, 1]),
              "rstd": sbl(pfx + "_rstd", [128, 1])}
        sc["xc"] = xc_ap if xc_ap is not None else sbl(pfx + "_xc", [128, D])
        sc["sqd"] = sqd_ap if sqd_ap is not None else sbl(pfx + "_sqd", [128, D])
        if quant:
            for k in ["axc", "mxn", "iam", "scq", "comb"]:
                sc[k] = sbl(pfx + "_" + k, [128, 1])
        return sc

    def phase_A(self):
        nc = self.nc
        V, S, TE, SY, G = self.V, self.S, self.TE, self.SY, self.G
        with contextlib.ExitStack() as ph:
            sbl = lambda n, shp, dt=F32: ph.enter_context(nc.sbuf_tensor(n, shp, dt))
            psl = lambda n, shp, dt=F32: ph.enter_context(nc.psum_tensor(n, shp, dt))
            XnF = sbl("A_XnF", [128, NK, T])
            Vtm = sbl("A_Vtm", [128, NT, D])
            wbuf = sbl("A_w", [128, NK, 512])
            qF = sbl("A_qF", [128, NK, 512])
            Ssb = sbl("A_S", [128, T])
            Ssb2 = sbl("A_S2", [128, T])
            scr = sbl("A_scr", [128, T])
            PT = sbl("A_PT", [128, 2, 128])
            xt = sbl("A_xt", [128, 2, D])
            B0 = sbl("A_B0", [128, T])
            amask = sbl("A_amask", [128, 128])
            lnsc = self.ln_scratch(ph, "A_ln", sqd_ap=scr[:, 0:D], xc_ap=Ssb[:, 0:D])
            xn = scr[:, D:2*D]
            mx = sbl("A_mx", [128, 1])
            nmx = sbl("A_nmx", [128, 1])
            sume = sbl("A_sume", [128, 1])
            rs = sbl("A_rs", [128, 1])
            qbv = sbl("A_qbv", [128, 1])
            ps_tr = psl("A_ps_tr", [128, 2, 512])
            ps_v = psl("A_ps_v", [128, 2, 512])
            ps_s = psl("A_ps_s", [128, 2, 512])
            ps_o = psl("A_ps_o", [128, 2, 512])
            wflat = wbuf[:].rearrange("p k n -> p (k n)")

            with nc.Block() as blk:
                def body(_):
                    rot_xt = Rot(self, "sS", 2)
                    rot_tr = Rot(self, "sV", 2)
                    rot_pv = Rot(self, "sV", 2)
                    rot_ps = Rot(self, "sV", 2)
                    so_tgt = {0: 0, 1: 0}
                    rot_wo = Rot(self, "sT", 4)
                    ot_i = [0]

                    self.dma(SY, amask[:], self.d_amask[:], "CST")
                    cst_t = self.dma(SY, B0[:], self.d_iota128[:], "CST")
                    last_s_done = 0
                    xn_free = 0
                    for t in range(NT):
                        sl = rot_xt.acquire(SY)
                        xw = self.dma(SY, xt[:, sl, :], self.d_x[t*128:(t+1)*128, :],
                                      "X%d" % sl)
                        if t == 0:
                            self.w(V, *cst_t)
                            V.tensor_scalar(B0[:], B0[:], self.slope_b[:], None, OP.mult)
                            V.drain()
                            self.inc(V.last, "sV")
                        v_done, s_done = self.emit_ln(
                            xt[:, sl, :], lnsc,
                            pre_wait=[xw, ("sS", last_s_done)])
                        self.w(S, "sV", v_done)
                        self.w(S, "sT", xn_free)
                        S.activation(xn, lnsc["xc"], AF.Identity, scale=lnsc["rstd"])
                        S.drain()
                        s_xn = self.inc(S.last, "sS")
                        last_s_done = s_xn
                        rot_xt.release()
                        for k in range(NK):
                            trs = rot_tr.acquire(TE)
                            self.w(TE, "sS", s_xn)
                            TE.transpose(ps_tr[:, trs, 0:128], xn[:, k*128:(k+1)*128], self.identf[:])
                            tt_ = self.inc(TE.last, "sT")
                            self.w(V, "sT", tt_)
                            V.tensor_copy(XnF[:, k, t*128:(t+1)*128], ps_tr[:, trs, 0:128])
                            self.inc(V.last, "sV")
                            rot_tr.release()
                        xn_free = self.cnt["sT"]
                    xnf_all = self.cnt["sV"]
                    if self.a_stop <= 1:
                        self.w(G, "sV", self.cnt["sV"])
                        return

                    for half in range(2):
                        self.w(SY, "sT", self.cnt["sT"])
                        ww = self.dma(SY, wbuf[:],
                                      self.d_wva[:, half*512:(half+1)*512].rearrange(
                                          "(k p) n -> p k n", p=128), "W0")
                        for t in range(NT):
                            pvs = rot_pv.acquire(TE)
                            self.w(TE, *ww)
                            self.w(TE, "sV", xnf_all)
                            for k in range(NK):
                                TE.matmul(ps_v[:, pvs, :], XnF[:, k, t*128:(t+1)*128],
                                          wbuf[:, k, :], start=(k == 0), stop=(k == NK - 1))
                            tt_ = self.inc(TE.last, "sT")
                            self.w(V, "sT", tt_)
                            V.tensor_copy(Vtm[:, t, half*512:(half+1)*512], ps_v[:, pvs, :])
                            self.inc(V.last, "sV")
                            rot_pv.release()
                    a1b_T = self.cnt["sT"]
                    if self.a_stop <= 2:
                        self.w(G, "sV", self.cnt["sV"])
                        return

                    ssb_free = {0: 0, 1: 0}
                    prev_sm_s = 0
                    prev_ot_v = 0
                    ot2 = [scr[:, 0:D], scr[:, D:2*D]]
                    self.w(SY, "sT", a1b_T)
                    for qc in range(4):
                        for o in range(NK):
                            wos = rot_wo.acquire(SY)
                            ow = self.dma(SY, wflat[:, wos*1024:(wos+1)*1024].rearrange(
                                              "p (k n) -> p k n", n=128),
                                          self.d_wqk[:, o*128:(o+1)*128].rearrange(
                                              "(k p) n -> p k n", p=128), "W%d" % wos)
                            pvs = rot_pv.acquire(TE)
                            self.w(TE, *ow)
                            for k in range(NK):
                                TE.matmul(ps_v[:, pvs, :],
                                          wflat[:, wos*1024 + k*128: wos*1024 + (k+1)*128],
                                          XnF[:, k, qc*512:(qc+1)*512],
                                          start=(k == 0), stop=(k == NK - 1))
                            tt_ = self.inc(TE.last, "sT")
                            rot_wo.release()
                            self.w(V, "sT", tt_)
                            V.tensor_copy(qF[:, o, :], ps_v[:, pvs, :])
                            self.inc(V.last, "sV")
                            rot_pv.release()
                        qf_done = self.cnt["sV"]
                        if self.a_stop <= 3:
                            continue

                        for qt in range(4):
                            q0 = qc * 512 + qt * 128
                            qi = q0 // 128
                            Sbuf = [Ssb, Ssb2][qi % 2]
                            xrs = rot_xt.acquire(S)
                            rw = self.dma(S, xt[:, xrs, :], self.d_x[q0:q0+128, :],
                                          "XR%d" % xrs)
                            V.tensor_scalar(qbv[:], self.iotacol[:], float(q0),
                                            self.slope_b[:], OP.add, OP.mult)
                            V.drain()
                            qb_v = self.inc(V.last, "sV")
                            kmax = q0 + 128
                            nkb = (kmax + 511) // 512
                            for kb in range(nkb):
                                kw = min(512, kmax - kb * 512)
                                pss = rot_ps.acquire(TE)
                                self.w(TE, "sV", qf_done)
                                for dch in range(NK):
                                    TE.matmul(ps_s[:, pss, 0:kw],
                                              qF[:, dch, qt*128:(qt+1)*128],
                                              XnF[:, dch, kb*512:kb*512+kw],
                                              start=(dch == 0), stop=(dch == NK - 1))
                                tt_ = self.inc(TE.last, "sT")
                                self.w(V, "sT", tt_)
                                self.w(V, "sV", qb_v)
                                self.w(V, "sT", ssb_free[qi % 2])
                                V.tensor_scalar(Sbuf[:, kb*512:kb*512+kw], ps_s[:, pss, 0:kw],
                                                SCALE_A, qbv[:], OP.mult, OP.subtract)
                                V.drain()
                                V.tensor_tensor(Sbuf[:, kb*512:kb*512+kw],
                                                Sbuf[:, kb*512:kb*512+kw],
                                                B0[:, kb*512:kb*512+kw], OP.add)
                                V.drain()
                                self.inc(V.last, "sV")
                                rot_ps.release()
                            V.tensor_tensor(Sbuf[:, q0:q0+128], Sbuf[:, q0:q0+128],
                                            amask[:], OP.add)
                            V.drain()
                            V.reduce_max(mx[:], Sbuf[:, 0:kmax], axis=AX.X)
                            V.drain()
                            self.w(V, "sS", prev_sm_s)
                            V.tensor_scalar(nmx[:], mx[:], -1.0, None, OP.mult)
                            V.drain()
                            sm_v = self.inc(V.last, "sV")
                            self.w(S, "sV", sm_v)
                            S.activation(Sbuf[:, 0:kmax], Sbuf[:, 0:kmax], AF.Exp,
                                         bias=nmx[:], scale=1.0, accum_out=sume[:])
                            S.drain()
                            sm_s = self.inc(S.last, "sS")
                            prev_sm_s = sm_s
                            self.w(V, "sS", sm_s)
                            V.reciprocal(rs[:], sume[:])
                            V.drain()
                            rs_v = self.inc(V.last, "sV")
                            if self.a_stop <= 4:
                                self.w(V, "sV", rs_v)
                                V.tensor_copy(lnsc["negm"], rs[:])
                                self.inc(V.last, "sV")
                                rot_xt.release()
                                continue
                            if self.a_stop == 5:
                                ssb_free_T = self.cnt["sT"]
                            nb = q0 // 128 + 1
                            for kb2 in range(nb):
                                trs = rot_tr.acquire(TE)
                                self.w(TE, "sS", sm_s)
                                if kb2 == 0:
                                    self.w(TE, "sV", prev_ot_v)
                                TE.transpose(ps_tr[:, trs, 0:128], Sbuf[:, kb2*128:(kb2+1)*128],
                                             self.identf[:])
                                tt_ = self.inc(TE.last, "sT")
                                self.w(V, "sT", tt_)
                                V.tensor_copy(PT[:, trs, :], ps_tr[:, trs, 0:128])
                                ev = self.inc(V.last, "sV")
                                rot_tr.release()
                                if self.a_stop <= 5:
                                    if kb2 == nb - 1:
                                        pass
                                    continue
                                self.w(TE, "sV", ev)
                                for half in range(2):
                                    TE.matmul(ps_o[:, half, :], PT[:, trs, :],
                                              Vtm[:, kb2, half*512:(half+1)*512],
                                              start=(kb2 == 0), stop=(kb2 == nb - 1))
                            if self.a_stop <= 5:
                                ssb_free_T = self.cnt["sT"]
                                self.w(V, "sV", self.cnt["sV"])
                                V.tensor_copy(lnsc["negm"], rs[:])
                                self.inc(V.last, "sV")
                                rot_xt.release()
                                continue
                            ssb_free_T = self.inc(TE.last, "sT")
                            if self.a_stop <= 6:
                                self.w(V, "sT", ssb_free_T)
                                V.tensor_copy(lnsc["negm"], rs[:])
                                self.inc(V.last, "sV")
                                rot_xt.release()
                                continue
                            ots = ot_i[0] % 2
                            ot_i[0] += 1
                            self.w(V, "SO%d" % ots, so_tgt[ots])
                            self.w(V, "sT", ssb_free_T)
                            self.w(V, "sV", rs_v)
                            self.w(V, *rw)
                            for half in range(2):
                                V.tensor_scalar(ot2[ots][:, half*512:(half+1)*512],
                                                ps_o[:, half, :], rs[:], None, OP.mult)
                            V.drain()
                            V.tensor_tensor(ot2[ots], ot2[ots], xt[:, xrs, :], OP.add)
                            V.drain()
                            prev_ot_v = self.inc(V.last, "sV")
                            rot_xt.release(("sV", prev_ot_v))
                            if self.a_stop <= 7:
                                continue
                            self.w(SY, "sV", prev_ot_v)
                            _, so_tgt[ots] = self.dma(SY, self.d_r1[q0:q0+128, :], ot2[ots],
                                                      "SO%d" % ots)
                    self.w(G, "SO0", self.cnt["SO0"])
                    self.w(G, "SO1", self.cnt["SO1"])
                blk.vector(body)

    def phase_B(self, src, dst, first, tag):
        nc = self.nc
        V, S, TE, SY, G = self.V, self.S, self.TE, self.SY, self.G
        with contextlib.ExitStack() as ph:
            sbl = lambda n, shp, dt=F32: ph.enter_context(nc.sbuf_tensor(tag + n, shp, dt))
            psl = lambda n, shp, dt=F32: ph.enter_context(nc.psum_tensor(tag + n, shp, dt))
            W1 = sbl("_W1", [128, NK, HID], BF16)
            W2 = sbl("_W2", [128, HID // 128, D], BF16)
            if first:
                self.cache_stores = []
                self._ternarize(W1, W2, tag)
            else:
                with nc.Block() as blk0:
                    def body0(_):
                        self.w(SY, "dG", self.cnt["dG"])
                        self.dma(SY, W1[:],
                                 self.d_w1q[:].rearrange("(k p) n -> p k n", p=128), "CST")
                        lw = self.dma(SY, W2[:],
                                      self.d_w2q[:].rearrange("(k p) n -> p k n", p=128),
                                      "CST")
                        self.w(SY, *lw)
                    blk0.vector(body0)

            rt = sbl("_rt", [128, 4, D])
            lnsc = self.ln_scratch(ph, tag + "_ln", quant=True)
            qs = sbl("_qs", [128, 1024])
            qi = psl("_qi", [128, 1024], I32)
            qd = sbl("_qd", [128, 1024])
            qb = sbl("_qb", [128, 1024])
            xqm = sbl("_xqm", [128, D], BF16)
            XqF = sbl("_XqF", [128, NK, 128], BF16)
            hbuf = sbl("_h", [128, HID])
            am2 = sbl("_am2", [128, 1])
            am2c = sbl("_am2c", [128, 1])
            iam2 = sbl("_iam2", [128, 1])
            scq2 = sbl("_scq2", [128, 1])
            hqm = sbl("_hqm", [128, HID], BF16)
            HqF = sbl("_HqF", [128, HID // 128, 128], BF16)
            dq1 = sbl("_dq1", [128, 2])
            dq2 = sbl("_dq2", [128, 2])
            ot = sbl("_ot", [128, 1, D])
            ps_h = psl("_ps_h", [128, 2, 512])
            ps_o = psl("_ps_o", [128, 2, 512])
            ps_tb = psl("_ps_tb", [128, 2, 1024], BF16)

            def round_slices(src_ap, dst_ap, width, v_wait, s_wait):
                for j in range(width // 512):
                    ssl = src_ap[:, j*512:(j+1)*512]
                    self.w(S, "sV", s_wait)
                    S.copy(qi[:], ssl)
                    sq = self.inc(S.last, "sS")
                    self.w(V, "sS", sq)
                    V.tensor_tensor(qd[:], ssl, qi[:], OP.subtract)
                    V.drain()
                    V.scalar_tensor_tensor(qb[:], qd[:], 0.5, qi[:], OP.is_ge, OP.add)
                    V.drain()
                    V.scalar_tensor_tensor(dst_ap[:, j*512:(j+1)*512], qd[:], -0.5,
                                           qb[:], OP.is_lt, OP.subtract)
                    V.drain()
                    s_wait = self.inc(V.last, "sV")
                return self.cnt["sV"]

            levels = 1 if first else 3
            levels2 = 1
            with nc.Block() as blk:
                def body(_):
                    rot_rt = Rot(self, "sV", 4)
                    so_tgt = {0: 0, 1: 0}
                    ot_i = [0]
                    rot_h = Rot(self, "sS", 2)
                    rot_ov = Rot(self, "sV", 2)
                    rot_tb = Rot(self, "sV", 2)
                    st = {"pipe_s": 0, "pipe_v": 0, "xq_tr_T": 0, "hq_tr_T": 0,
                          "q2_free_v": 0, "v_q2": 0}
                    pend = []
                    rt_rel_idx = {}
                    s1 = {}

                    def stage1a(t):
                        sl = rot_rt.acquire(SY)
                        xw = self.dma(SY, rt[:, sl, :], src[t*128:(t+1)*128, :],
                                      ["X0", "X1", "XR0", "XR1"][sl])
                        self.emit_ln(
                            rt[:, sl, :], lnsc, quant=True,
                            pre_wait=[xw],
                            s_pre_wait=[("sS", st["pipe_s"]), ("sV", st["pipe_v"])])
                        V.tensor_scalar(dq1[:, t % 2:t % 2 + 1], lnsc["mxn"][:],
                                        1.0 / 127.0, self.s1b[:], OP.mult, OP.mult)
                        V.drain()
                        v_dq1 = self.inc(V.last, "sV")
                        self.w(S, "sV", v_dq1)
                        self.w(S, "sV", st["q2_free_v"])
                        S.activation(qs[:], lnsc["xc"], AF.Identity, scale=lnsc["comb"])
                        S.drain()
                        S.copy(qi[:], qs[:])
                        sq = self.inc(S.last, "sS")
                        self.w(V, "sS", sq)
                        V.tensor_tensor(qd[:], qs[:], qi[:], OP.subtract)
                        V.drain()
                        V.scalar_tensor_tensor(qb[:], qd[:], 0.5, qi[:], OP.is_ge, OP.add)
                        V.drain()
                        st["pipe_v"] = v_dq1
                        st["pipe_s"] = sq
                        s1[t] = {"sl": sl, "v_dq1": v_dq1}

                    def stage1f(t):
                        self.w(V, "sT", st["xq_tr_T"])
                        if levels == 1:
                            V.scalar_tensor_tensor(xqm[:], qd[:], -0.5,
                                                   qb[:], OP.is_lt, OP.subtract)
                            V.drain()
                        else:
                            V.scalar_tensor_tensor(qs[:], qd[:], 1.5, qb[:], OP.is_ge, OP.add)
                            V.drain()
                            V.scalar_tensor_tensor(qb[:], qd[:], 2.5, qs[:], OP.is_ge, OP.add)
                            V.drain()
                            V.scalar_tensor_tensor(qs[:], qd[:], -0.5, qb[:], OP.is_lt, OP.subtract)
                            V.drain()
                            V.scalar_tensor_tensor(qb[:], qd[:], -1.5, qs[:], OP.is_lt, OP.add)
                            V.drain()
                            V.scalar_tensor_tensor(xqm[:], qd[:], -2.5,
                                                   qb[:], OP.is_lt, OP.add)
                            V.drain()
                        s1[t]["vq"] = self.inc(V.last, "sV")
                        rot_rt.release()
                        rt_rel_idx[t] = len(rot_rt.rel) - 1

                    def xq_transposes(t):
                        for k in range(NK):
                            tbs = rot_tb.acquire(TE)
                            self.w(TE, "sV", s1[t]["vq"])
                            TE.transpose(ps_tb[:, tbs, 0:128], xqm[:, k*128:(k+1)*128],
                                         self.identb[:])
                            tt_ = self.inc(TE.last, "sT")
                            self.w(S, "sT", tt_)
                            S.copy(XqF[:, k, :], ps_tb[:, tbs, 0:128])
                            S.drain()
                            self.inc(S.last, "sS")
                            rot_tb.release(("sS", self.cnt["sS"]))
                        st["xq_tr_T"] = self.cnt["sT"]
                        s1[t]["xqf_done"] = ("sS", self.cnt["sS"])

                    def ffn1_gelu(t, q2=None):
                        for b in range(8):
                            if q2 is not None:
                                q2(b)
                            hs = rot_h.acquire(TE)
                            self.w(TE, *s1[t]["xqf_done"])
                            for k in range(NK):
                                TE.matmul(ps_h[:, hs, :], XqF[:, k, :],
                                          W1[:, k, b*512:(b+1)*512],
                                          start=(k == 0), stop=(k == NK - 1))
                            tt_ = self.inc(TE.last, "sT")
                            self.w(S, "sT", tt_)
                            self.w(S, "sV", s1[t]["v_dq1"])
                            if b == 0:
                                self.w(S, "sV", st["v_q2"])
                            S.activation(hbuf[:, b*512:(b+1)*512], ps_h[:, hs, :],
                                         AF.Gelu, scale=dq1[:, t % 2:t % 2 + 1])
                            self.inc(S.last, "sS")
                            rot_h.release()
                        s1[t]["gelu_done"] = self.cnt["sS"]

                    stage1a(0)
                    stage1f(0)
                    xq_transposes(0)
                    stage1a(1)
                    stage1f(1)
                    ffn1_gelu(0)
                    for t in range(NT):
                        sl = s1[t]["sl"]
                        if pend:
                            pend.pop(0)()
                        self.w(V, "sS", s1[t]["gelu_done"])
                        V.reduce_max(am2[:], hbuf[:], axis=AX.X, apply_absolute_value=True)
                        V.drain()
                        V.tensor_scalar(am2c[:], am2[:], 1e-5, None, OP.max)
                        V.drain()
                        V.reciprocal(iam2[:], am2c[:])
                        V.drain()
                        V.tensor_scalar(scq2[:], iam2[:], 127.0, None, OP.mult)
                        V.drain()
                        V.tensor_scalar(dq2[:, (t % 2):(t % 2)+1], am2c[:], 1.0 / 127.0,
                                        self.s2b[:], OP.mult, OP.mult)
                        V.drain()
                        v_q2 = self.inc(V.last, "sV")
                        st["v_q2"] = v_q2
                        vq_next = s1[t + 1]["vq"] if t + 1 < NT else 0
                        svh = {0: max(v_q2, vq_next), 1: max(v_q2, vq_next)}

                        def q2slice(j2, t=t, svh=svh):
                            h = j2 % 2
                            hs_ = slice(h*512, (h+1)*512)
                            c0 = j2 * 512
                            self.w(S, "sV", svh[h])
                            S.activation(qs[:, hs_], hbuf[:, c0:c0+512],
                                         AF.Identity, scale=scq2[:])
                            S.drain()
                            S.copy(qi[:, hs_], qs[:, hs_])
                            sq = self.inc(S.last, "sS")
                            self.w(V, "sS", sq)
                            V.tensor_tensor(qd[:, hs_], qs[:, hs_], qi[:, hs_],
                                            OP.subtract)
                            V.drain()
                            V.scalar_tensor_tensor(qb[:, hs_], qd[:, hs_], 0.5,
                                                   qi[:, hs_], OP.is_ge, OP.add)
                            V.drain()
                            if j2 == 0:
                                self.w(V, "sT", st["hq_tr_T"])
                            V.scalar_tensor_tensor(hqm[:, c0:c0+512], qd[:, hs_], -0.5,
                                                   qb[:, hs_], OP.is_lt, OP.subtract)
                            V.drain()
                            svh[h] = self.inc(V.last, "sV")

                        if t + 1 < NT:
                            xq_transposes(t + 1)
                            ffn1_gelu(t + 1, q2=q2slice)
                        else:
                            for j2 in range(8):
                                q2slice(j2)
                        st["q2_free_v"] = max(svh[0], svh[1])
                        v_hq = self.cnt["sV"]
                        for k in range(HID // 128):
                            tbs = rot_tb.acquire(TE)
                            self.w(TE, "sV", v_hq)
                            TE.transpose(ps_tb[:, tbs, 0:128], hqm[:, k*128:(k+1)*128],
                                         self.identb[:])
                            tt_ = self.inc(TE.last, "sT")
                            self.w(S, "sT", tt_)
                            S.copy(HqF[:, k, :], ps_tb[:, tbs, 0:128])
                            S.drain()
                            self.inc(S.last, "sS")
                            rot_tb.release(("sS", self.cnt["sS"]))
                        st["hq_tr_T"] = self.cnt["sT"]
                        hqf_done = ("sS", self.cnt["sS"])
                        ots = 0
                        ffn2_T = []
                        for half in range(2):
                            ovs = rot_ov.acquire(TE)
                            self.w(TE, *hqf_done)
                            for k in range(HID // 128):
                                TE.matmul(ps_o[:, ovs, :], HqF[:, k, :],
                                          W2[:, k, half*512:(half+1)*512],
                                          start=(k == 0), stop=(k == HID // 128 - 1))
                            ffn2_T.append((ovs, self.inc(TE.last, "sT")))
                        if t + 2 < NT:
                            stage1a(t + 2)
                            stage1f(t + 2)
                        if first:
                            for _ in range(3):
                                if self.cache_stores:
                                    dc, wsrc, ev = self.cache_stores.pop(0)
                                    self.w(SY, "sV", ev)
                                    SY.dma_start(out=dc, in_=wsrc).then_inc(
                                        self.sem["dG"], 16)
                                    self.bump("dG", 16)

                        def stageB(t=t, sl=sl, ots=ots, ffn2_T=ffn2_T):
                            first_ot = True
                            for half, (ovs, tt_) in enumerate(ffn2_T):
                                self.w(V, "sT", tt_)
                                if first_ot:
                                    self.w(V, "SO%d" % ots, so_tgt[ots])
                                    first_ot = False
                                V.tensor_scalar(ot[:, ots, half*512:(half+1)*512],
                                                ps_o[:, ovs, :], dq2[:, (t % 2):(t % 2)+1],
                                                None, OP.mult)
                                self.inc(V.last, "sV")
                                rot_ov.release()
                            V.drain()
                            V.tensor_tensor(ot[:, ots, :], ot[:, ots, :], rt[:, sl, :], OP.add)
                            V.drain()
                            ot_v = self.inc(V.last, "sV")
                            rot_rt.rel[rt_rel_idx[t]] = ("sV", ot_v)
                            self.w(SY, "sV", ot_v)
                            _, so_tgt[ots] = self.dma(SY, dst[t*128:(t+1)*128, :],
                                                      ot[:, ots, :], "SO%d" % ots)
                        pend.append(stageB)
                    while pend:
                        pend.pop(0)()
                    self.w(SY, "SO0", self.cnt["SO0"])
                    self.w(SY, "SO1", self.cnt["SO1"])
                    if first:
                        self.w(SY, "dG", self.cnt["dG"])
                blk.vector(body)

    def _ternarize(self, W1, W2, tag):
        nc = self.nc
        V, S, TE, SY, G = self.V, self.S, self.TE, self.SY, self.G
        with contextlib.ExitStack() as p2:
            sbl = lambda n, shp, dt=F32: p2.enter_context(nc.sbuf_tensor(tag + "T" + n, shp, dt))
            wtmp = sbl("_wtmp", [128, 2, HID])
            acc = sbl("_acc", [128, 1])
            part = sbl("_part", [128, 1])
            zt = sbl("_zt", [128, HID])
            tp = sbl("_tp", [128, HID])
            onec = sbl("_onec", [128, 1])
            psm = p2.enter_context(nc.psum_tensor(tag + "T_psm", [128, 2], F32))

            with nc.Block() as blk:
                def body(_):
                    V.memset(onec[:], 1.0)
                    V.memset(acc[:], 0.0)
                    V.drain()
                    self.inc(V.last, "sV")
                    rot_w = Rot(self, "sV", 2)
                    plans = [(self.d_w1t, HID, 8, 0), (self.d_w2t, D, 32, 1)]
                    for dram, width, ntile, idx in plans:
                        for i in range(ntile):
                            sl = rot_w.acquire(SY)
                            lw = self.dma(SY, wtmp[:, sl, 0:width],
                                          dram[i*128:(i+1)*128, :], "X%d" % sl)
                            self.w(V, *lw)
                            V.reduce_sum(part[:], wtmp[:, sl, 0:width], axis=AX.X,
                                         apply_absolute_value=True)
                            V.drain()
                            V.tensor_tensor(acc[:], acc[:], part[:], OP.add)
                            V.drain()
                            self.inc(V.last, "sV")
                            rot_w.release()
                        av = self.cnt["sV"]
                        self.w(TE, "sV", av)
                        TE.matmul(psm[0:1, idx:idx+1], acc[:], onec[:], start=True, stop=True)
                        tt_ = self.inc(TE.last, "sT")
                        self.w(V, "sT", tt_)
                        V.tensor_scalar(self.s12row[0:1, idx:idx+1], psm[0:1, idx:idx+1],
                                        1.0 / (D * HID), None, OP.mult)
                        V.drain()
                        V.memset(acc[:], 0.0)
                        V.drain()
                        self.inc(V.last, "sV")
                    V.tensor_scalar(self.s12row[:], self.s12row[:], 1e-5, None, OP.max)
                    V.drain()
                    sv = self.inc(V.last, "sV")
                    self.w(S, "sV", sv)
                    S.dma_start(out=self.d_s12[:], in_=self.s12row[:]).then_inc(self.sem["dG"], 16)
                    self.bump("dG", 16)
                    self.w(TE, "sV", sv)
                    TE.matmul(psm[:, 0:1], self.ones1[:], self.s12row[0:1, 0:1], start=True, stop=True)
                    TE.matmul(psm[:, 1:2], self.ones1[:], self.s12row[0:1, 1:2], start=True, stop=True)
                    tt_ = self.inc(TE.last, "sT")
                    self.w(V, "sT", tt_)
                    V.tensor_copy(self.s1b[:], psm[:, 0:1])
                    V.tensor_copy(self.s2b[:], psm[:, 1:2])
                    V.drain()
                    V.reciprocal(self.is1b[:], self.s1b[:])
                    V.reciprocal(self.is2b[:], self.s2b[:])
                    V.drain()
                    bv = self.inc(V.last, "sV")

                    plans2 = [(self.d_w1t, self.d_w1q, W1, HID, 8, self.is1b),
                              (self.d_w2t, self.d_w2q, W2, D, 32, self.is2b)]
                    zt_free = 0
                    for dram, dcache, Wsb, width, ntile, isb in plans2:
                        for i in range(ntile):
                            sl = rot_w.acquire(SY)
                            lw = self.dma(SY, wtmp[:, sl, 0:width],
                                          dram[i*128:(i+1)*128, :], "X%d" % sl)
                            self.w(S, *lw)
                            self.w(S, "sV", max(bv, zt_free))
                            S.activation(zt[:, 0:width], wtmp[:, sl, 0:width],
                                         AF.Identity, scale=isb[:])
                            s_z = self.inc(S.last, "sS")
                            self.w(V, "sS", s_z)
                            V.tensor_scalar(tp[:, 0:width], zt[:, 0:width], 0.5, None, OP.is_gt)
                            V.drain()
                            V.scalar_tensor_tensor(Wsb[:, i, :], zt[:, 0:width], -0.5,
                                                   tp[:, 0:width], OP.is_lt, OP.subtract)
                            V.drain()
                            wv_ = self.inc(V.last, "sV")
                            zt_free = wv_
                            rot_w.release()
                            self.cache_stores.append(
                                (dcache[i*128:(i+1)*128, :], Wsb[:, i, :], wv_))
                blk.vector(body)

    def phase_C(self):
        nc = self.nc
        V, S, TE, SY, G = self.V, self.S, self.TE, self.SY, self.G
        CH = 512
        NCH = T // CH
        NS = CH // 128
        with contextlib.ExitStack() as ph:
            sbl = lambda n, shp, dt=F32: ph.enter_context(nc.sbuf_tensor("C" + n, shp, dt))
            psl = lambda n, shp, dt=F32: ph.enter_context(nc.psum_tensor("C" + n, shp, dt))
            XnFc = sbl("_XnF", [128, NK, CH])
            qFd = sbl("_qF", [128, NK, CH])
            kFd = sbl("_kF", [128, NK, CH])
            qS = sbl("_qS", [128, NK, 128])
            kS = sbl("_kS", [128, NK, 128])
            ktok = sbl("_ktok", [128, NS, D])
            Vp = sbl("_Vp", [128, NS, 64 * 17])
            wbufC = sbl("_w", [128, 2, NK, 512])
            Oc = sbl("_Oc", [128, NS, D])
            OF = sbl("_OF", [128, NK, 128])
            rtC = sbl("_rt", [128, NS, D])
            stateSB = sbl("_st", [128, 16 * 68])
            triu4 = sbl("_triu4", [128, 512])
            PTb = sbl("_PT", [128, 2, 512])
            lnsc = self.ln_scratch(ph, "C_ln")
            xn = sbl("_xn", [128, D])
            rden = sbl("_rden", [8, 512])
            zero512 = sbl("_zero512", [128, 512])
            rd4 = sbl("_rd4", [128, 4])
            bvb = sbl("_bvb", [128, D])
            bob = sbl("_bob", [128, D])
            brow = sbl("_brow", [1, D])
            ps_quad = psl("_ps_quad", [128, 4, 512])
            ps_N = psl("_ps_N", [128, 2, 512])
            ps_st = psl("_ps_state", [128, 512])
            ps_d = psl("_ps_d", [128, 512])


            with nc.Block() as blk:
                def body(_):
                    rot_tr = Rot(self, "sV", 2)
                    rot_w = Rot(self, "sT", 2)
                    rot_p = Rot(self, "sV", 2)
                    rot_N = Rot(self, "sV", 2)
                    rot_PT = Rot(self, "sT", 2)
                    rot_ot = Rot(self, "dG", 2)
                    rot_rt = Rot(self, "sV", 4)

                    V.memset(zero512[:], 0.0)
                    V.drain()
                    V.memset(stateSB[:], 0.0)
                    V.drain()
                    V.memset(qS[96:128, :, :], 0.0)
                    V.drain()
                    V.memset(kS[96:128, :, :], 0.0)
                    V.drain()
                    self.inc(V.last, "sV")
                    self.dma(SY, triu4[:], self.d_triu4[:], "CST")
                    for row_d, dst in [(self.d_bvrow, bvb), (self.d_borow, bob)]:
                        self.w(SY, "sT", self.cnt["sT"])
                        bw = self.dma(SY, brow[:], row_d[:], "CST")
                        self.w(TE, *bw)
                        for half in range(2):
                            TE.matmul(ps_d[:, 0:512], self.ones1[:],
                                      brow[:, half*512:(half+1)*512], start=True, stop=True)
                            tt_ = self.inc(TE.last, "sT")
                            self.w(V, "sT", tt_)
                            V.tensor_copy(dst[:, half*512:(half+1)*512], ps_d[:, 0:512])
                            fv = self.inc(V.last, "sV")
                            self.w(TE, "sV", fv)
                    bias_done = self.cnt["sV"]

                    st_upd_T = 0
                    chunk_T = 0
                    last_s = 0

                    for c in range(NCH):
                        for s_ in range(NS):
                            t0 = c * NS + s_
                            sl = rot_rt.acquire(SY)
                            xw = self.dma(SY, rtC[:, sl, :],
                                          self.d_r2[t0*128:(t0+1)*128, :],
                                          ["X0", "X1", "XR0", "XR1"][sl])
                            v_done, s_done = self.emit_ln(
                                rtC[:, sl, :], lnsc,
                                pre_wait=[xw, ("sS", last_s), ("sT", chunk_T)])
                            self.w(S, "sV", v_done)
                            self.w(S, "sT", self.cnt["sT"])
                            S.activation(xn[:], lnsc["xc"], AF.Identity, scale=lnsc["rstd"])
                            S.drain()
                            s_xn = self.inc(S.last, "sS")
                            last_s = s_xn
                            for k in range(NK):
                                trs = rot_tr.acquire(TE)
                                self.w(TE, "sS", s_xn)
                                TE.transpose(ps_quad[:, trs, 0:128], xn[:, k*128:(k+1)*128],
                                             self.identf[:])
                                tt_ = self.inc(TE.last, "sT")
                                self.w(V, "sT", tt_)
                                V.tensor_copy(XnFc[:, k, s_*128:(s_+1)*128], ps_quad[:, trs, 0:128])
                                self.inc(V.last, "sV")
                                rot_tr.release()
                        xnf_done = self.cnt["sV"]
                        if self.c_stop <= 1:
                            continue

                        for half in range(2):
                            ws = rot_w.acquire(SY)
                            w_ld = self.dma(SY, wbufC[:, ws, :, :],
                                            self.d_wq[:, half*512:(half+1)*512].rearrange(
                                                "(k p) n -> p k n", p=128), "W%d" % ws)
                            for oo in range(4):
                                o = half * 4 + oo
                                pps = rot_p.acquire(TE)
                                self.w(TE, *w_ld)
                                self.w(TE, "sV", xnf_done)
                                for k in range(NK):
                                    TE.matmul(ps_quad[:, 2 + pps, :], wbufC[:, ws, k, oo*128:(oo+1)*128],
                                              XnFc[:, k, :], start=(k == 0), stop=(k == NK - 1))
                                tt_ = self.inc(TE.last, "sT")
                                self.w(S, "sT", tt_)
                                S.activation(qFd[:, o, :], ps_quad[:, 2 + pps, :], AF.Exp,
                                             bias=self.bebq[:, o:o+1], scale=self.be_b[:])
                                S.drain()
                                sev = self.inc(S.last, "sS")
                                rot_p.release(("sS", sev))
                            rot_w.release()
                        q_done_s = self.cnt["sS"]
                        if self.c_stop <= 2:
                            continue

                        knorm_v = 0
                        for half in range(2):
                            ws = rot_w.acquire(SY)
                            w_ld = self.dma(SY, wbufC[:, ws, :, :],
                                            self.d_wk[:, half*512:(half+1)*512].rearrange(
                                                "(k p) n -> p k n", p=128), "W%d" % ws)
                            for oo in range(4):
                                o = half * 4 + oo
                                pps = rot_p.acquire(TE)
                                self.w(TE, *w_ld)
                                self.w(TE, "sV", xnf_done)
                                for k in range(NK):
                                    TE.matmul(ps_quad[:, 2 + pps, :], wbufC[:, ws, k, oo*128:(oo+1)*128],
                                              XnFc[:, k, :], start=(k == 0), stop=(k == NK - 1))
                                tt_ = self.inc(TE.last, "sT")
                                self.w(S, "sT", tt_)
                                S.activation(kFd[:, o, :], ps_quad[:, 2 + pps, :], AF.Exp,
                                             bias=self.bebk[:, o:o+1], scale=self.be_b[:])
                                S.drain()
                                sev = self.inc(S.last, "sS")
                                self.w(TE, "sS", sev)
                                self.w(TE, "sV", knorm_v)
                                TE.matmul(ps_d[0:8, 0:512], self.bd8[:], kFd[:, o, :],
                                          start=True, stop=True)
                                t1 = self.inc(TE.last, "sT")
                                self.w(V, "sT", t1)
                                V.reciprocal(rden[:], ps_d[0:8, 0:512])
                                V.drain()
                                rv = self.inc(V.last, "sV")
                                self.w(TE, "sV", rv)
                                TE.matmul(ps_d[:, 0:512], self.bd8t[:], rden[:],
                                          start=True, stop=True)
                                t2 = self.inc(TE.last, "sT")
                                self.w(V, "sT", t2)
                                V.tensor_tensor(kFd[:, o, :], kFd[:, o, :], ps_d[:, 0:512],
                                                OP.mult)
                                V.drain()
                                knorm_v = self.inc(V.last, "sV")
                                rot_p.release()
                            rot_w.release()
                        k_done_v = self.cnt["sV"]
                        if self.c_stop <= 3:
                            continue

                        for s_ in range(NS):
                            for o in range(NK):
                                trs = rot_tr.acquire(TE)
                                self.w(TE, "sV", k_done_v)
                                TE.transpose(ps_quad[:, trs, 0:128], kFd[:, o, s_*128:(s_+1)*128],
                                             self.identf[:])
                                tt_ = self.inc(TE.last, "sT")
                                self.w(V, "sT", tt_)
                                V.tensor_copy(ktok[:, s_, o*128:(o+1)*128], ps_quad[:, trs, 0:128])
                                self.inc(V.last, "sV")
                                rot_tr.release()
                        ktok_done = self.cnt["sV"]
                        if self.c_stop <= 4:
                            continue

                        self.w(V, "sV", bias_done)
                        V.memset(Vp[:].rearrange("p s (h e) -> p s h e", e=17)[:, :, :, 16:17],
                                 1.0)
                        V.drain()
                        ones_v = self.inc(V.last, "sV")
                        for half in range(2):
                            ws = rot_w.acquire(SY)
                            w_ld = self.dma(SY, wbufC[:, ws, :, :],
                                            self.d_wv[:, half*512:(half+1)*512].rearrange(
                                                "(k p) n -> p k n", p=128), "W%d" % ws)
                            for s_ in range(NS):
                                pps = rot_p.acquire(TE)
                                self.w(TE, *w_ld)
                                self.w(TE, "sV", xnf_done)
                                for k in range(NK):
                                    TE.matmul(ps_quad[:, 2 + pps, :],
                                              XnFc[:, k, s_*128:(s_+1)*128],
                                              wbufC[:, ws, k, :],
                                              start=(k == 0), stop=(k == NK - 1))
                                tt_ = self.inc(TE.last, "sT")
                                self.w(V, "sT", tt_)
                                self.w(V, "sV", ones_v)
                                vview = Vp[:, s_, :].rearrange("p (h e) -> p h e", e=17)
                                pview = ps_quad[:, 2 + pps, :].rearrange("p (h e) -> p h e", e=16)
                                bview = (bvb[:, half*512:(half+1)*512]
                                         .rearrange("p (h e) -> p h e", e=16))
                                V.tensor_tensor(vview[:, half*32:(half+1)*32, 0:16],
                                                pview, bview, OP.add)
                                V.drain()
                                self.inc(V.last, "sV")
                                rot_p.release()
                            rot_w.release()
                        vp_done = self.cnt["sV"]
                        if self.c_stop <= 5:
                            continue
                        prev_pt_v = vp_done

                        scan_sub_T = 0
                        for s_ in range(NS):
                            cg = c * NS + s_
                            self.w(S, "sS", q_done_s)
                            self.w(S, "sV", vp_done)
                            self.w(S, "sT", scan_sub_T)
                            self.dma(S, qS[0:112, :, :],
                                     qFd[16:128, :, s_*128:(s_+1)*128], "SH")
                            shift_ld = self.dma(S, kS[0:112, :, :],
                                                kFd[16:128, :, s_*128:(s_+1)*128], "SH")
                            if cg > 0:
                                self.w(V, "sT", st_upd_T)
                                for a4 in range(4):
                                    V.tensor_copy(
                                        stateSB[32*a4:32*a4+16, :].rearrange(
                                            "p (g e) -> p g e", e=68)[:, :, 17*a4:17*a4+17],
                                        ps_st[32*a4:32*a4+16, 0:272].rearrange(
                                            "p (g e) -> p g e", e=17))
                                V.drain()
                                st_ev = self.inc(V.last, "sV")
                            else:
                                st_ev = self.cnt["sV"]
                            if cg == 0:
                                self.w(TE, "sV", st_ev)
                                TE.matmul(ps_st[0:128, 0:512], self.identf[:],
                                          zero512[:], start=True, stop=True)
                                st_upd_T = self.inc(TE.last, "sT")
                            for o in range(NK):
                                for par in range(2):
                                    g = 2 * o + par
                                    if par == 0:
                                        qT = qFd[:, o, s_*128:(s_+1)*128]
                                        kT = kFd[:, o, s_*128:(s_+1)*128]
                                    else:
                                        qT = qS[:, o, :]
                                        kT = kS[:, o, :]
                                    self.w(TE, "sS", q_done_s)
                                    self.w(TE, "sV", k_done_v)
                                    self.w(TE, *shift_ld)
                                    self.w(TE, "sV", prev_pt_v)
                                    for a in range(4):
                                        bofs = 32 * a
                                        TE.matmul(ps_quad[:, a, 0:128],
                                                  kT[bofs:bofs+16, :],
                                                  qT[bofs:bofs+16, :],
                                                  start=True, stop=True,
                                                  tile_position=(bofs, 0),
                                                  skip_group_check=True)
                                    t_s = self.inc(TE.last, "sT")
                                    pts = rot_PT.acquire(V)
                                    self.w(V, "sT", t_s)
                                    V.tensor_tensor(
                                        PTb[:, pts, :].rearrange("p (a e) -> p a e", e=128),
                                        ps_quad[:, 0:4, 0:128],
                                        triu4[:].rearrange("p (a e) -> p a e", e=128),
                                        OP.mult)
                                    V.drain()
                                    pt_v = self.inc(V.last, "sV")
                                    prev_pt_v = pt_v
                                    Ns = rot_N.acquire(TE)
                                    self.w(TE, "sV", pt_v)
                                    self.w(TE, "sV", vp_done)
                                    self.w(TE, "sV", st_ev)
                                    first = True
                                    if cg > 0:
                                        TE.matmul(ps_N[:, Ns, 0:68], qT,
                                                  stateSB[:, g*68:(g+1)*68],
                                                  start=True, stop=False,
                                                  skip_group_check=True)
                                        first = False
                                    for a in range(4):
                                        h = 8 * o + par + 2 * a
                                        TE.matmul(ps_N[:, Ns, a*17:(a+1)*17],
                                                  PTb[:, pts, a*128:(a+1)*128],
                                                  Vp[:, s_, h*17:(h+1)*17],
                                                  start=first, stop=(a == 3),
                                                  skip_group_check=True)
                                        first = False
                                    t_n = self.inc(TE.last, "sT")
                                    rot_PT.release()
                                    self.w(TE, "sV", ktok_done)
                                    for a in range(4):
                                        h = 8 * o + par + 2 * a
                                        bofs = 32 * a
                                        TE.matmul(ps_st[bofs:bofs+16, g*17:(g+1)*17],
                                                  ktok[:, s_, h*16:(h+1)*16],
                                                  Vp[:, s_, h*17:(h+1)*17],
                                                  start=False, stop=False,
                                                  tile_position=(0, bofs),
                                                  skip_group_check=True)
                                    st_upd_T = self.inc(TE.last, "sT")
                                    self.w(V, "sT", t_n)
                                    if o == 0 and par == 0 and c > 0 and self.c_stop > 7:
                                        self.w(V, "SO%d" % s_, ot_store[s_])
                                    nview = ps_N[:, Ns, 0:68].rearrange("p (h e) -> p h e", e=17)
                                    V.reciprocal(rd4[:], nview[:, 0:4, 16])
                                    V.drain()
                                    h0 = 8 * o + par
                                    ov = Oc[:, s_, :].rearrange(
                                        "p (hh e) -> p hh e", e=16)[:, h0:h0+7:2, :]
                                    rb = rd4[:].unsqueeze(-1).broadcast_to((128, 4, 16))
                                    V.tensor_tensor(ov, nview[:, 0:4, 0:16], rb, OP.mult)
                                    V.drain()
                                    self.inc(V.last, "sV")
                                    rot_N.release()
                            scan_sub_T = self.cnt["sT"]
                        oc_done = self.cnt["sV"]
                        if self.c_stop <= 7:
                            chunk_T = self.cnt["sT"]
                            continue

                        if c == 0:
                            ot_store = {}
                        w_lds = []
                        wss = []
                        for half in range(2):
                            ws = rot_w.acquire(SY)
                            w_lds.append(self.dma(
                                SY, wbufC[:, ws, :, :],
                                self.d_wo[:, half*512:(half+1)*512].rearrange(
                                    "(k p) n -> p k n", p=128), "W%d" % ws))
                            wss.append(ws)
                        for s_ in range(NS):
                            t0 = c * NS + s_
                            for k in range(NK):
                                trs = rot_tr.acquire(TE)
                                self.w(TE, "sV", oc_done)
                                TE.transpose(ps_quad[:, trs, 0:128], Oc[:, s_, k*128:(k+1)*128],
                                             self.identf[:])
                                tt_ = self.inc(TE.last, "sT")
                                self.w(V, "sT", tt_)
                                V.tensor_copy(OF[:, k, :], ps_quad[:, trs, 0:128])
                                self.inc(V.last, "sV")
                                rot_tr.release()
                            of_done = self.cnt["sV"]
                            for half in range(2):
                                pps = rot_p.acquire(TE)
                                self.w(TE, *w_lds[half])
                                self.w(TE, "sV", of_done)
                                for k in range(NK):
                                    TE.matmul(ps_quad[:, 2 + pps, :], OF[:, k, :],
                                              wbufC[:, wss[half], k, :],
                                              start=(k == 0), stop=(k == NK - 1))
                                tt_ = self.inc(TE.last, "sT")
                                self.w(V, "sT", tt_)
                                self.w(V, "sV", bias_done)
                                V.tensor_tensor(Oc[:, s_, half*512:(half+1)*512],
                                                ps_quad[:, 2 + pps, :],
                                                bob[:, half*512:(half+1)*512], OP.add)
                                V.drain()
                                self.inc(V.last, "sV")
                                rot_p.release()
                            V.tensor_tensor(Oc[:, s_, :], Oc[:, s_, :],
                                            rtC[:, s_, :], OP.add)
                            V.drain()
                            ot_v = self.inc(V.last, "sV")
                            rot_rt.release(("sV", ot_v))
                            self.w(SY, "sV", ot_v)
                            _, ot_store[s_] = self.dma(
                                SY, self.d_r3[t0*128:(t0+1)*128, :],
                                Oc[:, s_, :], "SO%d" % s_)
                        for half in range(2):
                            rot_w.release()
                        chunk_T = self.cnt["sT"]
                    for i4 in range(4):
                        self.w(SY, "SO%d" % i4, self.cnt["SO%d" % i4])
                blk.vector(body)


    def copy_dram(self, src, dst, tagn):
        nc = self.nc
        with contextlib.ExitStack() as ph:
            buf = ph.enter_context(nc.sbuf_tensor("cp%s" % tagn, [128, 2, D], F32))
            with nc.Block() as blk:
                def body(_):
                    so = {0: 0, 1: 0}
                    for t in range(NT):
                        sl = t % 2
                        self.w(self.SY, "SO%d" % sl, so[sl])
                        lw = self.dma(self.SY, buf[:, sl, :], src[t*128:(t+1)*128, :],
                                      "X%d" % sl)
                        self.w(self.S, *lw)
                        _, so[sl] = self.dma(self.S, dst[t*128:(t+1)*128, :], buf[:, sl, :],
                                             "SO%d" % sl)
                    self.w(self.S, "SO0", self.cnt["SO0"])
                    self.w(self.S, "SO1", self.cnt["SO1"])
                blk.vector(body)

    def build(self):
        self.setup_consts()
        if "A" in self.phases:
            self.phase_A()
        else:
            self.copy_dram(self.d_x, self.d_r1, "A")
        if "B" in self.phases:
            self.phase_B(self.d_r1, self.d_r2, first=True, tag="B")
        else:
            self.copy_dram(self.d_r1, self.d_r2, "B")
        if "C" in self.phases:
            self.phase_C()
        else:
            self.copy_dram(self.d_r2, self.d_r3, "C")
        if "D" in self.phases:
            self.phase_B(self.d_r3, self.d_y, first=("B" not in self.phases), tag="D")
        else:
            self.copy_dram(self.d_r3, self.d_y, "D")
        self.st.close()
        return self.nc


def host_consts():
    import ml_dtypes
    eye = np.eye(128, dtype=np.float32)
    eyeb = np.eye(128, dtype=ml_dtypes.bfloat16)
    iota128 = np.broadcast_to(np.arange(T, dtype=np.float32)[None, :], (128, T)).copy()
    iotacol = np.arange(128, dtype=np.float32)[:, None].copy()
    q = np.arange(128)[:, None]
    kk = np.arange(128)[None, :]
    amask = np.where(kk <= q, 0.0, NEG).astype(np.float32)
    tri = (np.arange(128)[:, None] <= np.arange(128)[None, :]).astype(np.float32)
    triu4 = np.concatenate([tri] * 4, axis=1).astype(np.float32)
    bd8 = np.zeros((128, 8), dtype=np.float32)
    for h in range(8):
        bd8[h*16:(h+1)*16, h] = 1.0
    bd8t = np.ascontiguousarray(bd8.T)
    ones1 = np.ones((1, 128), dtype=np.float32)
    return dict(identf=eye, identb=eyeb, iota128=iota128, iotacol=iotacol,
                amask=amask, triu4=triu4, bd8=bd8, bd8t=bd8t, ones1=ones1)


def stage_inputs(inputs, core):
    c = host_consts()
    f32 = np.float32
    m = dict(c)
    m["x"] = np.ascontiguousarray(np.asarray(inputs["L0_attn_a_x"], dtype=f32)[core])
    m["wqk"] = np.ascontiguousarray(np.asarray(inputs["L0_attn_a_W_qk"], dtype=f32))
    m["wva"] = np.ascontiguousarray(np.asarray(inputs["L0_attn_a_W_v"], dtype=f32).T)
    m["alibi"] = np.asarray(inputs["L0_attn_a_alibi_slope"], dtype=f32).reshape(1, 1).copy()
    m["wq"] = np.ascontiguousarray(np.asarray(inputs["L0_attn_b_Wq"], dtype=f32).T)
    m["wk"] = np.ascontiguousarray(np.asarray(inputs["L0_attn_b_Wk"], dtype=f32).T)
    m["wv"] = np.ascontiguousarray(np.asarray(inputs["L0_attn_b_Wv"], dtype=f32).T)
    m["wo"] = np.ascontiguousarray(np.asarray(inputs["L0_attn_b_Wo"], dtype=f32).T)
    m["bqF"] = np.ascontiguousarray(np.asarray(inputs["L0_attn_b_bq"], dtype=f32).reshape(8, 128).T)
    m["bkF"] = np.ascontiguousarray(np.asarray(inputs["L0_attn_b_bk"], dtype=f32).reshape(8, 128).T)
    m["bvrow"] = np.asarray(inputs["L0_attn_b_bv"], dtype=f32).reshape(1, D).copy()
    m["borow"] = np.asarray(inputs["L0_attn_b_bo"], dtype=f32).reshape(1, D).copy()
    m["beta"] = np.asarray(inputs["L0_attn_b_beta"], dtype=f32).reshape(1, 1).copy()
    m["w1t"] = np.ascontiguousarray(np.asarray(inputs["L0_ffn_w1"], dtype=f32).T)
    m["w2t"] = np.ascontiguousarray(np.asarray(inputs["L0_ffn_w2"], dtype=f32).T)
    return m


def kernel(**inputs):
    import numpy as np
    from concourse.bass_utils import run_bass_kernel_spmd
    n = 8
    nc = MK(phases=("A", "B", "C", "D")).build()
    in_maps = [stage_inputs(inputs, c) for c in range(n)]
    res = run_bass_kernel_spmd(nc, in_maps, core_ids=list(range(n)))
    return np.stack([np.asarray(res.results[c]["y"]) for c in range(n)], axis=0)


# version 19
# speedup vs baseline: 1.0767x; 1.0297x over previous
import contextlib
import numpy as np
import concourse.bass as bass
import concourse.mybir as mybir

F32 = mybir.dt.float32
BF16 = mybir.dt.bfloat16
I32 = mybir.dt.int32
AF = mybir.ActivationFunctionType
OP = mybir.AluOpType
AX = mybir.AxisListType

T = 2048
D = 1024
HID = 4096
NT = T // 128
NK = D // 128
SCALE_A = 1.0 / 32.0
NEG = -1e30


class Rot:

    def __init__(self, mk, sem, n):
        self.mk, self.sem, self.n = mk, sem, n
        self.rel = []
        self.i = 0

    def acquire(self, eng):
        if self.i >= self.n:
            ev = self.rel[self.i - self.n]
            self.mk.w(eng, ev[0], ev[1])
        s = self.i % self.n
        self.i += 1
        return s

    def release(self, ev=None):
        if ev is None:
            ev = (self.sem, self.mk.cnt[self.sem])
        self.rel.append(ev)


class EngW:
    _PASS = {"wait_ge", "drain", "nop", "sem_inc", "sem_clear"}

    def __init__(self, eng):
        object.__setattr__(self, "_e", eng)
        object.__setattr__(self, "last", None)

    def __getattr__(self, n):
        f = getattr(self._e, n)
        if n in EngW._PASS or not callable(f):
            return f
        def call(*a, **k):
            r = f(*a, **k)
            object.__setattr__(self, "last", r)
            return r
        return call


class MK:
    def __init__(self, phases=("A", "B", "C", "D"), a_stop=99, c_stop=99):
        self.a_stop = a_stop
        self.c_stop = c_stop
        self.phases = phases
        nc = bass.Bass("TRN2", target_bir_lowering=False)
        self.nc = nc
        self.st = contextlib.ExitStack()
        self.cnt = {}
        self.sem = {}
        for name in ["dS", "dA", "dG", "sT", "sV", "sS",
                     "X0", "X1", "XR0", "XR1", "W0", "W1", "W2", "W3",
                     "SH", "CST", "SO0", "SO1", "SO2", "SO3"]:
            self.sem[name] = self.st.enter_context(nc.semaphore(name))
            self.cnt[name] = 0
        self.V, self.S, self.TE = EngW(nc.vector), EngW(nc.scalar), EngW(nc.tensor)
        self.SY, self.G = EngW(nc.sync), EngW(nc.gpsimd)
        self._declare_io()

    def inc(self, res, name, n=1):
        res.then_inc(self.sem[name], n)
        self.cnt[name] += n
        return self.cnt[name]

    def bump(self, name, n):
        self.cnt[name] += n
        return self.cnt[name]

    def w(self, eng, name, v):
        if v > 0:
            eng.wait_ge(self.sem[name], v)

    def dma(self, eng, out, in_, slot):
        r = eng.dma_start(out=out, in_=in_)
        r.then_inc(self.sem[slot], 16)
        self.cnt[slot] += 16
        return (slot, self.cnt[slot])

    def _declare_io(self):
        nc = self.nc
        dt_ = lambda n, shp, dt=F32, kind="ExternalInput": nc.dram_tensor(n, shp, dt, kind=kind).ap()
        self.d_x = dt_("x", [T, D])
        self.d_y = dt_("y", [T, D], kind="ExternalOutput")
        self.d_wqk = dt_("wqk", [D, D])
        self.d_wva = dt_("wva", [D, D])
        self.d_alibi = dt_("alibi", [1, 1])
        self.d_wq = dt_("wq", [D, D])
        self.d_wk = dt_("wk", [D, D])
        self.d_wv = dt_("wv", [D, D])
        self.d_wo = dt_("wo", [D, D])
        self.d_bqF = dt_("bqF", [128, 8])
        self.d_bkF = dt_("bkF", [128, 8])
        self.d_bvrow = dt_("bvrow", [1, D])
        self.d_borow = dt_("borow", [1, D])
        self.d_beta = dt_("beta", [1, 1])
        self.d_w1t = dt_("w1t", [D, HID])
        self.d_w2t = dt_("w2t", [HID, D])
        self.d_identf = dt_("identf", [128, 128])
        self.d_identb = dt_("identb", [128, 128], BF16)
        self.d_iota128 = dt_("iota128", [128, T])
        self.d_iotacol = dt_("iotacol", [128, 1])
        self.d_amask = dt_("amask", [128, 128])
        self.d_triu4 = dt_("triu4", [128, 512])
        self.d_bd8 = dt_("bd8", [128, 8])
        self.d_bd8t = dt_("bd8t", [8, 128])
        self.d_ones1 = dt_("ones1", [1, 128])
        self.d_r1 = dt_("r1", [T, D], kind="Internal")
        self.d_r2 = dt_("r2", [T, D], kind="Internal")
        self.d_r3 = dt_("r3", [T, D], kind="Internal")
        self.d_w1q = dt_("w1q", [D, HID], BF16, kind="Internal")
        self.d_w2q = dt_("w2q", [HID, D], BF16, kind="Internal")
        self.d_s12 = dt_("s12", [1, 2], kind="Internal")
        self.dbg_xq = dt_("dbg_xq", [T, D], BF16, kind="Internal")
        self.dbg_h = dt_("dbg_h", [T, HID], kind="Internal")
        self.dbg_hq = dt_("dbg_hq", [T, HID], BF16, kind="Internal")
        self.dbg_am = dt_("dbg_am", [T, 4], kind="Internal")
        self.dbg_yq = dt_("dbg_yq", [T, D], kind="Internal")
        self.dbg_i1 = dt_("dbg_i1", [T, D], I32, kind="Internal")
        self.dbg_dl = dt_("dbg_dl", [T, D], kind="Internal")

    def sb(self, name, shape, dt=F32):
        return self.st.enter_context(self.nc.sbuf_tensor(name, shape, dt))

    def setup_consts(self):
        sb = self.sb
        self.identf = sb("c_identf", [128, 128])
        self.identb = sb("c_identb", [128, 128], BF16)
        self.iotacol = sb("c_iotacol", [128, 1])
        self.bd8 = sb("c_bd8", [128, 8])
        self.bd8t = sb("c_bd8t", [8, 128])
        self.ones1 = sb("c_ones1", [1, 128])
        self.slope_b = sb("c_slope_b", [128, 1])
        self.be_b = sb("c_be_b", [128, 1])
        self.bebq = sb("c_bebq", [128, 8])
        self.bebk = sb("c_bebk", [128, 8])
        self.alibi_sb = sb("c_alibi", [1, 1])
        self.beta_sb = sb("c_beta", [1, 1])
        self.bqF_sb = sb("c_bqF", [128, 8])
        self.bkF_sb = sb("c_bkF", [128, 8])
        self.s1b = sb("c_s1b", [128, 1])
        self.s2b = sb("c_s2b", [128, 1])
        self.is1b = sb("c_is1b", [128, 1])
        self.is2b = sb("c_is2b", [128, 1])
        self.epsb = sb("c_epsb", [128, 1])
        self.s12row = sb("c_s12row", [1, 2])

        with contextlib.ExitStack() as ph:
            ps_bc = ph.enter_context(self.nc.psum_tensor("c_ps", [128, 2], F32))
            with self.nc.Block() as blk:
                def body(_):
                    V, S, TE, SY = self.V, self.S, self.TE, self.SY
                    V.memset(self.epsb[:], 1e-5)
                    V.drain()
                    for dst, src in [(self.identf, self.d_identf), (self.identb, self.d_identb),
                                     (self.iotacol, self.d_iotacol), (self.bd8, self.d_bd8),
                                     (self.bd8t, self.d_bd8t), (self.ones1, self.d_ones1),
                                     (self.alibi_sb, self.d_alibi), (self.beta_sb, self.d_beta),
                                     (self.bqF_sb, self.d_bqF), (self.bkF_sb, self.d_bkF)]:
                        SY.dma_start(out=dst[:], in_=src[:]).then_inc(self.sem["dS"], 16)
                        self.bump("dS", 16)
                    t0 = self.cnt["dS"]
                    self.w(S, "dS", t0)
                    S.activation(self.be_b[0:1, :], self.beta_sb[:], AF.Exp)
                    s0 = self.inc(S.last, "sS")
                    self.w(TE, "dS", t0)
                    self.w(TE, "sS", s0)
                    TE.matmul(ps_bc[:, 0:1], self.ones1[:], self.alibi_sb[:], start=True, stop=True)
                    TE.matmul(ps_bc[:, 1:2], self.ones1[:], self.be_b[0:1, :], start=True, stop=True)
                    tt = self.inc(TE.last, "sT")
                    self.w(V, "sT", tt)
                    V.tensor_copy(self.slope_b[:], ps_bc[:, 0:1])
                    V.tensor_copy(self.be_b[:], ps_bc[:, 1:2])
                    V.drain()
                    V.tensor_scalar(self.bebq[:], self.bqF_sb[:], self.be_b[:], None, OP.mult)
                    V.tensor_scalar(self.bebk[:], self.bkF_sb[:], self.be_b[:], None, OP.mult)
                    self.inc(V.last, "sV")
                blk.vector(body)

    def emit_ln(self, xt, sc, quant=False, pre_wait=(), s_pre_wait=()):
        V, S = self.V, self.S
        for s_, v_ in pre_wait:
            self.w(V, s_, v_)
        V.reduce_sum(sc["s1"], xt, axis=AX.X)
        V.drain()
        V.tensor_scalar(sc["negm"], sc["s1"], -1.0 / D, None, OP.mult)
        V.drain()
        v1 = self.inc(V.last, "sV")
        for s_, v_ in s_pre_wait:
            self.w(S, s_, v_)
        self.w(S, "sV", v1)
        S.activation(sc["xc"], xt, AF.Identity, bias=sc["negm"], scale=1.0)
        S.drain()
        S.activation(sc["sqd"], sc["xc"], AF.Square, accum_out=sc["ssq"])
        S.drain()
        S.activation(sc["sdt"], sc["ssq"], AF.Sqrt, bias=self.epsb[:], scale=1.0 / D)
        S.drain()
        s1 = self.inc(S.last, "sS")
        self.w(V, "sS", s1)
        V.reciprocal(sc["rstd"], sc["sdt"])
        V.drain()
        if quant:
            V.reduce_max(sc["axc"], sc["xc"], axis=AX.X, apply_absolute_value=True)
            V.drain()
            V.tensor_scalar(sc["mxn"], sc["axc"], sc["rstd"], None, OP.mult)
            V.drain()
            V.tensor_scalar(sc["mxn"], sc["mxn"], 1e-5, None, OP.max)
            V.drain()
            V.reciprocal(sc["iam"], sc["mxn"])
            V.drain()
            V.tensor_scalar(sc["scq"], sc["iam"], 127.0, None, OP.mult)
            V.drain()
            V.tensor_scalar(sc["comb"], sc["scq"], sc["rstd"], None, OP.mult)
            V.drain()
        v2 = self.inc(V.last, "sV")
        return v2, s1

    def ln_scratch(self, ph, pfx, quant=False, sqd_ap=None, xc_ap=None):
        sbl = lambda n, shp, dt=F32: ph.enter_context(self.nc.sbuf_tensor(n, shp, dt))[:]
        sc = {"s1": sbl(pfx + "_s1", [128, 1]), "negm": sbl(pfx + "_negm", [128, 1]),
              "ssq": sbl(pfx + "_ssq", [128, 1]), "sdt": sbl(pfx + "_sdt", [128, 1]),
              "rstd": sbl(pfx + "_rstd", [128, 1])}
        sc["xc"] = xc_ap if xc_ap is not None else sbl(pfx + "_xc", [128, D])
        sc["sqd"] = sqd_ap if sqd_ap is not None else sbl(pfx + "_sqd", [128, D])
        if quant:
            for k in ["axc", "mxn", "iam", "scq", "comb"]:
                sc[k] = sbl(pfx + "_" + k, [128, 1])
        return sc

    def phase_A(self):
        nc = self.nc
        V, S, TE, SY, G = self.V, self.S, self.TE, self.SY, self.G
        with contextlib.ExitStack() as ph:
            sbl = lambda n, shp, dt=F32: ph.enter_context(nc.sbuf_tensor(n, shp, dt))
            psl = lambda n, shp, dt=F32: ph.enter_context(nc.psum_tensor(n, shp, dt))
            XnF = sbl("A_XnF", [128, NK, T])
            Vtm = sbl("A_Vtm", [128, NT, D])
            wbuf = sbl("A_w", [128, NK, 512])
            qF = sbl("A_qF", [128, NK, 512])
            Ssb = sbl("A_S", [128, T])
            Ssb2 = sbl("A_S2", [128, T])
            scr = sbl("A_scr", [128, T])
            PT = sbl("A_PT", [128, 2, 128])
            xt = sbl("A_xt", [128, 2, D])
            B0 = sbl("A_B0", [128, T])
            amask = sbl("A_amask", [128, 128])
            lnsc = self.ln_scratch(ph, "A_ln", sqd_ap=scr[:, 0:D], xc_ap=Ssb[:, 0:D])
            xn = scr[:, D:2*D]
            mx = sbl("A_mx", [128, 1])
            nmx = sbl("A_nmx", [128, 1])
            sume = sbl("A_sume", [128, 1])
            rs = sbl("A_rs", [128, 1])
            qbv = sbl("A_qbv", [128, 1])
            ps_tr = psl("A_ps_tr", [128, 2, 512])
            ps_v = psl("A_ps_v", [128, 2, 512])
            ps_s = psl("A_ps_s", [128, 2, 512])
            ps_o = psl("A_ps_o", [128, 2, 512])
            wflat = wbuf[:].rearrange("p k n -> p (k n)")

            with nc.Block() as blk:
                def body(_):
                    rot_xt = Rot(self, "sS", 2)
                    rot_tr = Rot(self, "sV", 2)
                    rot_pv = Rot(self, "sV", 2)
                    rot_ps = Rot(self, "sV", 2)
                    so_tgt = {0: 0, 1: 0}
                    rot_wo = Rot(self, "sT", 4)
                    ot_i = [0]

                    self.dma(SY, amask[:], self.d_amask[:], "CST")
                    cst_t = self.dma(SY, B0[:], self.d_iota128[:], "CST")
                    last_s_done = 0
                    xn_free = 0
                    for t in range(NT):
                        sl = rot_xt.acquire(SY)
                        xw = self.dma(SY, xt[:, sl, :], self.d_x[t*128:(t+1)*128, :],
                                      "X%d" % sl)
                        if t == 0:
                            self.w(V, *cst_t)
                            V.tensor_scalar(B0[:], B0[:], self.slope_b[:], None, OP.mult)
                            V.drain()
                            self.inc(V.last, "sV")
                        v_done, s_done = self.emit_ln(
                            xt[:, sl, :], lnsc,
                            pre_wait=[xw, ("sS", last_s_done)])
                        self.w(S, "sV", v_done)
                        self.w(S, "sT", xn_free)
                        S.activation(xn, lnsc["xc"], AF.Identity, scale=lnsc["rstd"])
                        S.drain()
                        s_xn = self.inc(S.last, "sS")
                        last_s_done = s_xn
                        rot_xt.release()
                        for k in range(NK):
                            trs = rot_tr.acquire(TE)
                            self.w(TE, "sS", s_xn)
                            TE.transpose(ps_tr[:, trs, 0:128], xn[:, k*128:(k+1)*128], self.identf[:])
                            tt_ = self.inc(TE.last, "sT")
                            self.w(V, "sT", tt_)
                            V.tensor_copy(XnF[:, k, t*128:(t+1)*128], ps_tr[:, trs, 0:128])
                            self.inc(V.last, "sV")
                            rot_tr.release()
                        xn_free = self.cnt["sT"]
                    xnf_all = self.cnt["sV"]
                    if self.a_stop <= 1:
                        self.w(G, "sV", self.cnt["sV"])
                        return

                    for half in range(2):
                        self.w(SY, "sT", self.cnt["sT"])
                        ww = self.dma(SY, wbuf[:],
                                      self.d_wva[:, half*512:(half+1)*512].rearrange(
                                          "(k p) n -> p k n", p=128), "W0")
                        for t in range(NT):
                            pvs = rot_pv.acquire(TE)
                            self.w(TE, *ww)
                            self.w(TE, "sV", xnf_all)
                            for k in range(NK):
                                TE.matmul(ps_v[:, pvs, :], XnF[:, k, t*128:(t+1)*128],
                                          wbuf[:, k, :], start=(k == 0), stop=(k == NK - 1))
                            tt_ = self.inc(TE.last, "sT")
                            self.w(V, "sT", tt_)
                            V.tensor_copy(Vtm[:, t, half*512:(half+1)*512], ps_v[:, pvs, :])
                            self.inc(V.last, "sV")
                            rot_pv.release()
                    a1b_T = self.cnt["sT"]
                    if self.a_stop <= 2:
                        self.w(G, "sV", self.cnt["sV"])
                        return

                    ssb_free = {0: 0, 1: 0}
                    prev_sm_s = 0
                    prev_ot_v = 0
                    ot2 = [scr[:, 0:D], scr[:, D:2*D]]
                    self.w(SY, "sT", a1b_T)
                    for qc in range(4):
                        for o in range(NK):
                            wos = rot_wo.acquire(SY)
                            ow = self.dma(SY, wflat[:, wos*1024:(wos+1)*1024].rearrange(
                                              "p (k n) -> p k n", n=128),
                                          self.d_wqk[:, o*128:(o+1)*128].rearrange(
                                              "(k p) n -> p k n", p=128), "W%d" % wos)
                            pvs = rot_pv.acquire(TE)
                            self.w(TE, *ow)
                            for k in range(NK):
                                TE.matmul(ps_v[:, pvs, :],
                                          wflat[:, wos*1024 + k*128: wos*1024 + (k+1)*128],
                                          XnF[:, k, qc*512:(qc+1)*512],
                                          start=(k == 0), stop=(k == NK - 1))
                            tt_ = self.inc(TE.last, "sT")
                            rot_wo.release()
                            self.w(V, "sT", tt_)
                            V.tensor_copy(qF[:, o, :], ps_v[:, pvs, :])
                            self.inc(V.last, "sV")
                            rot_pv.release()
                        qf_done = self.cnt["sV"]
                        if self.a_stop <= 3:
                            continue

                        for qt in range(4):
                            q0 = qc * 512 + qt * 128
                            qi = q0 // 128
                            Sbuf = [Ssb, Ssb2][qi % 2]
                            xrs = rot_xt.acquire(S)
                            rw = self.dma(S, xt[:, xrs, :], self.d_x[q0:q0+128, :],
                                          "XR%d" % xrs)
                            V.tensor_scalar(qbv[:], self.iotacol[:], float(q0),
                                            self.slope_b[:], OP.add, OP.mult)
                            V.drain()
                            qb_v = self.inc(V.last, "sV")
                            kmax = q0 + 128
                            nkb = (kmax + 511) // 512
                            for kb in range(nkb):
                                kw = min(512, kmax - kb * 512)
                                pss = rot_ps.acquire(TE)
                                self.w(TE, "sV", qf_done)
                                for dch in range(NK):
                                    TE.matmul(ps_s[:, pss, 0:kw],
                                              qF[:, dch, qt*128:(qt+1)*128],
                                              XnF[:, dch, kb*512:kb*512+kw],
                                              start=(dch == 0), stop=(dch == NK - 1))
                                tt_ = self.inc(TE.last, "sT")
                                self.w(V, "sT", tt_)
                                self.w(V, "sV", qb_v)
                                self.w(V, "sT", ssb_free[qi % 2])
                                V.tensor_scalar(Sbuf[:, kb*512:kb*512+kw], ps_s[:, pss, 0:kw],
                                                SCALE_A, qbv[:], OP.mult, OP.subtract)
                                V.drain()
                                V.tensor_tensor(Sbuf[:, kb*512:kb*512+kw],
                                                Sbuf[:, kb*512:kb*512+kw],
                                                B0[:, kb*512:kb*512+kw], OP.add)
                                V.drain()
                                self.inc(V.last, "sV")
                                rot_ps.release()
                            V.tensor_tensor(Sbuf[:, q0:q0+128], Sbuf[:, q0:q0+128],
                                            amask[:], OP.add)
                            V.drain()
                            V.reduce_max(mx[:], Sbuf[:, 0:kmax], axis=AX.X)
                            V.drain()
                            self.w(V, "sS", prev_sm_s)
                            V.tensor_scalar(nmx[:], mx[:], -1.0, None, OP.mult)
                            V.drain()
                            sm_v = self.inc(V.last, "sV")
                            self.w(S, "sV", sm_v)
                            S.activation(Sbuf[:, 0:kmax], Sbuf[:, 0:kmax], AF.Exp,
                                         bias=nmx[:], scale=1.0, accum_out=sume[:])
                            S.drain()
                            sm_s = self.inc(S.last, "sS")
                            prev_sm_s = sm_s
                            self.w(V, "sS", sm_s)
                            V.reciprocal(rs[:], sume[:])
                            V.drain()
                            rs_v = self.inc(V.last, "sV")
                            if self.a_stop <= 4:
                                self.w(V, "sV", rs_v)
                                V.tensor_copy(lnsc["negm"], rs[:])
                                self.inc(V.last, "sV")
                                rot_xt.release()
                                continue
                            if self.a_stop == 5:
                                ssb_free_T = self.cnt["sT"]
                            nb = q0 // 128 + 1
                            for kb2 in range(nb):
                                trs = rot_tr.acquire(TE)
                                self.w(TE, "sS", sm_s)
                                if kb2 == 0:
                                    self.w(TE, "sV", prev_ot_v)
                                TE.transpose(ps_tr[:, trs, 0:128], Sbuf[:, kb2*128:(kb2+1)*128],
                                             self.identf[:])
                                tt_ = self.inc(TE.last, "sT")
                                self.w(V, "sT", tt_)
                                V.tensor_copy(PT[:, trs, :], ps_tr[:, trs, 0:128])
                                ev = self.inc(V.last, "sV")
                                rot_tr.release()
                                if self.a_stop <= 5:
                                    if kb2 == nb - 1:
                                        pass
                                    continue
                                self.w(TE, "sV", ev)
                                for half in range(2):
                                    TE.matmul(ps_o[:, half, :], PT[:, trs, :],
                                              Vtm[:, kb2, half*512:(half+1)*512],
                                              start=(kb2 == 0), stop=(kb2 == nb - 1))
                            if self.a_stop <= 5:
                                ssb_free_T = self.cnt["sT"]
                                self.w(V, "sV", self.cnt["sV"])
                                V.tensor_copy(lnsc["negm"], rs[:])
                                self.inc(V.last, "sV")
                                rot_xt.release()
                                continue
                            ssb_free_T = self.inc(TE.last, "sT")
                            if self.a_stop <= 6:
                                self.w(V, "sT", ssb_free_T)
                                V.tensor_copy(lnsc["negm"], rs[:])
                                self.inc(V.last, "sV")
                                rot_xt.release()
                                continue
                            ots = ot_i[0] % 2
                            ot_i[0] += 1
                            self.w(V, "SO%d" % ots, so_tgt[ots])
                            self.w(V, "sT", ssb_free_T)
                            self.w(V, "sV", rs_v)
                            self.w(V, *rw)
                            for half in range(2):
                                V.tensor_scalar(ot2[ots][:, half*512:(half+1)*512],
                                                ps_o[:, half, :], rs[:], None, OP.mult)
                            V.drain()
                            V.tensor_tensor(ot2[ots], ot2[ots], xt[:, xrs, :], OP.add)
                            V.drain()
                            prev_ot_v = self.inc(V.last, "sV")
                            rot_xt.release(("sV", prev_ot_v))
                            if self.a_stop <= 7:
                                continue
                            self.w(SY, "sV", prev_ot_v)
                            _, so_tgt[ots] = self.dma(SY, self.d_r1[q0:q0+128, :], ot2[ots],
                                                      "SO%d" % ots)
                    self.w(G, "SO0", self.cnt["SO0"])
                    self.w(G, "SO1", self.cnt["SO1"])
                blk.vector(body)

    def phase_B(self, src, dst, first, tag):
        nc = self.nc
        V, S, TE, SY, G = self.V, self.S, self.TE, self.SY, self.G
        with contextlib.ExitStack() as ph:
            sbl = lambda n, shp, dt=F32: ph.enter_context(nc.sbuf_tensor(tag + n, shp, dt))
            psl = lambda n, shp, dt=F32: ph.enter_context(nc.psum_tensor(tag + n, shp, dt))
            W1 = sbl("_W1", [128, NK, HID], BF16)
            W2 = sbl("_W2", [128, HID // 128, D], BF16)
            if first:
                self.cache_stores = []
                self._ternarize(W1, W2, tag)
            else:
                with nc.Block() as blk0:
                    def body0(_):
                        self.w(SY, "dG", self.cnt["dG"])
                        self.dma(SY, W1[:],
                                 self.d_w1q[:].rearrange("(k p) n -> p k n", p=128), "CST")
                        lw = self.dma(SY, W2[:],
                                      self.d_w2q[:].rearrange("(k p) n -> p k n", p=128),
                                      "CST")
                        self.w(SY, *lw)
                    blk0.vector(body0)

            rt = sbl("_rt", [128, 4, D])
            lnsc = self.ln_scratch(ph, tag + "_ln", quant=True)
            qs = sbl("_qs", [128, 1024])
            qi = psl("_qi", [128, 1024], I32)
            qd = sbl("_qd", [128, 1024])
            qb = sbl("_qb", [128, 1024])
            xqm = sbl("_xqm", [128, D], BF16)
            XqF = sbl("_XqF", [128, NK, 128], BF16)
            hbuf = sbl("_h", [128, HID])
            am2 = sbl("_am2", [128, 1])
            am2c = sbl("_am2c", [128, 1])
            iam2 = sbl("_iam2", [128, 1])
            scq2 = sbl("_scq2", [128, 1])
            hqm = sbl("_hqm", [128, HID], BF16)
            HqF = sbl("_HqF", [128, HID // 128, 128], BF16)
            dq1 = sbl("_dq1", [128, 2])
            dq2 = sbl("_dq2", [128, 2])
            ot = sbl("_ot", [128, 1, D])
            ps_h = psl("_ps_h", [128, 2, 512])
            ps_o = psl("_ps_o", [128, 2, 512])
            ps_tb = psl("_ps_tb", [128, 2, 1024], BF16)

            def round_slices(src_ap, dst_ap, width, v_wait, s_wait):
                for j in range(width // 512):
                    ssl = src_ap[:, j*512:(j+1)*512]
                    self.w(S, "sV", s_wait)
                    S.copy(qi[:], ssl)
                    sq = self.inc(S.last, "sS")
                    self.w(V, "sS", sq)
                    V.tensor_tensor(qd[:], ssl, qi[:], OP.subtract)
                    V.drain()
                    V.scalar_tensor_tensor(qb[:], qd[:], 0.5, qi[:], OP.is_ge, OP.add)
                    V.drain()
                    V.scalar_tensor_tensor(dst_ap[:, j*512:(j+1)*512], qd[:], -0.5,
                                           qb[:], OP.is_lt, OP.subtract)
                    V.drain()
                    s_wait = self.inc(V.last, "sV")
                return self.cnt["sV"]

            levels = 1 if first else 3
            levels2 = 1
            with nc.Block() as blk:
                def body(_):
                    rot_rt = Rot(self, "sV", 4)
                    so_tgt = {0: 0, 1: 0}
                    ot_i = [0]
                    rot_h = Rot(self, "sS", 2)
                    rot_ov = Rot(self, "sV", 2)
                    rot_tb = Rot(self, "sV", 2)
                    st = {"pipe_s": 0, "pipe_v": 0, "xq_tr_T": 0, "hq_tr_T": 0,
                          "q2_free_v": 0, "v_q2": 0}
                    pend = []
                    rt_rel_idx = {}
                    s1 = {}

                    def stage1a(t):
                        sl = rot_rt.acquire(SY)
                        xw = self.dma(SY, rt[:, sl, :], src[t*128:(t+1)*128, :],
                                      ["X0", "X1", "XR0", "XR1"][sl])
                        self.emit_ln(
                            rt[:, sl, :], lnsc, quant=True,
                            pre_wait=[xw],
                            s_pre_wait=[("sS", st["pipe_s"]), ("sV", st["pipe_v"])])
                        V.tensor_scalar(dq1[:, t % 2:t % 2 + 1], lnsc["mxn"][:],
                                        1.0 / 127.0, self.s1b[:], OP.mult, OP.mult)
                        V.drain()
                        v_dq1 = self.inc(V.last, "sV")
                        self.w(S, "sV", v_dq1)
                        self.w(S, "sV", st["q2_free_v"])
                        S.activation(qs[:], lnsc["xc"], AF.Identity, scale=lnsc["comb"])
                        S.drain()
                        S.copy(qi[:], qs[:])
                        sq = self.inc(S.last, "sS")
                        self.w(V, "sS", sq)
                        V.tensor_tensor(qd[:], qs[:], qi[:], OP.subtract)
                        V.drain()
                        V.scalar_tensor_tensor(qb[:], qd[:], 0.5, qi[:], OP.is_ge, OP.add)
                        V.drain()
                        st["pipe_v"] = v_dq1
                        st["pipe_s"] = sq
                        s1[t] = {"sl": sl, "v_dq1": v_dq1}

                    def stage1f(t):
                        self.w(V, "sT", st["xq_tr_T"])
                        if levels == 1:
                            V.scalar_tensor_tensor(xqm[:], qd[:], -0.5,
                                                   qb[:], OP.is_lt, OP.subtract)
                            V.drain()
                        else:
                            V.scalar_tensor_tensor(qs[:], qd[:], 1.5, qb[:], OP.is_ge, OP.add)
                            V.drain()
                            V.scalar_tensor_tensor(qb[:], qd[:], 2.5, qs[:], OP.is_ge, OP.add)
                            V.drain()
                            V.scalar_tensor_tensor(qs[:], qd[:], -0.5, qb[:], OP.is_lt, OP.subtract)
                            V.drain()
                            V.scalar_tensor_tensor(qb[:], qd[:], -1.5, qs[:], OP.is_lt, OP.add)
                            V.drain()
                            V.scalar_tensor_tensor(xqm[:], qd[:], -2.5,
                                                   qb[:], OP.is_lt, OP.add)
                            V.drain()
                        s1[t]["vq"] = self.inc(V.last, "sV")
                        rot_rt.release()
                        rt_rel_idx[t] = len(rot_rt.rel) - 1

                    def xq_transposes(t):
                        for k in range(NK):
                            tbs = rot_tb.acquire(TE)
                            self.w(TE, "sV", s1[t]["vq"])
                            TE.transpose(ps_tb[:, tbs, 0:128], xqm[:, k*128:(k+1)*128],
                                         self.identb[:])
                            tt_ = self.inc(TE.last, "sT")
                            self.w(S, "sT", tt_)
                            S.copy(XqF[:, k, :], ps_tb[:, tbs, 0:128])
                            S.drain()
                            self.inc(S.last, "sS")
                            rot_tb.release(("sS", self.cnt["sS"]))
                        st["xq_tr_T"] = self.cnt["sT"]
                        s1[t]["xqf_done"] = ("sS", self.cnt["sS"])

                    def ffn1_gelu(t, q2=None):
                        for b in range(8):
                            if q2 is not None:
                                q2(b)
                            hs = rot_h.acquire(TE)
                            self.w(TE, *s1[t]["xqf_done"])
                            for k in range(NK):
                                TE.matmul(ps_h[:, hs, :], XqF[:, k, :],
                                          W1[:, k, b*512:(b+1)*512],
                                          start=(k == 0), stop=(k == NK - 1))
                            tt_ = self.inc(TE.last, "sT")
                            self.w(S, "sT", tt_)
                            self.w(S, "sV", s1[t]["v_dq1"])
                            if b == 0:
                                self.w(S, "sV", st["v_q2"])
                            S.activation(hbuf[:, b*512:(b+1)*512], ps_h[:, hs, :],
                                         AF.Gelu, scale=dq1[:, t % 2:t % 2 + 1])
                            self.inc(S.last, "sS")
                            rot_h.release()
                        s1[t]["gelu_done"] = self.cnt["sS"]

                    stage1a(0)
                    stage1f(0)
                    xq_transposes(0)
                    stage1a(1)
                    stage1f(1)
                    ffn1_gelu(0)
                    for t in range(NT):
                        sl = s1[t]["sl"]
                        if pend:
                            pend.pop(0)()
                        self.w(V, "sS", s1[t]["gelu_done"])
                        V.reduce_max(am2[:], hbuf[:], axis=AX.X, apply_absolute_value=True)
                        V.drain()
                        V.tensor_scalar(am2c[:], am2[:], 1e-5, None, OP.max)
                        V.drain()
                        V.reciprocal(iam2[:], am2c[:])
                        V.drain()
                        V.tensor_scalar(scq2[:], iam2[:], 127.0, None, OP.mult)
                        V.drain()
                        V.tensor_scalar(dq2[:, (t % 2):(t % 2)+1], am2c[:], 1.0 / 127.0,
                                        self.s2b[:], OP.mult, OP.mult)
                        V.drain()
                        v_q2 = self.inc(V.last, "sV")
                        st["v_q2"] = v_q2
                        vq_next = s1[t + 1]["vq"] if t + 1 < NT else 0
                        svh = {0: max(v_q2, vq_next), 1: max(v_q2, vq_next)}

                        def q2slice(j2, t=t, svh=svh):
                            h = j2 % 2
                            hs_ = slice(h*512, (h+1)*512)
                            c0 = j2 * 512
                            self.w(S, "sV", svh[h])
                            S.activation(qs[:, hs_], hbuf[:, c0:c0+512],
                                         AF.Identity, scale=scq2[:])
                            S.drain()
                            S.copy(qi[:, hs_], qs[:, hs_])
                            sq = self.inc(S.last, "sS")
                            self.w(V, "sS", sq)
                            V.tensor_tensor(qd[:, hs_], qs[:, hs_], qi[:, hs_],
                                            OP.subtract)
                            V.drain()
                            V.scalar_tensor_tensor(qb[:, hs_], qd[:, hs_], 0.5,
                                                   qi[:, hs_], OP.is_ge, OP.add)
                            V.drain()
                            if j2 == 0:
                                self.w(V, "sT", st["hq_tr_T"])
                            V.scalar_tensor_tensor(hqm[:, c0:c0+512], qd[:, hs_], -0.5,
                                                   qb[:, hs_], OP.is_lt, OP.subtract)
                            V.drain()
                            svh[h] = self.inc(V.last, "sV")

                        if t + 1 < NT:
                            xq_transposes(t + 1)
                            ffn1_gelu(t + 1, q2=q2slice)
                        else:
                            for j2 in range(8):
                                q2slice(j2)
                        st["q2_free_v"] = max(svh[0], svh[1])
                        v_hq = self.cnt["sV"]
                        for k in range(HID // 128):
                            tbs = rot_tb.acquire(TE)
                            self.w(TE, "sV", v_hq)
                            TE.transpose(ps_tb[:, tbs, 0:128], hqm[:, k*128:(k+1)*128],
                                         self.identb[:])
                            tt_ = self.inc(TE.last, "sT")
                            self.w(S, "sT", tt_)
                            S.copy(HqF[:, k, :], ps_tb[:, tbs, 0:128])
                            S.drain()
                            self.inc(S.last, "sS")
                            rot_tb.release(("sS", self.cnt["sS"]))
                        st["hq_tr_T"] = self.cnt["sT"]
                        hqf_done = ("sS", self.cnt["sS"])
                        ots = 0
                        ffn2_T = []
                        for half in range(2):
                            ovs = rot_ov.acquire(TE)
                            self.w(TE, *hqf_done)
                            for k in range(HID // 128):
                                TE.matmul(ps_o[:, ovs, :], HqF[:, k, :],
                                          W2[:, k, half*512:(half+1)*512],
                                          start=(k == 0), stop=(k == HID // 128 - 1))
                            ffn2_T.append((ovs, self.inc(TE.last, "sT")))
                        if t + 2 < NT:
                            stage1a(t + 2)
                            stage1f(t + 2)
                        if first:
                            for _ in range(3):
                                if self.cache_stores:
                                    dc, wsrc, ev = self.cache_stores.pop(0)
                                    self.w(SY, "sV", ev)
                                    SY.dma_start(out=dc, in_=wsrc).then_inc(
                                        self.sem["dG"], 16)
                                    self.bump("dG", 16)

                        def stageB(t=t, sl=sl, ots=ots, ffn2_T=ffn2_T):
                            first_ot = True
                            for half, (ovs, tt_) in enumerate(ffn2_T):
                                self.w(V, "sT", tt_)
                                if first_ot:
                                    self.w(V, "SO%d" % ots, so_tgt[ots])
                                    first_ot = False
                                V.tensor_scalar(ot[:, ots, half*512:(half+1)*512],
                                                ps_o[:, ovs, :], dq2[:, (t % 2):(t % 2)+1],
                                                None, OP.mult)
                                self.inc(V.last, "sV")
                                rot_ov.release()
                            V.drain()
                            V.tensor_tensor(ot[:, ots, :], ot[:, ots, :], rt[:, sl, :], OP.add)
                            V.drain()
                            ot_v = self.inc(V.last, "sV")
                            rot_rt.rel[rt_rel_idx[t]] = ("sV", ot_v)
                            self.w(SY, "sV", ot_v)
                            _, so_tgt[ots] = self.dma(SY, dst[t*128:(t+1)*128, :],
                                                      ot[:, ots, :], "SO%d" % ots)
                        pend.append(stageB)
                    while pend:
                        pend.pop(0)()
                    self.w(SY, "SO0", self.cnt["SO0"])
                    self.w(SY, "SO1", self.cnt["SO1"])
                    if first:
                        self.w(SY, "dG", self.cnt["dG"])
                blk.vector(body)

    def _ternarize(self, W1, W2, tag):
        nc = self.nc
        V, S, TE, SY, G = self.V, self.S, self.TE, self.SY, self.G
        with contextlib.ExitStack() as p2:
            sbl = lambda n, shp, dt=F32: p2.enter_context(nc.sbuf_tensor(tag + "T" + n, shp, dt))
            wtmp = sbl("_wtmp", [128, 2, HID])
            acc = sbl("_acc", [128, 1])
            part = sbl("_part", [128, 1])
            zt = sbl("_zt", [128, HID])
            tp = sbl("_tp", [128, HID])
            onec = sbl("_onec", [128, 1])
            psm = p2.enter_context(nc.psum_tensor(tag + "T_psm", [128, 2], F32))

            with nc.Block() as blk:
                def body(_):
                    V.memset(onec[:], 1.0)
                    V.memset(acc[:], 0.0)
                    V.drain()
                    self.inc(V.last, "sV")
                    rot_w = Rot(self, "sV", 2)
                    plans = [(self.d_w1t, HID, 8, 0), (self.d_w2t, D, 32, 1)]
                    for dram, width, ntile, idx in plans:
                        for i in range(ntile):
                            sl = rot_w.acquire(SY)
                            lw = self.dma(SY, wtmp[:, sl, 0:width],
                                          dram[i*128:(i+1)*128, :], "X%d" % sl)
                            self.w(V, *lw)
                            V.reduce_sum(part[:], wtmp[:, sl, 0:width], axis=AX.X,
                                         apply_absolute_value=True)
                            V.drain()
                            V.tensor_tensor(acc[:], acc[:], part[:], OP.add)
                            V.drain()
                            self.inc(V.last, "sV")
                            rot_w.release()
                        av = self.cnt["sV"]
                        self.w(TE, "sV", av)
                        TE.matmul(psm[0:1, idx:idx+1], acc[:], onec[:], start=True, stop=True)
                        tt_ = self.inc(TE.last, "sT")
                        self.w(V, "sT", tt_)
                        V.tensor_scalar(self.s12row[0:1, idx:idx+1], psm[0:1, idx:idx+1],
                                        1.0 / (D * HID), None, OP.mult)
                        V.drain()
                        V.memset(acc[:], 0.0)
                        V.drain()
                        self.inc(V.last, "sV")
                    V.tensor_scalar(self.s12row[:], self.s12row[:], 1e-5, None, OP.max)
                    V.drain()
                    sv = self.inc(V.last, "sV")
                    self.w(S, "sV", sv)
                    S.dma_start(out=self.d_s12[:], in_=self.s12row[:]).then_inc(self.sem["dG"], 16)
                    self.bump("dG", 16)
                    self.w(TE, "sV", sv)
                    TE.matmul(psm[:, 0:1], self.ones1[:], self.s12row[0:1, 0:1], start=True, stop=True)
                    TE.matmul(psm[:, 1:2], self.ones1[:], self.s12row[0:1, 1:2], start=True, stop=True)
                    tt_ = self.inc(TE.last, "sT")
                    self.w(V, "sT", tt_)
                    V.tensor_copy(self.s1b[:], psm[:, 0:1])
                    V.tensor_copy(self.s2b[:], psm[:, 1:2])
                    V.drain()
                    V.reciprocal(self.is1b[:], self.s1b[:])
                    V.reciprocal(self.is2b[:], self.s2b[:])
                    V.drain()
                    bv = self.inc(V.last, "sV")

                    plans2 = [(self.d_w1t, self.d_w1q, W1, HID, 8, self.is1b),
                              (self.d_w2t, self.d_w2q, W2, D, 32, self.is2b)]
                    zt_free = 0
                    for dram, dcache, Wsb, width, ntile, isb in plans2:
                        for i in range(ntile):
                            sl = rot_w.acquire(SY)
                            lw = self.dma(SY, wtmp[:, sl, 0:width],
                                          dram[i*128:(i+1)*128, :], "X%d" % sl)
                            self.w(S, *lw)
                            self.w(S, "sV", max(bv, zt_free))
                            S.activation(zt[:, 0:width], wtmp[:, sl, 0:width],
                                         AF.Identity, scale=isb[:])
                            s_z = self.inc(S.last, "sS")
                            self.w(V, "sS", s_z)
                            V.tensor_scalar(tp[:, 0:width], zt[:, 0:width], 0.5, None, OP.is_gt)
                            V.drain()
                            V.scalar_tensor_tensor(Wsb[:, i, :], zt[:, 0:width], -0.5,
                                                   tp[:, 0:width], OP.is_lt, OP.subtract)
                            V.drain()
                            wv_ = self.inc(V.last, "sV")
                            zt_free = wv_
                            rot_w.release()
                            self.cache_stores.append(
                                (dcache[i*128:(i+1)*128, :], Wsb[:, i, :], wv_))
                blk.vector(body)

    def phase_C(self):
        nc = self.nc
        V, S, TE, SY, G = self.V, self.S, self.TE, self.SY, self.G
        CH = 512
        NCH = T // CH
        NS = CH // 128
        with contextlib.ExitStack() as ph:
            sbl = lambda n, shp, dt=F32: ph.enter_context(nc.sbuf_tensor("C" + n, shp, dt))
            psl = lambda n, shp, dt=F32: ph.enter_context(nc.psum_tensor("C" + n, shp, dt))
            XnFc = sbl("_XnF", [128, NK, CH])
            qFd = sbl("_qF", [128, NK, CH])
            kFd = sbl("_kF", [128, NK, CH])
            qS = sbl("_qS", [128, NK, 128])
            kS = sbl("_kS", [128, NK, 128])
            ktok = sbl("_ktok", [128, NS, D])
            Vp = sbl("_Vp", [128, NS, 64 * 17])
            wbufC = sbl("_w", [128, 2, NK, 512])
            Oc = sbl("_Oc", [128, NS, D])
            OF = sbl("_OF", [128, NK, 128])
            rtC = sbl("_rt", [128, NS, D])
            stateSB = sbl("_st", [128, 16 * 68])
            triu4 = sbl("_triu4", [128, 512])
            PTb = sbl("_PT", [128, 2, 512])
            lnsc = self.ln_scratch(ph, "C_ln")
            xn = sbl("_xn", [128, D])
            rden = sbl("_rden", [8, 512])
            zero512 = sbl("_zero512", [128, 512])
            rd4 = sbl("_rd4", [128, 4])
            bvb = sbl("_bvb", [128, D])
            bob = sbl("_bob", [128, D])
            brow = sbl("_brow", [1, D])
            ps_quad = psl("_ps_quad", [128, 4, 512])
            ps_N = psl("_ps_N", [128, 2, 512])
            ps_st = psl("_ps_state", [128, 512])
            ps_d = psl("_ps_d", [128, 512])


            with nc.Block() as blk:
                def body(_):
                    rot_tr = Rot(self, "sV", 2)
                    rot_w = Rot(self, "sT", 2)
                    rot_p = Rot(self, "sV", 2)
                    rot_N = Rot(self, "sV", 2)
                    rot_PT = Rot(self, "sT", 2)
                    rot_ot = Rot(self, "dG", 2)
                    rot_rt = Rot(self, "sV", 4)

                    V.memset(zero512[:], 0.0)
                    V.drain()
                    V.memset(stateSB[:], 0.0)
                    V.drain()
                    V.memset(qS[96:128, :, :], 0.0)
                    V.drain()
                    V.memset(kS[96:128, :, :], 0.0)
                    V.drain()
                    self.inc(V.last, "sV")
                    self.dma(SY, triu4[:], self.d_triu4[:], "CST")
                    for row_d, dst in [(self.d_bvrow, bvb), (self.d_borow, bob)]:
                        self.w(SY, "sT", self.cnt["sT"])
                        bw = self.dma(SY, brow[:], row_d[:], "CST")
                        self.w(TE, *bw)
                        for half in range(2):
                            TE.matmul(ps_d[:, 0:512], self.ones1[:],
                                      brow[:, half*512:(half+1)*512], start=True, stop=True)
                            tt_ = self.inc(TE.last, "sT")
                            self.w(V, "sT", tt_)
                            V.tensor_copy(dst[:, half*512:(half+1)*512], ps_d[:, 0:512])
                            fv = self.inc(V.last, "sV")
                            self.w(TE, "sV", fv)
                    bias_done = self.cnt["sV"]

                    st_upd_T = 0
                    chunk_T = 0
                    last_s = 0

                    for c in range(NCH):
                        for s_ in range(NS):
                            t0 = c * NS + s_
                            sl = rot_rt.acquire(SY)
                            xw = self.dma(SY, rtC[:, sl, :],
                                          self.d_r2[t0*128:(t0+1)*128, :],
                                          ["X0", "X1", "XR0", "XR1"][sl])
                            v_done, s_done = self.emit_ln(
                                rtC[:, sl, :], lnsc,
                                pre_wait=[xw, ("sS", last_s), ("sT", chunk_T)])
                            self.w(S, "sV", v_done)
                            self.w(S, "sT", self.cnt["sT"])
                            S.activation(xn[:], lnsc["xc"], AF.Identity, scale=lnsc["rstd"])
                            S.drain()
                            s_xn = self.inc(S.last, "sS")
                            last_s = s_xn
                            for k in range(NK):
                                trs = rot_tr.acquire(TE)
                                self.w(TE, "sS", s_xn)
                                TE.transpose(ps_quad[:, trs, 0:128], xn[:, k*128:(k+1)*128],
                                             self.identf[:])
                                tt_ = self.inc(TE.last, "sT")
                                self.w(V, "sT", tt_)
                                V.tensor_copy(XnFc[:, k, s_*128:(s_+1)*128], ps_quad[:, trs, 0:128])
                                self.inc(V.last, "sV")
                                rot_tr.release()
                        xnf_done = self.cnt["sV"]
                        if self.c_stop <= 1:
                            continue

                        for half in range(2):
                            ws = rot_w.acquire(SY)
                            w_ld = self.dma(SY, wbufC[:, ws, :, :],
                                            self.d_wq[:, half*512:(half+1)*512].rearrange(
                                                "(k p) n -> p k n", p=128), "W%d" % ws)
                            for oo in range(4):
                                o = half * 4 + oo
                                pps = rot_p.acquire(TE)
                                self.w(TE, *w_ld)
                                self.w(TE, "sV", xnf_done)
                                for k in range(NK):
                                    TE.matmul(ps_quad[:, 2 + pps, :], wbufC[:, ws, k, oo*128:(oo+1)*128],
                                              XnFc[:, k, :], start=(k == 0), stop=(k == NK - 1))
                                tt_ = self.inc(TE.last, "sT")
                                self.w(S, "sT", tt_)
                                S.activation(qFd[:, o, :], ps_quad[:, 2 + pps, :], AF.Exp,
                                             bias=self.bebq[:, o:o+1], scale=self.be_b[:])
                                S.drain()
                                sev = self.inc(S.last, "sS")
                                rot_p.release(("sS", sev))
                            rot_w.release()
                        q_done_s = self.cnt["sS"]
                        if self.c_stop <= 2:
                            continue

                        knorm_v = 0
                        for half in range(2):
                            ws = rot_w.acquire(SY)
                            w_ld = self.dma(SY, wbufC[:, ws, :, :],
                                            self.d_wk[:, half*512:(half+1)*512].rearrange(
                                                "(k p) n -> p k n", p=128), "W%d" % ws)
                            for oo in range(4):
                                o = half * 4 + oo
                                pps = rot_p.acquire(TE)
                                self.w(TE, *w_ld)
                                self.w(TE, "sV", xnf_done)
                                for k in range(NK):
                                    TE.matmul(ps_quad[:, 2 + pps, :], wbufC[:, ws, k, oo*128:(oo+1)*128],
                                              XnFc[:, k, :], start=(k == 0), stop=(k == NK - 1))
                                tt_ = self.inc(TE.last, "sT")
                                self.w(S, "sT", tt_)
                                S.activation(kFd[:, o, :], ps_quad[:, 2 + pps, :], AF.Exp,
                                             bias=self.bebk[:, o:o+1], scale=self.be_b[:])
                                S.drain()
                                sev = self.inc(S.last, "sS")
                                self.w(TE, "sS", sev)
                                self.w(TE, "sV", knorm_v)
                                TE.matmul(ps_d[0:8, 0:512], self.bd8[:], kFd[:, o, :],
                                          start=True, stop=True)
                                t1 = self.inc(TE.last, "sT")
                                self.w(V, "sT", t1)
                                V.reciprocal(rden[:], ps_d[0:8, 0:512])
                                V.drain()
                                rv = self.inc(V.last, "sV")
                                self.w(TE, "sV", rv)
                                TE.matmul(ps_d[:, 0:512], self.bd8t[:], rden[:],
                                          start=True, stop=True)
                                t2 = self.inc(TE.last, "sT")
                                self.w(V, "sT", t2)
                                V.tensor_tensor(kFd[:, o, :], kFd[:, o, :], ps_d[:, 0:512],
                                                OP.mult)
                                V.drain()
                                knorm_v = self.inc(V.last, "sV")
                                rot_p.release()
                            rot_w.release()
                        k_done_v = self.cnt["sV"]
                        if self.c_stop <= 3:
                            continue

                        for s_ in range(NS):
                            for o in range(NK):
                                trs = rot_tr.acquire(TE)
                                self.w(TE, "sV", k_done_v)
                                TE.transpose(ps_quad[:, trs, 0:128], kFd[:, o, s_*128:(s_+1)*128],
                                             self.identf[:])
                                tt_ = self.inc(TE.last, "sT")
                                self.w(V, "sT", tt_)
                                V.tensor_copy(ktok[:, s_, o*128:(o+1)*128], ps_quad[:, trs, 0:128])
                                self.inc(V.last, "sV")
                                rot_tr.release()
                        ktok_done = self.cnt["sV"]
                        if self.c_stop <= 4:
                            continue

                        self.w(V, "sV", bias_done)
                        V.memset(Vp[:].rearrange("p s (h e) -> p s h e", e=17)[:, :, :, 16:17],
                                 1.0)
                        V.drain()
                        ones_v = self.inc(V.last, "sV")
                        for half in range(2):
                            ws = rot_w.acquire(SY)
                            w_ld = self.dma(SY, wbufC[:, ws, :, :],
                                            self.d_wv[:, half*512:(half+1)*512].rearrange(
                                                "(k p) n -> p k n", p=128), "W%d" % ws)
                            for s_ in range(NS):
                                pps = rot_p.acquire(TE)
                                self.w(TE, *w_ld)
                                self.w(TE, "sV", xnf_done)
                                for k in range(NK):
                                    TE.matmul(ps_quad[:, 2 + pps, :],
                                              XnFc[:, k, s_*128:(s_+1)*128],
                                              wbufC[:, ws, k, :],
                                              start=(k == 0), stop=(k == NK - 1))
                                tt_ = self.inc(TE.last, "sT")
                                self.w(V, "sT", tt_)
                                self.w(V, "sV", ones_v)
                                vview = Vp[:, s_, :].rearrange("p (h e) -> p h e", e=17)
                                pview = ps_quad[:, 2 + pps, :].rearrange("p (h e) -> p h e", e=16)
                                bview = (bvb[:, half*512:(half+1)*512]
                                         .rearrange("p (h e) -> p h e", e=16))
                                V.tensor_tensor(vview[:, half*32:(half+1)*32, 0:16],
                                                pview, bview, OP.add)
                                V.drain()
                                self.inc(V.last, "sV")
                                rot_p.release()
                            rot_w.release()
                        vp_done = self.cnt["sV"]
                        if self.c_stop <= 5:
                            continue
                        prev_pt_v = vp_done

                        scan_sub_T = 0
                        for s_ in range(NS):
                            cg = c * NS + s_
                            self.w(S, "sS", q_done_s)
                            self.w(S, "sV", vp_done)
                            self.w(S, "sT", scan_sub_T)
                            self.dma(S, qS[0:112, :, :],
                                     qFd[16:128, :, s_*128:(s_+1)*128], "SH")
                            shift_ld = self.dma(S, kS[0:112, :, :],
                                                kFd[16:128, :, s_*128:(s_+1)*128], "SH")
                            if cg > 0:
                                self.w(V, "sT", st_upd_T)
                                for a4 in range(4):
                                    V.tensor_copy(
                                        stateSB[32*a4:32*a4+16, :].rearrange(
                                            "p (g e) -> p g e", e=68)[:, :, 17*a4:17*a4+17],
                                        ps_st[32*a4:32*a4+16, 0:272].rearrange(
                                            "p (g e) -> p g e", e=17))
                                V.drain()
                                st_ev = self.inc(V.last, "sV")
                            else:
                                st_ev = self.cnt["sV"]
                            if cg == 0:
                                self.w(TE, "sV", st_ev)
                                TE.matmul(ps_st[0:128, 0:512], self.identf[:],
                                          zero512[:], start=True, stop=True)
                                st_upd_T = self.inc(TE.last, "sT")
                            for o in range(NK):
                                for par in range(2):
                                    g = 2 * o + par
                                    if par == 0:
                                        qT = qFd[:, o, s_*128:(s_+1)*128]
                                        kT = kFd[:, o, s_*128:(s_+1)*128]
                                    else:
                                        qT = qS[:, o, :]
                                        kT = kS[:, o, :]
                                    self.w(TE, "sS", q_done_s)
                                    self.w(TE, "sV", k_done_v)
                                    self.w(TE, *shift_ld)
                                    self.w(TE, "sV", prev_pt_v)
                                    for a in range(4):
                                        bofs = 32 * a
                                        TE.matmul(ps_quad[:, a, 0:128],
                                                  kT[bofs:bofs+16, :],
                                                  qT[bofs:bofs+16, :],
                                                  start=True, stop=True,
                                                  tile_position=(bofs, 0),
                                                  skip_group_check=True)
                                    t_s = self.inc(TE.last, "sT")
                                    pts = rot_PT.acquire(V)
                                    self.w(V, "sT", t_s)
                                    V.tensor_tensor(
                                        PTb[:, pts, :].rearrange("p (a e) -> p a e", e=128),
                                        ps_quad[:, 0:4, 0:128],
                                        triu4[:].rearrange("p (a e) -> p a e", e=128),
                                        OP.mult)
                                    V.drain()
                                    pt_v = self.inc(V.last, "sV")
                                    prev_pt_v = pt_v
                                    Ns = rot_N.acquire(TE)
                                    self.w(TE, "sV", vp_done)
                                    self.w(TE, "sV", st_ev)
                                    first = True
                                    if cg > 0:
                                        TE.matmul(ps_N[:, Ns, 0:68], qT,
                                                  stateSB[:, g*68:(g+1)*68],
                                                  start=True, stop=False,
                                                  skip_group_check=True)
                                        first = False
                                    self.w(TE, "sV", ktok_done)
                                    for a in range(4):
                                        h = 8 * o + par + 2 * a
                                        bofs = 32 * a
                                        TE.matmul(ps_st[bofs:bofs+16, g*17:(g+1)*17],
                                                  ktok[:, s_, h*16:(h+1)*16],
                                                  Vp[:, s_, h*17:(h+1)*17],
                                                  start=False, stop=False,
                                                  tile_position=(0, bofs),
                                                  skip_group_check=True)
                                    st_upd_T = self.inc(TE.last, "sT")
                                    self.w(TE, "sV", pt_v)
                                    for a in range(4):
                                        h = 8 * o + par + 2 * a
                                        TE.matmul(ps_N[:, Ns, a*17:(a+1)*17],
                                                  PTb[:, pts, a*128:(a+1)*128],
                                                  Vp[:, s_, h*17:(h+1)*17],
                                                  start=first, stop=(a == 3),
                                                  skip_group_check=True)
                                        first = False
                                    t_n = self.inc(TE.last, "sT")
                                    rot_PT.release()
                                    self.w(V, "sT", t_n)
                                    if o == 0 and par == 0 and c > 0 and self.c_stop > 7:
                                        self.w(V, "SO%d" % s_, ot_store[s_])
                                    nview = ps_N[:, Ns, 0:68].rearrange("p (h e) -> p h e", e=17)
                                    V.reciprocal(rd4[:], nview[:, 0:4, 16])
                                    V.drain()
                                    h0 = 8 * o + par
                                    ov = Oc[:, s_, :].rearrange(
                                        "p (hh e) -> p hh e", e=16)[:, h0:h0+7:2, :]
                                    rb = rd4[:].unsqueeze(-1).broadcast_to((128, 4, 16))
                                    V.tensor_tensor(ov, nview[:, 0:4, 0:16], rb, OP.mult)
                                    V.drain()
                                    self.inc(V.last, "sV")
                                    rot_N.release()
                            scan_sub_T = self.cnt["sT"]
                        oc_done = self.cnt["sV"]
                        if self.c_stop <= 7:
                            chunk_T = self.cnt["sT"]
                            continue

                        if c == 0:
                            ot_store = {}
                        w_lds = []
                        wss = []
                        for half in range(2):
                            ws = rot_w.acquire(SY)
                            w_lds.append(self.dma(
                                SY, wbufC[:, ws, :, :],
                                self.d_wo[:, half*512:(half+1)*512].rearrange(
                                    "(k p) n -> p k n", p=128), "W%d" % ws))
                            wss.append(ws)
                        for s_ in range(NS):
                            t0 = c * NS + s_
                            for k in range(NK):
                                trs = rot_tr.acquire(TE)
                                self.w(TE, "sV", oc_done)
                                TE.transpose(ps_quad[:, trs, 0:128], Oc[:, s_, k*128:(k+1)*128],
                                             self.identf[:])
                                tt_ = self.inc(TE.last, "sT")
                                self.w(V, "sT", tt_)
                                V.tensor_copy(OF[:, k, :], ps_quad[:, trs, 0:128])
                                self.inc(V.last, "sV")
                                rot_tr.release()
                            of_done = self.cnt["sV"]
                            for half in range(2):
                                pps = rot_p.acquire(TE)
                                self.w(TE, *w_lds[half])
                                self.w(TE, "sV", of_done)
                                for k in range(NK):
                                    TE.matmul(ps_quad[:, 2 + pps, :], OF[:, k, :],
                                              wbufC[:, wss[half], k, :],
                                              start=(k == 0), stop=(k == NK - 1))
                                tt_ = self.inc(TE.last, "sT")
                                self.w(V, "sT", tt_)
                                self.w(V, "sV", bias_done)
                                V.tensor_tensor(Oc[:, s_, half*512:(half+1)*512],
                                                ps_quad[:, 2 + pps, :],
                                                bob[:, half*512:(half+1)*512], OP.add)
                                V.drain()
                                self.inc(V.last, "sV")
                                rot_p.release()
                            V.tensor_tensor(Oc[:, s_, :], Oc[:, s_, :],
                                            rtC[:, s_, :], OP.add)
                            V.drain()
                            ot_v = self.inc(V.last, "sV")
                            rot_rt.release(("sV", ot_v))
                            self.w(SY, "sV", ot_v)
                            _, ot_store[s_] = self.dma(
                                SY, self.d_r3[t0*128:(t0+1)*128, :],
                                Oc[:, s_, :], "SO%d" % s_)
                        for half in range(2):
                            rot_w.release()
                        chunk_T = self.cnt["sT"]
                    for i4 in range(4):
                        self.w(SY, "SO%d" % i4, self.cnt["SO%d" % i4])
                blk.vector(body)


    def copy_dram(self, src, dst, tagn):
        nc = self.nc
        with contextlib.ExitStack() as ph:
            buf = ph.enter_context(nc.sbuf_tensor("cp%s" % tagn, [128, 2, D], F32))
            with nc.Block() as blk:
                def body(_):
                    so = {0: 0, 1: 0}
                    for t in range(NT):
                        sl = t % 2
                        self.w(self.SY, "SO%d" % sl, so[sl])
                        lw = self.dma(self.SY, buf[:, sl, :], src[t*128:(t+1)*128, :],
                                      "X%d" % sl)
                        self.w(self.S, *lw)
                        _, so[sl] = self.dma(self.S, dst[t*128:(t+1)*128, :], buf[:, sl, :],
                                             "SO%d" % sl)
                    self.w(self.S, "SO0", self.cnt["SO0"])
                    self.w(self.S, "SO1", self.cnt["SO1"])
                blk.vector(body)

    def build(self):
        self.setup_consts()
        if "A" in self.phases:
            self.phase_A()
        else:
            self.copy_dram(self.d_x, self.d_r1, "A")
        if "B" in self.phases:
            self.phase_B(self.d_r1, self.d_r2, first=True, tag="B")
        else:
            self.copy_dram(self.d_r1, self.d_r2, "B")
        if "C" in self.phases:
            self.phase_C()
        else:
            self.copy_dram(self.d_r2, self.d_r3, "C")
        if "D" in self.phases:
            self.phase_B(self.d_r3, self.d_y, first=("B" not in self.phases), tag="D")
        else:
            self.copy_dram(self.d_r3, self.d_y, "D")
        self.st.close()
        return self.nc


def host_consts():
    import ml_dtypes
    eye = np.eye(128, dtype=np.float32)
    eyeb = np.eye(128, dtype=ml_dtypes.bfloat16)
    iota128 = np.broadcast_to(np.arange(T, dtype=np.float32)[None, :], (128, T)).copy()
    iotacol = np.arange(128, dtype=np.float32)[:, None].copy()
    q = np.arange(128)[:, None]
    kk = np.arange(128)[None, :]
    amask = np.where(kk <= q, 0.0, NEG).astype(np.float32)
    tri = (np.arange(128)[:, None] <= np.arange(128)[None, :]).astype(np.float32)
    triu4 = np.concatenate([tri] * 4, axis=1).astype(np.float32)
    bd8 = np.zeros((128, 8), dtype=np.float32)
    for h in range(8):
        bd8[h*16:(h+1)*16, h] = 1.0
    bd8t = np.ascontiguousarray(bd8.T)
    ones1 = np.ones((1, 128), dtype=np.float32)
    return dict(identf=eye, identb=eyeb, iota128=iota128, iotacol=iotacol,
                amask=amask, triu4=triu4, bd8=bd8, bd8t=bd8t, ones1=ones1)


def stage_inputs(inputs, core):
    c = host_consts()
    f32 = np.float32
    m = dict(c)
    m["x"] = np.ascontiguousarray(np.asarray(inputs["L0_attn_a_x"], dtype=f32)[core])
    m["wqk"] = np.ascontiguousarray(np.asarray(inputs["L0_attn_a_W_qk"], dtype=f32))
    m["wva"] = np.ascontiguousarray(np.asarray(inputs["L0_attn_a_W_v"], dtype=f32).T)
    m["alibi"] = np.asarray(inputs["L0_attn_a_alibi_slope"], dtype=f32).reshape(1, 1).copy()
    m["wq"] = np.ascontiguousarray(np.asarray(inputs["L0_attn_b_Wq"], dtype=f32).T)
    m["wk"] = np.ascontiguousarray(np.asarray(inputs["L0_attn_b_Wk"], dtype=f32).T)
    m["wv"] = np.ascontiguousarray(np.asarray(inputs["L0_attn_b_Wv"], dtype=f32).T)
    m["wo"] = np.ascontiguousarray(np.asarray(inputs["L0_attn_b_Wo"], dtype=f32).T)
    m["bqF"] = np.ascontiguousarray(np.asarray(inputs["L0_attn_b_bq"], dtype=f32).reshape(8, 128).T)
    m["bkF"] = np.ascontiguousarray(np.asarray(inputs["L0_attn_b_bk"], dtype=f32).reshape(8, 128).T)
    m["bvrow"] = np.asarray(inputs["L0_attn_b_bv"], dtype=f32).reshape(1, D).copy()
    m["borow"] = np.asarray(inputs["L0_attn_b_bo"], dtype=f32).reshape(1, D).copy()
    m["beta"] = np.asarray(inputs["L0_attn_b_beta"], dtype=f32).reshape(1, 1).copy()
    m["w1t"] = np.ascontiguousarray(np.asarray(inputs["L0_ffn_w1"], dtype=f32).T)
    m["w2t"] = np.ascontiguousarray(np.asarray(inputs["L0_ffn_w2"], dtype=f32).T)
    return m


def kernel(**inputs):
    import numpy as np
    from concourse.bass_utils import run_bass_kernel_spmd
    n = 8
    nc = MK(phases=("A", "B", "C", "D")).build()
    in_maps = [stage_inputs(inputs, c) for c in range(n)]
    res = run_bass_kernel_spmd(nc, in_maps, core_ids=list(range(n)))
    return np.stack([np.asarray(res.results[c]["y"]) for c in range(n)], axis=0)


# version 20
# speedup vs baseline: 1.0798x; 1.0028x over previous
import contextlib
import numpy as np
import concourse.bass as bass
import concourse.mybir as mybir

F32 = mybir.dt.float32
BF16 = mybir.dt.bfloat16
I32 = mybir.dt.int32
AF = mybir.ActivationFunctionType
OP = mybir.AluOpType
AX = mybir.AxisListType

T = 2048
D = 1024
HID = 4096
NT = T // 128
NK = D // 128
SCALE_A = 1.0 / 32.0
NEG = -1e30


class Rot:

    def __init__(self, mk, sem, n):
        self.mk, self.sem, self.n = mk, sem, n
        self.rel = []
        self.i = 0

    def acquire(self, eng):
        if self.i >= self.n:
            ev = self.rel[self.i - self.n]
            self.mk.w(eng, ev[0], ev[1])
        s = self.i % self.n
        self.i += 1
        return s

    def release(self, ev=None):
        if ev is None:
            ev = (self.sem, self.mk.cnt[self.sem])
        self.rel.append(ev)


class EngW:
    _PASS = {"wait_ge", "drain", "nop", "sem_inc", "sem_clear"}

    def __init__(self, eng):
        object.__setattr__(self, "_e", eng)
        object.__setattr__(self, "last", None)

    def __getattr__(self, n):
        f = getattr(self._e, n)
        if n in EngW._PASS or not callable(f):
            return f
        def call(*a, **k):
            r = f(*a, **k)
            object.__setattr__(self, "last", r)
            return r
        return call


class MK:
    def __init__(self, phases=("A", "B", "C", "D"), a_stop=99, c_stop=99):
        self.a_stop = a_stop
        self.c_stop = c_stop
        self.phases = phases
        nc = bass.Bass("TRN2", target_bir_lowering=False)
        self.nc = nc
        self.st = contextlib.ExitStack()
        self.cnt = {}
        self.sem = {}
        for name in ["dS", "dA", "dG", "sT", "sV", "sS",
                     "X0", "X1", "XR0", "XR1", "W0", "W1", "W2", "W3",
                     "SH", "CST", "SO0", "SO1", "SO2", "SO3"]:
            self.sem[name] = self.st.enter_context(nc.semaphore(name))
            self.cnt[name] = 0
        self.V, self.S, self.TE = EngW(nc.vector), EngW(nc.scalar), EngW(nc.tensor)
        self.SY, self.G = EngW(nc.sync), EngW(nc.gpsimd)
        self._declare_io()

    def inc(self, res, name, n=1):
        res.then_inc(self.sem[name], n)
        self.cnt[name] += n
        return self.cnt[name]

    def bump(self, name, n):
        self.cnt[name] += n
        return self.cnt[name]

    def w(self, eng, name, v):
        if v > 0:
            eng.wait_ge(self.sem[name], v)

    def dma(self, eng, out, in_, slot):
        r = eng.dma_start(out=out, in_=in_)
        r.then_inc(self.sem[slot], 16)
        self.cnt[slot] += 16
        return (slot, self.cnt[slot])

    def _declare_io(self):
        nc = self.nc
        dt_ = lambda n, shp, dt=F32, kind="ExternalInput": nc.dram_tensor(n, shp, dt, kind=kind).ap()
        self.d_x = dt_("x", [T, D])
        self.d_y = dt_("y", [T, D], kind="ExternalOutput")
        self.d_wqk = dt_("wqk", [D, D])
        self.d_wva = dt_("wva", [D, D])
        self.d_alibi = dt_("alibi", [1, 1])
        self.d_wq = dt_("wq", [D, D])
        self.d_wk = dt_("wk", [D, D])
        self.d_wv = dt_("wv", [D, D])
        self.d_wo = dt_("wo", [D, D])
        self.d_bqF = dt_("bqF", [128, 8])
        self.d_bkF = dt_("bkF", [128, 8])
        self.d_bvrow = dt_("bvrow", [1, D])
        self.d_borow = dt_("borow", [1, D])
        self.d_beta = dt_("beta", [1, 1])
        self.d_w1t = dt_("w1t", [D, HID])
        self.d_w2t = dt_("w2t", [HID, D])
        self.d_identf = dt_("identf", [128, 128])
        self.d_identb = dt_("identb", [128, 128], BF16)
        self.d_iota128 = dt_("iota128", [128, T])
        self.d_iotacol = dt_("iotacol", [128, 1])
        self.d_amask = dt_("amask", [128, 128])
        self.d_triu4 = dt_("triu4", [128, 512])
        self.d_bd8 = dt_("bd8", [128, 8])
        self.d_bd8t = dt_("bd8t", [8, 128])
        self.d_ones1 = dt_("ones1", [1, 128])
        self.d_r1 = dt_("r1", [T, D], kind="Internal")
        self.d_r2 = dt_("r2", [T, D], kind="Internal")
        self.d_r3 = dt_("r3", [T, D], kind="Internal")
        self.d_w1q = dt_("w1q", [D, HID], BF16, kind="Internal")
        self.d_w2q = dt_("w2q", [HID, D], BF16, kind="Internal")
        self.d_s12 = dt_("s12", [1, 2], kind="Internal")
        self.dbg_xq = dt_("dbg_xq", [T, D], BF16, kind="Internal")
        self.dbg_h = dt_("dbg_h", [T, HID], kind="Internal")
        self.dbg_hq = dt_("dbg_hq", [T, HID], BF16, kind="Internal")
        self.dbg_am = dt_("dbg_am", [T, 4], kind="Internal")
        self.dbg_yq = dt_("dbg_yq", [T, D], kind="Internal")
        self.dbg_i1 = dt_("dbg_i1", [T, D], I32, kind="Internal")
        self.dbg_dl = dt_("dbg_dl", [T, D], kind="Internal")

    def sb(self, name, shape, dt=F32):
        return self.st.enter_context(self.nc.sbuf_tensor(name, shape, dt))

    def setup_consts(self):
        sb = self.sb
        self.identf = sb("c_identf", [128, 128])
        self.identb = sb("c_identb", [128, 128], BF16)
        self.iotacol = sb("c_iotacol", [128, 1])
        self.bd8 = sb("c_bd8", [128, 8])
        self.bd8t = sb("c_bd8t", [8, 128])
        self.ones1 = sb("c_ones1", [1, 128])
        self.slope_b = sb("c_slope_b", [128, 1])
        self.be_b = sb("c_be_b", [128, 1])
        self.bebq = sb("c_bebq", [128, 8])
        self.bebk = sb("c_bebk", [128, 8])
        self.alibi_sb = sb("c_alibi", [1, 1])
        self.beta_sb = sb("c_beta", [1, 1])
        self.bqF_sb = sb("c_bqF", [128, 8])
        self.bkF_sb = sb("c_bkF", [128, 8])
        self.s1b = sb("c_s1b", [128, 1])
        self.s2b = sb("c_s2b", [128, 1])
        self.is1b = sb("c_is1b", [128, 1])
        self.is2b = sb("c_is2b", [128, 1])
        self.epsb = sb("c_epsb", [128, 1])
        self.s12row = sb("c_s12row", [1, 2])

        with contextlib.ExitStack() as ph:
            ps_bc = ph.enter_context(self.nc.psum_tensor("c_ps", [128, 2], F32))
            with self.nc.Block() as blk:
                def body(_):
                    V, S, TE, SY = self.V, self.S, self.TE, self.SY
                    V.memset(self.epsb[:], 1e-5)
                    V.drain()
                    for dst, src in [(self.identf, self.d_identf), (self.identb, self.d_identb),
                                     (self.iotacol, self.d_iotacol), (self.bd8, self.d_bd8),
                                     (self.bd8t, self.d_bd8t), (self.ones1, self.d_ones1),
                                     (self.alibi_sb, self.d_alibi), (self.beta_sb, self.d_beta),
                                     (self.bqF_sb, self.d_bqF), (self.bkF_sb, self.d_bkF)]:
                        SY.dma_start(out=dst[:], in_=src[:]).then_inc(self.sem["dS"], 16)
                        self.bump("dS", 16)
                    t0 = self.cnt["dS"]
                    self.w(S, "dS", t0)
                    S.activation(self.be_b[0:1, :], self.beta_sb[:], AF.Exp)
                    s0 = self.inc(S.last, "sS")
                    self.w(TE, "dS", t0)
                    self.w(TE, "sS", s0)
                    TE.matmul(ps_bc[:, 0:1], self.ones1[:], self.alibi_sb[:], start=True, stop=True)
                    TE.matmul(ps_bc[:, 1:2], self.ones1[:], self.be_b[0:1, :], start=True, stop=True)
                    tt = self.inc(TE.last, "sT")
                    self.w(V, "sT", tt)
                    V.tensor_copy(self.slope_b[:], ps_bc[:, 0:1])
                    V.tensor_copy(self.be_b[:], ps_bc[:, 1:2])
                    V.drain()
                    V.tensor_scalar(self.bebq[:], self.bqF_sb[:], self.be_b[:], None, OP.mult)
                    V.tensor_scalar(self.bebk[:], self.bkF_sb[:], self.be_b[:], None, OP.mult)
                    self.inc(V.last, "sV")
                blk.vector(body)

    def emit_ln(self, xt, sc, quant=False, pre_wait=(), s_pre_wait=()):
        V, S = self.V, self.S
        for s_, v_ in pre_wait:
            self.w(V, s_, v_)
        V.reduce_sum(sc["s1"], xt, axis=AX.X)
        V.drain()
        V.tensor_scalar(sc["negm"], sc["s1"], -1.0 / D, None, OP.mult)
        V.drain()
        v1 = self.inc(V.last, "sV")
        for s_, v_ in s_pre_wait:
            self.w(S, s_, v_)
        self.w(S, "sV", v1)
        S.activation(sc["xc"], xt, AF.Identity, bias=sc["negm"], scale=1.0)
        S.drain()
        S.activation(sc["sqd"], sc["xc"], AF.Square, accum_out=sc["ssq"])
        S.drain()
        S.activation(sc["sdt"], sc["ssq"], AF.Sqrt, bias=self.epsb[:], scale=1.0 / D)
        S.drain()
        s1 = self.inc(S.last, "sS")
        self.w(V, "sS", s1)
        V.reciprocal(sc["rstd"], sc["sdt"])
        V.drain()
        if quant:
            V.reduce_max(sc["axc"], sc["xc"], axis=AX.X, apply_absolute_value=True)
            V.drain()
            V.tensor_scalar(sc["mxn"], sc["axc"], sc["rstd"], None, OP.mult)
            V.drain()
            V.tensor_scalar(sc["mxn"], sc["mxn"], 1e-5, None, OP.max)
            V.drain()
            V.reciprocal(sc["iam"], sc["mxn"])
            V.drain()
            V.tensor_scalar(sc["scq"], sc["iam"], 127.0, None, OP.mult)
            V.drain()
            V.tensor_scalar(sc["comb"], sc["scq"], sc["rstd"], None, OP.mult)
            V.drain()
        v2 = self.inc(V.last, "sV")
        return v2, s1

    def ln_scratch(self, ph, pfx, quant=False, sqd_ap=None, xc_ap=None):
        sbl = lambda n, shp, dt=F32: ph.enter_context(self.nc.sbuf_tensor(n, shp, dt))[:]
        sc = {"s1": sbl(pfx + "_s1", [128, 1]), "negm": sbl(pfx + "_negm", [128, 1]),
              "ssq": sbl(pfx + "_ssq", [128, 1]), "sdt": sbl(pfx + "_sdt", [128, 1]),
              "rstd": sbl(pfx + "_rstd", [128, 1])}
        sc["xc"] = xc_ap if xc_ap is not None else sbl(pfx + "_xc", [128, D])
        sc["sqd"] = sqd_ap if sqd_ap is not None else sbl(pfx + "_sqd", [128, D])
        if quant:
            for k in ["axc", "mxn", "iam", "scq", "comb"]:
                sc[k] = sbl(pfx + "_" + k, [128, 1])
        return sc

    def phase_A(self):
        nc = self.nc
        V, S, TE, SY, G = self.V, self.S, self.TE, self.SY, self.G
        with contextlib.ExitStack() as ph:
            sbl = lambda n, shp, dt=F32: ph.enter_context(nc.sbuf_tensor(n, shp, dt))
            psl = lambda n, shp, dt=F32: ph.enter_context(nc.psum_tensor(n, shp, dt))
            XnF = sbl("A_XnF", [128, NK, T])
            Vtm = sbl("A_Vtm", [128, NT, D])
            wbuf = sbl("A_w", [128, NK, 512])
            qF = sbl("A_qF", [128, NK, 512])
            Ssb = sbl("A_S", [128, T])
            Ssb2 = sbl("A_S2", [128, T])
            scr = sbl("A_scr", [128, T])
            PT = sbl("A_PT", [128, 2, 128])
            xt = sbl("A_xt", [128, 2, D])
            B0 = sbl("A_B0", [128, T])
            amask = sbl("A_amask", [128, 128])
            lnsc = self.ln_scratch(ph, "A_ln", sqd_ap=scr[:, 0:D], xc_ap=Ssb[:, 0:D])
            xn = scr[:, D:2*D]
            mx = sbl("A_mx", [128, 1])
            nmx = sbl("A_nmx", [128, 1])
            sume = sbl("A_sume", [128, 1])
            rs = sbl("A_rs", [128, 1])
            qbv = sbl("A_qbv", [128, 1])
            ps_tr = psl("A_ps_tr", [128, 2, 512])
            ps_v = psl("A_ps_v", [128, 2, 512])
            ps_s = psl("A_ps_s", [128, 2, 512])
            ps_o = psl("A_ps_o", [128, 2, 512])
            wflat = wbuf[:].rearrange("p k n -> p (k n)")

            with nc.Block() as blk:
                def body(_):
                    rot_xt = Rot(self, "sS", 2)
                    rot_tr = Rot(self, "sV", 2)
                    rot_pv = Rot(self, "sV", 2)
                    rot_ps = Rot(self, "sV", 2)
                    so_tgt = {0: 0, 1: 0}
                    rot_wo = Rot(self, "sT", 4)
                    ot_i = [0]

                    self.dma(SY, amask[:], self.d_amask[:], "CST")
                    cst_t = self.dma(SY, B0[:], self.d_iota128[:], "CST")
                    last_s_done = 0
                    xn_free = 0
                    for t in range(NT):
                        sl = rot_xt.acquire(SY)
                        xw = self.dma(SY, xt[:, sl, :], self.d_x[t*128:(t+1)*128, :],
                                      "X%d" % sl)
                        if t == 0:
                            self.w(V, *cst_t)
                            V.tensor_scalar(B0[:], B0[:], self.slope_b[:], None, OP.mult)
                            V.drain()
                            self.inc(V.last, "sV")
                        v_done, s_done = self.emit_ln(
                            xt[:, sl, :], lnsc,
                            pre_wait=[xw, ("sS", last_s_done)])
                        self.w(S, "sV", v_done)
                        self.w(S, "sT", xn_free)
                        S.activation(xn, lnsc["xc"], AF.Identity, scale=lnsc["rstd"])
                        S.drain()
                        s_xn = self.inc(S.last, "sS")
                        last_s_done = s_xn
                        rot_xt.release()
                        for k in range(NK):
                            trs = rot_tr.acquire(TE)
                            self.w(TE, "sS", s_xn)
                            TE.transpose(ps_tr[:, trs, 0:128], xn[:, k*128:(k+1)*128], self.identf[:])
                            tt_ = self.inc(TE.last, "sT")
                            self.w(V, "sT", tt_)
                            V.tensor_copy(XnF[:, k, t*128:(t+1)*128], ps_tr[:, trs, 0:128])
                            self.inc(V.last, "sV")
                            rot_tr.release()
                        xn_free = self.cnt["sT"]
                    xnf_all = self.cnt["sV"]
                    if self.a_stop <= 1:
                        self.w(G, "sV", self.cnt["sV"])
                        return

                    for half in range(2):
                        self.w(SY, "sT", self.cnt["sT"])
                        ww = self.dma(SY, wbuf[:],
                                      self.d_wva[:, half*512:(half+1)*512].rearrange(
                                          "(k p) n -> p k n", p=128), "W0")
                        for t in range(NT):
                            pvs = rot_pv.acquire(TE)
                            self.w(TE, *ww)
                            self.w(TE, "sV", xnf_all)
                            for k in range(NK):
                                TE.matmul(ps_v[:, pvs, :], XnF[:, k, t*128:(t+1)*128],
                                          wbuf[:, k, :], start=(k == 0), stop=(k == NK - 1))
                            tt_ = self.inc(TE.last, "sT")
                            self.w(V, "sT", tt_)
                            V.tensor_copy(Vtm[:, t, half*512:(half+1)*512], ps_v[:, pvs, :])
                            self.inc(V.last, "sV")
                            rot_pv.release()
                    a1b_T = self.cnt["sT"]
                    if self.a_stop <= 2:
                        self.w(G, "sV", self.cnt["sV"])
                        return

                    ssb_free = {0: 0, 1: 0}
                    prev_sm_s = 0
                    prev_ot_v = 0
                    ot2 = [scr[:, 0:D], scr[:, D:2*D]]
                    self.w(SY, "sT", a1b_T)
                    for qc in range(4):
                        for o in range(NK):
                            wos = rot_wo.acquire(SY)
                            ow = self.dma(SY, wflat[:, wos*1024:(wos+1)*1024].rearrange(
                                              "p (k n) -> p k n", n=128),
                                          self.d_wqk[:, o*128:(o+1)*128].rearrange(
                                              "(k p) n -> p k n", p=128), "W%d" % wos)
                            pvs = rot_pv.acquire(TE)
                            self.w(TE, *ow)
                            for k in range(NK):
                                TE.matmul(ps_v[:, pvs, :],
                                          wflat[:, wos*1024 + k*128: wos*1024 + (k+1)*128],
                                          XnF[:, k, qc*512:(qc+1)*512],
                                          start=(k == 0), stop=(k == NK - 1))
                            tt_ = self.inc(TE.last, "sT")
                            rot_wo.release()
                            self.w(V, "sT", tt_)
                            V.tensor_copy(qF[:, o, :], ps_v[:, pvs, :])
                            self.inc(V.last, "sV")
                            rot_pv.release()
                        qf_done = self.cnt["sV"]
                        if self.a_stop <= 3:
                            continue

                        for qt in range(4):
                            q0 = qc * 512 + qt * 128
                            qi = q0 // 128
                            Sbuf = [Ssb, Ssb2][qi % 2]
                            xrs = rot_xt.acquire(S)
                            rw = self.dma(S, xt[:, xrs, :], self.d_x[q0:q0+128, :],
                                          "XR%d" % xrs)
                            V.tensor_scalar(qbv[:], self.iotacol[:], float(q0),
                                            self.slope_b[:], OP.add, OP.mult)
                            V.drain()
                            qb_v = self.inc(V.last, "sV")
                            kmax = q0 + 128
                            nkb = (kmax + 511) // 512
                            for kb in range(nkb):
                                kw = min(512, kmax - kb * 512)
                                pss = rot_ps.acquire(TE)
                                self.w(TE, "sV", qf_done)
                                for dch in range(NK):
                                    TE.matmul(ps_s[:, pss, 0:kw],
                                              qF[:, dch, qt*128:(qt+1)*128],
                                              XnF[:, dch, kb*512:kb*512+kw],
                                              start=(dch == 0), stop=(dch == NK - 1))
                                tt_ = self.inc(TE.last, "sT")
                                self.w(V, "sT", tt_)
                                self.w(V, "sV", qb_v)
                                self.w(V, "sT", ssb_free[qi % 2])
                                V.tensor_scalar(Sbuf[:, kb*512:kb*512+kw], ps_s[:, pss, 0:kw],
                                                SCALE_A, qbv[:], OP.mult, OP.subtract)
                                V.drain()
                                V.tensor_tensor(Sbuf[:, kb*512:kb*512+kw],
                                                Sbuf[:, kb*512:kb*512+kw],
                                                B0[:, kb*512:kb*512+kw], OP.add)
                                V.drain()
                                self.inc(V.last, "sV")
                                rot_ps.release()
                            V.tensor_tensor(Sbuf[:, q0:q0+128], Sbuf[:, q0:q0+128],
                                            amask[:], OP.add)
                            V.drain()
                            V.reduce_max(mx[:], Sbuf[:, 0:kmax], axis=AX.X)
                            V.drain()
                            self.w(V, "sS", prev_sm_s)
                            V.tensor_scalar(nmx[:], mx[:], -1.0, None, OP.mult)
                            V.drain()
                            sm_v = self.inc(V.last, "sV")
                            self.w(S, "sV", sm_v)
                            S.activation(Sbuf[:, 0:kmax], Sbuf[:, 0:kmax], AF.Exp,
                                         bias=nmx[:], scale=1.0, accum_out=sume[:])
                            S.drain()
                            sm_s = self.inc(S.last, "sS")
                            prev_sm_s = sm_s
                            self.w(V, "sS", sm_s)
                            V.reciprocal(rs[:], sume[:])
                            V.drain()
                            rs_v = self.inc(V.last, "sV")
                            if self.a_stop <= 4:
                                self.w(V, "sV", rs_v)
                                V.tensor_copy(lnsc["negm"], rs[:])
                                self.inc(V.last, "sV")
                                rot_xt.release()
                                continue
                            if self.a_stop == 5:
                                ssb_free_T = self.cnt["sT"]
                            nb = q0 // 128 + 1
                            for kb2 in range(nb):
                                trs = rot_tr.acquire(TE)
                                self.w(TE, "sS", sm_s)
                                if kb2 == 0:
                                    self.w(TE, "sV", prev_ot_v)
                                TE.transpose(ps_tr[:, trs, 0:128], Sbuf[:, kb2*128:(kb2+1)*128],
                                             self.identf[:])
                                tt_ = self.inc(TE.last, "sT")
                                self.w(V, "sT", tt_)
                                V.tensor_copy(PT[:, trs, :], ps_tr[:, trs, 0:128])
                                ev = self.inc(V.last, "sV")
                                rot_tr.release()
                                if self.a_stop <= 5:
                                    if kb2 == nb - 1:
                                        pass
                                    continue
                                self.w(TE, "sV", ev)
                                for half in range(2):
                                    TE.matmul(ps_o[:, half, :], PT[:, trs, :],
                                              Vtm[:, kb2, half*512:(half+1)*512],
                                              start=(kb2 == 0), stop=(kb2 == nb - 1))
                            if self.a_stop <= 5:
                                ssb_free_T = self.cnt["sT"]
                                self.w(V, "sV", self.cnt["sV"])
                                V.tensor_copy(lnsc["negm"], rs[:])
                                self.inc(V.last, "sV")
                                rot_xt.release()
                                continue
                            ssb_free_T = self.inc(TE.last, "sT")
                            if self.a_stop <= 6:
                                self.w(V, "sT", ssb_free_T)
                                V.tensor_copy(lnsc["negm"], rs[:])
                                self.inc(V.last, "sV")
                                rot_xt.release()
                                continue
                            ots = ot_i[0] % 2
                            ot_i[0] += 1
                            self.w(V, "SO%d" % ots, so_tgt[ots])
                            self.w(V, "sT", ssb_free_T)
                            self.w(V, "sV", rs_v)
                            self.w(V, *rw)
                            for half in range(2):
                                V.tensor_scalar(ot2[ots][:, half*512:(half+1)*512],
                                                ps_o[:, half, :], rs[:], None, OP.mult)
                            V.drain()
                            V.tensor_tensor(ot2[ots], ot2[ots], xt[:, xrs, :], OP.add)
                            V.drain()
                            prev_ot_v = self.inc(V.last, "sV")
                            rot_xt.release(("sV", prev_ot_v))
                            if self.a_stop <= 7:
                                continue
                            self.w(SY, "sV", prev_ot_v)
                            _, so_tgt[ots] = self.dma(SY, self.d_r1[q0:q0+128, :], ot2[ots],
                                                      "SO%d" % ots)
                    self.w(G, "SO0", self.cnt["SO0"])
                    self.w(G, "SO1", self.cnt["SO1"])
                blk.vector(body)

    def phase_B(self, src, dst, first, tag):
        nc = self.nc
        V, S, TE, SY, G = self.V, self.S, self.TE, self.SY, self.G
        with contextlib.ExitStack() as ph:
            sbl = lambda n, shp, dt=F32: ph.enter_context(nc.sbuf_tensor(tag + n, shp, dt))
            psl = lambda n, shp, dt=F32: ph.enter_context(nc.psum_tensor(tag + n, shp, dt))
            W1 = sbl("_W1", [128, NK, HID], BF16)
            W2 = sbl("_W2", [128, HID // 128, D], BF16)
            if first:
                self.cache_stores = []
                self._ternarize(W1, W2, tag)
            else:
                with nc.Block() as blk0:
                    def body0(_):
                        self.w(SY, "dG", self.cnt["dG"])
                        self.dma(SY, W1[:],
                                 self.d_w1q[:].rearrange("(k p) n -> p k n", p=128), "CST")
                        lw = self.dma(SY, W2[:],
                                      self.d_w2q[:].rearrange("(k p) n -> p k n", p=128),
                                      "CST")
                        self.w(SY, *lw)
                    blk0.vector(body0)

            rt = sbl("_rt", [128, 4, D])
            lnsc = self.ln_scratch(ph, tag + "_ln", quant=True)
            qs = sbl("_qs", [128, 1024])
            qi = psl("_qi", [128, 1024], I32)
            qd = sbl("_qd", [128, 1024])
            qb = sbl("_qb", [128, 1024])
            xqm = sbl("_xqm", [128, D], BF16)
            XqF = sbl("_XqF", [128, NK, 128], BF16)
            hbuf = sbl("_h", [128, HID])
            am2 = sbl("_am2", [128, 1])
            am2c = sbl("_am2c", [128, 1])
            iam2 = sbl("_iam2", [128, 1])
            scq2 = sbl("_scq2", [128, 1])
            hqm = sbl("_hqm", [128, HID], BF16)
            HqF = sbl("_HqF", [128, HID // 128, 128], BF16)
            dq1 = sbl("_dq1", [128, 2])
            dq2 = sbl("_dq2", [128, 2])
            ot = sbl("_ot", [128, 1, D])
            ps_h = psl("_ps_h", [128, 2, 512])
            ps_o = psl("_ps_o", [128, 2, 512])
            ps_tb = psl("_ps_tb", [128, 2, 1024], BF16)

            def round_slices(src_ap, dst_ap, width, v_wait, s_wait):
                for j in range(width // 512):
                    ssl = src_ap[:, j*512:(j+1)*512]
                    self.w(S, "sV", s_wait)
                    S.copy(qi[:], ssl)
                    sq = self.inc(S.last, "sS")
                    self.w(V, "sS", sq)
                    V.tensor_tensor(qd[:], ssl, qi[:], OP.subtract)
                    V.drain()
                    V.scalar_tensor_tensor(qb[:], qd[:], 0.5, qi[:], OP.is_ge, OP.add)
                    V.drain()
                    V.scalar_tensor_tensor(dst_ap[:, j*512:(j+1)*512], qd[:], -0.5,
                                           qb[:], OP.is_lt, OP.subtract)
                    V.drain()
                    s_wait = self.inc(V.last, "sV")
                return self.cnt["sV"]

            levels = 1 if first else 3
            levels2 = 1
            with nc.Block() as blk:
                def body(_):
                    rot_rt = Rot(self, "sV", 4)
                    so_tgt = {0: 0, 1: 0}
                    ot_i = [0]
                    rot_h = Rot(self, "sS", 2)
                    rot_ov = Rot(self, "sV", 2)
                    rot_tb = Rot(self, "sV", 2)
                    st = {"pipe_s": 0, "pipe_v": 0, "xq_tr_T": 0, "hq_tr_T": 0,
                          "q2_free_v": 0, "v_q2": 0}
                    pend = []
                    rt_rel_idx = {}
                    s1 = {}

                    def stage1a(t):
                        sl = rot_rt.acquire(SY)
                        xw = self.dma(SY, rt[:, sl, :], src[t*128:(t+1)*128, :],
                                      ["X0", "X1", "XR0", "XR1"][sl])
                        self.emit_ln(
                            rt[:, sl, :], lnsc, quant=True,
                            pre_wait=[xw],
                            s_pre_wait=[("sS", st["pipe_s"]), ("sV", st["pipe_v"])])
                        V.tensor_scalar(dq1[:, t % 2:t % 2 + 1], lnsc["mxn"][:],
                                        1.0 / 127.0, self.s1b[:], OP.mult, OP.mult)
                        V.drain()
                        v_dq1 = self.inc(V.last, "sV")
                        self.w(S, "sV", v_dq1)
                        self.w(S, "sV", st["q2_free_v"])
                        S.activation(qs[:], lnsc["xc"], AF.Identity, scale=lnsc["comb"])
                        S.drain()
                        S.copy(qi[:], qs[:])
                        sq = self.inc(S.last, "sS")
                        self.w(V, "sS", sq)
                        V.tensor_tensor(qd[:], qs[:], qi[:], OP.subtract)
                        V.drain()
                        V.scalar_tensor_tensor(qb[:], qd[:], 0.5, qi[:], OP.is_ge, OP.add)
                        V.drain()
                        st["pipe_v"] = v_dq1
                        st["pipe_s"] = sq
                        s1[t] = {"sl": sl, "v_dq1": v_dq1}

                    def stage1f(t):
                        self.w(V, "sT", st["xq_tr_T"])
                        if levels == 1:
                            V.scalar_tensor_tensor(xqm[:], qd[:], -0.5,
                                                   qb[:], OP.is_lt, OP.subtract)
                            V.drain()
                        else:
                            V.scalar_tensor_tensor(qs[:], qd[:], 1.5, qb[:], OP.is_ge, OP.add)
                            V.drain()
                            V.scalar_tensor_tensor(qb[:], qd[:], 2.5, qs[:], OP.is_ge, OP.add)
                            V.drain()
                            V.scalar_tensor_tensor(qs[:], qd[:], -0.5, qb[:], OP.is_lt, OP.subtract)
                            V.drain()
                            V.scalar_tensor_tensor(qb[:], qd[:], -1.5, qs[:], OP.is_lt, OP.add)
                            V.drain()
                            V.scalar_tensor_tensor(xqm[:], qd[:], -2.5,
                                                   qb[:], OP.is_lt, OP.add)
                            V.drain()
                        s1[t]["vq"] = self.inc(V.last, "sV")
                        rot_rt.release()
                        rt_rel_idx[t] = len(rot_rt.rel) - 1

                    def xq_transposes(t):
                        for k in range(NK):
                            tbs = rot_tb.acquire(TE)
                            self.w(TE, "sV", s1[t]["vq"])
                            TE.transpose(ps_tb[:, tbs, 0:128], xqm[:, k*128:(k+1)*128],
                                         self.identb[:])
                            tt_ = self.inc(TE.last, "sT")
                            self.w(S, "sT", tt_)
                            S.copy(XqF[:, k, :], ps_tb[:, tbs, 0:128])
                            S.drain()
                            self.inc(S.last, "sS")
                            rot_tb.release(("sS", self.cnt["sS"]))
                        st["xq_tr_T"] = self.cnt["sT"]
                        s1[t]["xqf_done"] = ("sS", self.cnt["sS"])

                    def ffn1_gelu(t, q2=None):
                        for b in range(8):
                            if q2 is not None:
                                q2(b)
                            hs = rot_h.acquire(TE)
                            self.w(TE, *s1[t]["xqf_done"])
                            for k in range(NK):
                                TE.matmul(ps_h[:, hs, :], XqF[:, k, :],
                                          W1[:, k, b*512:(b+1)*512],
                                          start=(k == 0), stop=(k == NK - 1))
                            tt_ = self.inc(TE.last, "sT")
                            self.w(S, "sT", tt_)
                            self.w(S, "sV", s1[t]["v_dq1"])
                            if b == 0:
                                self.w(S, "sV", st["v_q2"])
                            S.activation(hbuf[:, b*512:(b+1)*512], ps_h[:, hs, :],
                                         AF.Gelu, scale=dq1[:, t % 2:t % 2 + 1])
                            self.inc(S.last, "sS")
                            rot_h.release()
                        s1[t]["gelu_done"] = self.cnt["sS"]

                    stage1a(0)
                    stage1f(0)
                    xq_transposes(0)
                    stage1a(1)
                    stage1f(1)
                    ffn1_gelu(0)
                    for t in range(NT):
                        sl = s1[t]["sl"]
                        if pend:
                            pend.pop(0)()
                        self.w(V, "sS", s1[t]["gelu_done"])
                        V.reduce_max(am2[:], hbuf[:], axis=AX.X, apply_absolute_value=True)
                        V.drain()
                        V.tensor_scalar(am2c[:], am2[:], 1e-5, None, OP.max)
                        V.drain()
                        V.reciprocal(iam2[:], am2c[:])
                        V.drain()
                        V.tensor_scalar(scq2[:], iam2[:], 127.0, None, OP.mult)
                        V.drain()
                        V.tensor_scalar(dq2[:, (t % 2):(t % 2)+1], am2c[:], 1.0 / 127.0,
                                        self.s2b[:], OP.mult, OP.mult)
                        V.drain()
                        v_q2 = self.inc(V.last, "sV")
                        st["v_q2"] = v_q2
                        vq_next = s1[t + 1]["vq"] if t + 1 < NT else 0
                        svh = {0: max(v_q2, vq_next), 1: max(v_q2, vq_next)}

                        def q2slice(j2, t=t, svh=svh):
                            h = j2 % 2
                            hs_ = slice(h*512, (h+1)*512)
                            c0 = j2 * 512
                            self.w(S, "sV", svh[h])
                            S.activation(qs[:, hs_], hbuf[:, c0:c0+512],
                                         AF.Identity, scale=scq2[:])
                            S.drain()
                            S.copy(qi[:, hs_], qs[:, hs_])
                            sq = self.inc(S.last, "sS")
                            self.w(V, "sS", sq)
                            V.tensor_tensor(qd[:, hs_], qs[:, hs_], qi[:, hs_],
                                            OP.subtract)
                            V.drain()
                            V.scalar_tensor_tensor(qb[:, hs_], qd[:, hs_], 0.5,
                                                   qi[:, hs_], OP.is_ge, OP.add)
                            V.drain()
                            if j2 == 0:
                                self.w(V, "sT", st["hq_tr_T"])
                            V.scalar_tensor_tensor(hqm[:, c0:c0+512], qd[:, hs_], -0.5,
                                                   qb[:, hs_], OP.is_lt, OP.subtract)
                            V.drain()
                            svh[h] = self.inc(V.last, "sV")

                        if t + 1 < NT:
                            xq_transposes(t + 1)
                            ffn1_gelu(t + 1, q2=q2slice)
                        else:
                            for j2 in range(8):
                                q2slice(j2)
                        st["q2_free_v"] = max(svh[0], svh[1])
                        v_hq = self.cnt["sV"]
                        for k in range(HID // 128):
                            tbs = rot_tb.acquire(TE)
                            self.w(TE, "sV", v_hq)
                            TE.transpose(ps_tb[:, tbs, 0:128], hqm[:, k*128:(k+1)*128],
                                         self.identb[:])
                            tt_ = self.inc(TE.last, "sT")
                            self.w(S, "sT", tt_)
                            S.copy(HqF[:, k, :], ps_tb[:, tbs, 0:128])
                            S.drain()
                            self.inc(S.last, "sS")
                            rot_tb.release(("sS", self.cnt["sS"]))
                        st["hq_tr_T"] = self.cnt["sT"]
                        hqf_done = ("sS", self.cnt["sS"])
                        ots = 0
                        ffn2_T = []
                        for half in range(2):
                            ovs = rot_ov.acquire(TE)
                            self.w(TE, *hqf_done)
                            for k in range(HID // 128):
                                TE.matmul(ps_o[:, ovs, :], HqF[:, k, :],
                                          W2[:, k, half*512:(half+1)*512],
                                          start=(k == 0), stop=(k == HID // 128 - 1))
                            ffn2_T.append((ovs, self.inc(TE.last, "sT")))
                        if t + 2 < NT:
                            stage1a(t + 2)
                            stage1f(t + 2)
                        if first:
                            for _ in range(3):
                                if self.cache_stores:
                                    dc, wsrc, ev = self.cache_stores.pop(0)
                                    self.w(SY, "sV", ev)
                                    SY.dma_start(out=dc, in_=wsrc).then_inc(
                                        self.sem["dG"], 16)
                                    self.bump("dG", 16)

                        def stageB(t=t, sl=sl, ots=ots, ffn2_T=ffn2_T):
                            first_ot = True
                            for half, (ovs, tt_) in enumerate(ffn2_T):
                                self.w(V, "sT", tt_)
                                if first_ot:
                                    self.w(V, "SO%d" % ots, so_tgt[ots])
                                    first_ot = False
                                V.tensor_scalar(ot[:, ots, half*512:(half+1)*512],
                                                ps_o[:, ovs, :], dq2[:, (t % 2):(t % 2)+1],
                                                None, OP.mult)
                                self.inc(V.last, "sV")
                                rot_ov.release()
                            V.drain()
                            V.tensor_tensor(ot[:, ots, :], ot[:, ots, :], rt[:, sl, :], OP.add)
                            V.drain()
                            ot_v = self.inc(V.last, "sV")
                            rot_rt.rel[rt_rel_idx[t]] = ("sV", ot_v)
                            self.w(SY, "sV", ot_v)
                            _, so_tgt[ots] = self.dma(SY, dst[t*128:(t+1)*128, :],
                                                      ot[:, ots, :], "SO%d" % ots)
                        pend.append(stageB)
                    while pend:
                        pend.pop(0)()
                    self.w(SY, "SO0", self.cnt["SO0"])
                    self.w(SY, "SO1", self.cnt["SO1"])
                    if first:
                        self.w(SY, "dG", self.cnt["dG"])
                blk.vector(body)

    def _ternarize(self, W1, W2, tag):
        nc = self.nc
        V, S, TE, SY, G = self.V, self.S, self.TE, self.SY, self.G
        with contextlib.ExitStack() as p2:
            sbl = lambda n, shp, dt=F32: p2.enter_context(nc.sbuf_tensor(tag + "T" + n, shp, dt))
            wtmp = sbl("_wtmp", [128, 2, HID])
            acc = sbl("_acc", [128, 1])
            part = sbl("_part", [128, 1])
            zt = sbl("_zt", [128, HID])
            tp = sbl("_tp", [128, HID])
            onec = sbl("_onec", [128, 1])
            psm = p2.enter_context(nc.psum_tensor(tag + "T_psm", [128, 2], F32))

            with nc.Block() as blk:
                def body(_):
                    V.memset(onec[:], 1.0)
                    V.memset(acc[:], 0.0)
                    V.drain()
                    self.inc(V.last, "sV")
                    rot_w = Rot(self, "sV", 2)
                    plans = [(self.d_w1t, HID, 8, 0), (self.d_w2t, D, 32, 1)]
                    for dram, width, ntile, idx in plans:
                        for i in range(ntile):
                            sl = rot_w.acquire(SY)
                            lw = self.dma(SY, wtmp[:, sl, 0:width],
                                          dram[i*128:(i+1)*128, :], "X%d" % sl)
                            self.w(V, *lw)
                            V.reduce_sum(part[:], wtmp[:, sl, 0:width], axis=AX.X,
                                         apply_absolute_value=True)
                            V.drain()
                            V.tensor_tensor(acc[:], acc[:], part[:], OP.add)
                            V.drain()
                            self.inc(V.last, "sV")
                            rot_w.release()
                        av = self.cnt["sV"]
                        self.w(TE, "sV", av)
                        TE.matmul(psm[0:1, idx:idx+1], acc[:], onec[:], start=True, stop=True)
                        tt_ = self.inc(TE.last, "sT")
                        self.w(V, "sT", tt_)
                        V.tensor_scalar(self.s12row[0:1, idx:idx+1], psm[0:1, idx:idx+1],
                                        1.0 / (D * HID), None, OP.mult)
                        V.drain()
                        V.memset(acc[:], 0.0)
                        V.drain()
                        self.inc(V.last, "sV")
                    V.tensor_scalar(self.s12row[:], self.s12row[:], 1e-5, None, OP.max)
                    V.drain()
                    sv = self.inc(V.last, "sV")
                    self.w(S, "sV", sv)
                    S.dma_start(out=self.d_s12[:], in_=self.s12row[:]).then_inc(self.sem["dG"], 16)
                    self.bump("dG", 16)
                    self.w(TE, "sV", sv)
                    TE.matmul(psm[:, 0:1], self.ones1[:], self.s12row[0:1, 0:1], start=True, stop=True)
                    TE.matmul(psm[:, 1:2], self.ones1[:], self.s12row[0:1, 1:2], start=True, stop=True)
                    tt_ = self.inc(TE.last, "sT")
                    self.w(V, "sT", tt_)
                    V.tensor_copy(self.s1b[:], psm[:, 0:1])
                    V.tensor_copy(self.s2b[:], psm[:, 1:2])
                    V.drain()
                    V.reciprocal(self.is1b[:], self.s1b[:])
                    V.reciprocal(self.is2b[:], self.s2b[:])
                    V.drain()
                    bv = self.inc(V.last, "sV")

                    plans2 = [(self.d_w1t, self.d_w1q, W1, HID, 8, self.is1b),
                              (self.d_w2t, self.d_w2q, W2, D, 32, self.is2b)]
                    zt_free = 0
                    for dram, dcache, Wsb, width, ntile, isb in plans2:
                        for i in range(ntile):
                            sl = rot_w.acquire(SY)
                            lw = self.dma(SY, wtmp[:, sl, 0:width],
                                          dram[i*128:(i+1)*128, :], "X%d" % sl)
                            self.w(S, *lw)
                            self.w(S, "sV", max(bv, zt_free))
                            S.activation(zt[:, 0:width], wtmp[:, sl, 0:width],
                                         AF.Identity, scale=isb[:])
                            s_z = self.inc(S.last, "sS")
                            self.w(V, "sS", s_z)
                            V.tensor_scalar(tp[:, 0:width], zt[:, 0:width], 0.5, None, OP.is_gt)
                            V.drain()
                            V.scalar_tensor_tensor(Wsb[:, i, :], zt[:, 0:width], -0.5,
                                                   tp[:, 0:width], OP.is_lt, OP.subtract)
                            V.drain()
                            wv_ = self.inc(V.last, "sV")
                            zt_free = wv_
                            rot_w.release()
                            self.cache_stores.append(
                                (dcache[i*128:(i+1)*128, :], Wsb[:, i, :], wv_))
                blk.vector(body)

    def phase_C(self):
        nc = self.nc
        V, S, TE, SY, G = self.V, self.S, self.TE, self.SY, self.G
        CH = 512
        NCH = T // CH
        NS = CH // 128
        with contextlib.ExitStack() as ph:
            sbl = lambda n, shp, dt=F32: ph.enter_context(nc.sbuf_tensor("C" + n, shp, dt))
            psl = lambda n, shp, dt=F32: ph.enter_context(nc.psum_tensor("C" + n, shp, dt))
            XnFc = sbl("_XnF", [128, NK, CH])
            qFd = sbl("_qF", [128, NK, CH])
            kFd = sbl("_kF", [128, NK, CH])
            qS = sbl("_qS", [128, NK, 128])
            kS = sbl("_kS", [128, NK, 128])
            ktok = sbl("_ktok", [128, NS, D])
            Vp = sbl("_Vp", [128, NS, 64 * 17])
            wbufC = sbl("_w", [128, 2, NK, 512])
            Oc = sbl("_Oc", [128, NS, D])
            OF = sbl("_OF", [128, NK, 128])
            rtC = sbl("_rt", [128, NS, D])
            stateSB = sbl("_st", [128, 16 * 68])
            triu4 = sbl("_triu4", [128, 512])
            PTb = sbl("_PT", [128, 2, 512])
            lnsc = self.ln_scratch(ph, "C_ln")
            xn = sbl("_xn", [128, D])
            rden = sbl("_rden", [8, 512])
            zero512 = sbl("_zero512", [128, 512])
            rd4 = sbl("_rd4", [128, 4])
            bvb = sbl("_bvb", [128, D])
            bob = sbl("_bob", [128, D])
            brow = sbl("_brow", [1, D])
            ps_quad = psl("_ps_quad", [128, 4, 512])
            ps_N = psl("_ps_N", [128, 2, 512])
            ps_st = psl("_ps_state", [128, 512])
            ps_d = psl("_ps_d", [128, 512])


            with nc.Block() as blk:
                def body(_):
                    rot_tr = Rot(self, "sV", 2)
                    rot_w = Rot(self, "sT", 2)
                    rot_p = Rot(self, "sV", 2)
                    rot_N = Rot(self, "sV", 2)
                    rot_PT = Rot(self, "sT", 2)
                    rot_ot = Rot(self, "dG", 2)
                    rot_rt = Rot(self, "sV", 4)

                    V.memset(zero512[:], 0.0)
                    V.drain()
                    V.memset(stateSB[:], 0.0)
                    V.drain()
                    V.memset(qS[96:128, :, :], 0.0)
                    V.drain()
                    V.memset(kS[96:128, :, :], 0.0)
                    V.drain()
                    self.inc(V.last, "sV")
                    self.dma(SY, triu4[:], self.d_triu4[:], "CST")
                    for row_d, dst in [(self.d_bvrow, bvb), (self.d_borow, bob)]:
                        self.w(SY, "sT", self.cnt["sT"])
                        bw = self.dma(SY, brow[:], row_d[:], "CST")
                        self.w(TE, *bw)
                        for half in range(2):
                            TE.matmul(ps_d[:, 0:512], self.ones1[:],
                                      brow[:, half*512:(half+1)*512], start=True, stop=True)
                            tt_ = self.inc(TE.last, "sT")
                            self.w(V, "sT", tt_)
                            V.tensor_copy(dst[:, half*512:(half+1)*512], ps_d[:, 0:512])
                            fv = self.inc(V.last, "sV")
                            self.w(TE, "sV", fv)
                    bias_done = self.cnt["sV"]

                    st_upd_T = 0
                    chunk_T = 0
                    last_s = 0

                    for c in range(NCH):
                        for s_ in range(NS):
                            t0 = c * NS + s_
                            sl = rot_rt.acquire(SY)
                            xw = self.dma(SY, rtC[:, sl, :],
                                          self.d_r2[t0*128:(t0+1)*128, :],
                                          ["X0", "X1", "XR0", "XR1"][sl])
                            v_done, s_done = self.emit_ln(
                                rtC[:, sl, :], lnsc,
                                pre_wait=[xw, ("sS", last_s), ("sT", chunk_T)])
                            self.w(S, "sV", v_done)
                            self.w(S, "sT", self.cnt["sT"])
                            S.activation(xn[:], lnsc["xc"], AF.Identity, scale=lnsc["rstd"])
                            S.drain()
                            s_xn = self.inc(S.last, "sS")
                            last_s = s_xn
                            for k in range(NK):
                                trs = rot_tr.acquire(TE)
                                self.w(TE, "sS", s_xn)
                                TE.transpose(ps_quad[:, trs, 0:128], xn[:, k*128:(k+1)*128],
                                             self.identf[:])
                                tt_ = self.inc(TE.last, "sT")
                                self.w(V, "sT", tt_)
                                V.tensor_copy(XnFc[:, k, s_*128:(s_+1)*128], ps_quad[:, trs, 0:128])
                                self.inc(V.last, "sV")
                                rot_tr.release()
                        xnf_done = self.cnt["sV"]
                        if self.c_stop <= 1:
                            continue

                        for half in range(2):
                            ws = rot_w.acquire(SY)
                            w_ld = self.dma(SY, wbufC[:, ws, :, :],
                                            self.d_wq[:, half*512:(half+1)*512].rearrange(
                                                "(k p) n -> p k n", p=128), "W%d" % ws)
                            for oo in range(4):
                                o = half * 4 + oo
                                pps = rot_p.acquire(TE)
                                self.w(TE, *w_ld)
                                self.w(TE, "sV", xnf_done)
                                for k in range(NK):
                                    TE.matmul(ps_quad[:, 2 + pps, :], wbufC[:, ws, k, oo*128:(oo+1)*128],
                                              XnFc[:, k, :], start=(k == 0), stop=(k == NK - 1))
                                tt_ = self.inc(TE.last, "sT")
                                self.w(S, "sT", tt_)
                                S.activation(qFd[:, o, :], ps_quad[:, 2 + pps, :], AF.Exp,
                                             bias=self.bebq[:, o:o+1], scale=self.be_b[:])
                                S.drain()
                                sev = self.inc(S.last, "sS")
                                rot_p.release(("sS", sev))
                            rot_w.release()
                        q_done_s = self.cnt["sS"]
                        if self.c_stop <= 2:
                            continue

                        knv = [0]
                        pend_k = []
                        for half in range(2):
                            ws = rot_w.acquire(SY)
                            w_ld = self.dma(SY, wbufC[:, ws, :, :],
                                            self.d_wk[:, half*512:(half+1)*512].rearrange(
                                                "(k p) n -> p k n", p=128), "W%d" % ws)
                            for oo in range(4):
                                o = half * 4 + oo
                                pps = rot_p.acquire(TE)
                                self.w(TE, *w_ld)
                                self.w(TE, "sV", xnf_done)
                                for k in range(NK):
                                    TE.matmul(ps_quad[:, 2 + pps, :], wbufC[:, ws, k, oo*128:(oo+1)*128],
                                              XnFc[:, k, :], start=(k == 0), stop=(k == NK - 1))
                                tt_ = self.inc(TE.last, "sT")
                                self.w(S, "sT", tt_)
                                S.activation(kFd[:, o, :], ps_quad[:, 2 + pps, :], AF.Exp,
                                             bias=self.bebk[:, o:o+1], scale=self.be_b[:])
                                S.drain()
                                sev = self.inc(S.last, "sS")
                                rot_p.release(("sS", sev))

                                def kchain(o=o, sev=sev):
                                    self.w(TE, "sS", sev)
                                    self.w(TE, "sV", knv[0])
                                    TE.matmul(ps_d[0:8, 0:512], self.bd8[:], kFd[:, o, :],
                                              start=True, stop=True)
                                    t1 = self.inc(TE.last, "sT")
                                    self.w(V, "sT", t1)
                                    V.reciprocal(rden[:], ps_d[0:8, 0:512])
                                    V.drain()
                                    rv = self.inc(V.last, "sV")
                                    self.w(TE, "sV", rv)
                                    TE.matmul(ps_d[:, 0:512], self.bd8t[:], rden[:],
                                              start=True, stop=True)
                                    t2 = self.inc(TE.last, "sT")
                                    self.w(V, "sT", t2)
                                    V.tensor_tensor(kFd[:, o, :], kFd[:, o, :], ps_d[:, 0:512],
                                                    OP.mult)
                                    V.drain()
                                    knv[0] = self.inc(V.last, "sV")
                                pend_k.append(kchain)
                                if len(pend_k) > 1:
                                    pend_k.pop(0)()
                            rot_w.release()
                        while pend_k:
                            pend_k.pop(0)()
                        k_done_v = self.cnt["sV"]
                        if self.c_stop <= 3:
                            continue

                        for s_ in range(NS):
                            for o in range(NK):
                                trs = rot_tr.acquire(TE)
                                self.w(TE, "sV", k_done_v)
                                TE.transpose(ps_quad[:, trs, 0:128], kFd[:, o, s_*128:(s_+1)*128],
                                             self.identf[:])
                                tt_ = self.inc(TE.last, "sT")
                                self.w(V, "sT", tt_)
                                V.tensor_copy(ktok[:, s_, o*128:(o+1)*128], ps_quad[:, trs, 0:128])
                                self.inc(V.last, "sV")
                                rot_tr.release()
                        ktok_done = self.cnt["sV"]
                        if self.c_stop <= 4:
                            continue

                        self.w(V, "sV", bias_done)
                        V.memset(Vp[:].rearrange("p s (h e) -> p s h e", e=17)[:, :, :, 16:17],
                                 1.0)
                        V.drain()
                        ones_v = self.inc(V.last, "sV")
                        for half in range(2):
                            ws = rot_w.acquire(SY)
                            w_ld = self.dma(SY, wbufC[:, ws, :, :],
                                            self.d_wv[:, half*512:(half+1)*512].rearrange(
                                                "(k p) n -> p k n", p=128), "W%d" % ws)
                            for s_ in range(NS):
                                pps = rot_p.acquire(TE)
                                self.w(TE, *w_ld)
                                self.w(TE, "sV", xnf_done)
                                for k in range(NK):
                                    TE.matmul(ps_quad[:, 2 + pps, :],
                                              XnFc[:, k, s_*128:(s_+1)*128],
                                              wbufC[:, ws, k, :],
                                              start=(k == 0), stop=(k == NK - 1))
                                tt_ = self.inc(TE.last, "sT")
                                self.w(V, "sT", tt_)
                                self.w(V, "sV", ones_v)
                                vview = Vp[:, s_, :].rearrange("p (h e) -> p h e", e=17)
                                pview = ps_quad[:, 2 + pps, :].rearrange("p (h e) -> p h e", e=16)
                                bview = (bvb[:, half*512:(half+1)*512]
                                         .rearrange("p (h e) -> p h e", e=16))
                                V.tensor_tensor(vview[:, half*32:(half+1)*32, 0:16],
                                                pview, bview, OP.add)
                                V.drain()
                                self.inc(V.last, "sV")
                                rot_p.release()
                            rot_w.release()
                        vp_done = self.cnt["sV"]
                        if self.c_stop <= 5:
                            continue
                        prev_pt_v = vp_done

                        scan_sub_T = 0
                        for s_ in range(NS):
                            cg = c * NS + s_
                            self.w(S, "sS", q_done_s)
                            self.w(S, "sV", vp_done)
                            self.w(S, "sT", scan_sub_T)
                            self.dma(S, qS[0:112, :, :],
                                     qFd[16:128, :, s_*128:(s_+1)*128], "SH")
                            shift_ld = self.dma(S, kS[0:112, :, :],
                                                kFd[16:128, :, s_*128:(s_+1)*128], "SH")
                            if cg > 0:
                                self.w(V, "sT", st_upd_T)
                                for a4 in range(4):
                                    V.tensor_copy(
                                        stateSB[32*a4:32*a4+16, :].rearrange(
                                            "p (g e) -> p g e", e=68)[:, :, 17*a4:17*a4+17],
                                        ps_st[32*a4:32*a4+16, 0:272].rearrange(
                                            "p (g e) -> p g e", e=17))
                                V.drain()
                                st_ev = self.inc(V.last, "sV")
                            else:
                                st_ev = self.cnt["sV"]
                            if cg == 0:
                                self.w(TE, "sV", st_ev)
                                TE.matmul(ps_st[0:128, 0:512], self.identf[:],
                                          zero512[:], start=True, stop=True)
                                st_upd_T = self.inc(TE.last, "sT")
                            for o in range(NK):
                                for par in range(2):
                                    g = 2 * o + par
                                    if par == 0:
                                        qT = qFd[:, o, s_*128:(s_+1)*128]
                                        kT = kFd[:, o, s_*128:(s_+1)*128]
                                    else:
                                        qT = qS[:, o, :]
                                        kT = kS[:, o, :]
                                    self.w(TE, "sS", q_done_s)
                                    self.w(TE, "sV", k_done_v)
                                    self.w(TE, *shift_ld)
                                    self.w(TE, "sV", prev_pt_v)
                                    for a in range(4):
                                        bofs = 32 * a
                                        TE.matmul(ps_quad[:, a, 0:128],
                                                  kT[bofs:bofs+16, :],
                                                  qT[bofs:bofs+16, :],
                                                  start=True, stop=True,
                                                  tile_position=(bofs, 0),
                                                  skip_group_check=True)
                                    t_s = self.inc(TE.last, "sT")
                                    pts = rot_PT.acquire(V)
                                    self.w(V, "sT", t_s)
                                    V.tensor_tensor(
                                        PTb[:, pts, :].rearrange("p (a e) -> p a e", e=128),
                                        ps_quad[:, 0:4, 0:128],
                                        triu4[:].rearrange("p (a e) -> p a e", e=128),
                                        OP.mult)
                                    V.drain()
                                    pt_v = self.inc(V.last, "sV")
                                    prev_pt_v = pt_v
                                    Ns = rot_N.acquire(TE)
                                    self.w(TE, "sV", vp_done)
                                    self.w(TE, "sV", st_ev)
                                    first = True
                                    if cg > 0:
                                        TE.matmul(ps_N[:, Ns, 0:68], qT,
                                                  stateSB[:, g*68:(g+1)*68],
                                                  start=True, stop=False,
                                                  skip_group_check=True)
                                        first = False
                                    self.w(TE, "sV", ktok_done)
                                    for a in range(4):
                                        h = 8 * o + par + 2 * a
                                        bofs = 32 * a
                                        TE.matmul(ps_st[bofs:bofs+16, g*17:(g+1)*17],
                                                  ktok[:, s_, h*16:(h+1)*16],
                                                  Vp[:, s_, h*17:(h+1)*17],
                                                  start=False, stop=False,
                                                  tile_position=(0, bofs),
                                                  skip_group_check=True)
                                    st_upd_T = self.inc(TE.last, "sT")
                                    self.w(TE, "sV", pt_v)
                                    for a in range(4):
                                        h = 8 * o + par + 2 * a
                                        TE.matmul(ps_N[:, Ns, a*17:(a+1)*17],
                                                  PTb[:, pts, a*128:(a+1)*128],
                                                  Vp[:, s_, h*17:(h+1)*17],
                                                  start=first, stop=(a == 3),
                                                  skip_group_check=True)
                                        first = False
                                    t_n = self.inc(TE.last, "sT")
                                    rot_PT.release()
                                    self.w(V, "sT", t_n)
                                    if o == 0 and par == 0 and c > 0 and self.c_stop > 7:
                                        self.w(V, "SO%d" % s_, ot_store[s_])
                                    nview = ps_N[:, Ns, 0:68].rearrange("p (h e) -> p h e", e=17)
                                    V.reciprocal(rd4[:], nview[:, 0:4, 16])
                                    V.drain()
                                    h0 = 8 * o + par
                                    ov = Oc[:, s_, :].rearrange(
                                        "p (hh e) -> p hh e", e=16)[:, h0:h0+7:2, :]
                                    rb = rd4[:].unsqueeze(-1).broadcast_to((128, 4, 16))
                                    V.tensor_tensor(ov, nview[:, 0:4, 0:16], rb, OP.mult)
                                    V.drain()
                                    self.inc(V.last, "sV")
                                    rot_N.release()
                            scan_sub_T = self.cnt["sT"]
                        oc_done = self.cnt["sV"]
                        if self.c_stop <= 7:
                            chunk_T = self.cnt["sT"]
                            continue

                        if c == 0:
                            ot_store = {}
                        w_lds = []
                        wss = []
                        for half in range(2):
                            ws = rot_w.acquire(SY)
                            w_lds.append(self.dma(
                                SY, wbufC[:, ws, :, :],
                                self.d_wo[:, half*512:(half+1)*512].rearrange(
                                    "(k p) n -> p k n", p=128), "W%d" % ws))
                            wss.append(ws)
                        for s_ in range(NS):
                            t0 = c * NS + s_
                            for k in range(NK):
                                trs = rot_tr.acquire(TE)
                                self.w(TE, "sV", oc_done)
                                TE.transpose(ps_quad[:, trs, 0:128], Oc[:, s_, k*128:(k+1)*128],
                                             self.identf[:])
                                tt_ = self.inc(TE.last, "sT")
                                self.w(V, "sT", tt_)
                                V.tensor_copy(OF[:, k, :], ps_quad[:, trs, 0:128])
                                self.inc(V.last, "sV")
                                rot_tr.release()
                            of_done = self.cnt["sV"]
                            for half in range(2):
                                pps = rot_p.acquire(TE)
                                self.w(TE, *w_lds[half])
                                self.w(TE, "sV", of_done)
                                for k in range(NK):
                                    TE.matmul(ps_quad[:, 2 + pps, :], OF[:, k, :],
                                              wbufC[:, wss[half], k, :],
                                              start=(k == 0), stop=(k == NK - 1))
                                tt_ = self.inc(TE.last, "sT")
                                self.w(V, "sT", tt_)
                                self.w(V, "sV", bias_done)
                                V.tensor_tensor(Oc[:, s_, half*512:(half+1)*512],
                                                ps_quad[:, 2 + pps, :],
                                                bob[:, half*512:(half+1)*512], OP.add)
                                V.drain()
                                self.inc(V.last, "sV")
                                rot_p.release()
                            V.tensor_tensor(Oc[:, s_, :], Oc[:, s_, :],
                                            rtC[:, s_, :], OP.add)
                            V.drain()
                            ot_v = self.inc(V.last, "sV")
                            rot_rt.release(("sV", ot_v))
                            self.w(SY, "sV", ot_v)
                            _, ot_store[s_] = self.dma(
                                SY, self.d_r3[t0*128:(t0+1)*128, :],
                                Oc[:, s_, :], "SO%d" % s_)
                        for half in range(2):
                            rot_w.release()
                        chunk_T = self.cnt["sT"]
                    for i4 in range(4):
                        self.w(SY, "SO%d" % i4, self.cnt["SO%d" % i4])
                blk.vector(body)


    def copy_dram(self, src, dst, tagn):
        nc = self.nc
        with contextlib.ExitStack() as ph:
            buf = ph.enter_context(nc.sbuf_tensor("cp%s" % tagn, [128, 2, D], F32))
            with nc.Block() as blk:
                def body(_):
                    so = {0: 0, 1: 0}
                    for t in range(NT):
                        sl = t % 2
                        self.w(self.SY, "SO%d" % sl, so[sl])
                        lw = self.dma(self.SY, buf[:, sl, :], src[t*128:(t+1)*128, :],
                                      "X%d" % sl)
                        self.w(self.S, *lw)
                        _, so[sl] = self.dma(self.S, dst[t*128:(t+1)*128, :], buf[:, sl, :],
                                             "SO%d" % sl)
                    self.w(self.S, "SO0", self.cnt["SO0"])
                    self.w(self.S, "SO1", self.cnt["SO1"])
                blk.vector(body)

    def build(self):
        self.setup_consts()
        if "A" in self.phases:
            self.phase_A()
        else:
            self.copy_dram(self.d_x, self.d_r1, "A")
        if "B" in self.phases:
            self.phase_B(self.d_r1, self.d_r2, first=True, tag="B")
        else:
            self.copy_dram(self.d_r1, self.d_r2, "B")
        if "C" in self.phases:
            self.phase_C()
        else:
            self.copy_dram(self.d_r2, self.d_r3, "C")
        if "D" in self.phases:
            self.phase_B(self.d_r3, self.d_y, first=("B" not in self.phases), tag="D")
        else:
            self.copy_dram(self.d_r3, self.d_y, "D")
        self.st.close()
        return self.nc


def host_consts():
    import ml_dtypes
    eye = np.eye(128, dtype=np.float32)
    eyeb = np.eye(128, dtype=ml_dtypes.bfloat16)
    iota128 = np.broadcast_to(np.arange(T, dtype=np.float32)[None, :], (128, T)).copy()
    iotacol = np.arange(128, dtype=np.float32)[:, None].copy()
    q = np.arange(128)[:, None]
    kk = np.arange(128)[None, :]
    amask = np.where(kk <= q, 0.0, NEG).astype(np.float32)
    tri = (np.arange(128)[:, None] <= np.arange(128)[None, :]).astype(np.float32)
    triu4 = np.concatenate([tri] * 4, axis=1).astype(np.float32)
    bd8 = np.zeros((128, 8), dtype=np.float32)
    for h in range(8):
        bd8[h*16:(h+1)*16, h] = 1.0
    bd8t = np.ascontiguousarray(bd8.T)
    ones1 = np.ones((1, 128), dtype=np.float32)
    return dict(identf=eye, identb=eyeb, iota128=iota128, iotacol=iotacol,
                amask=amask, triu4=triu4, bd8=bd8, bd8t=bd8t, ones1=ones1)


def stage_inputs(inputs, core):
    c = host_consts()
    f32 = np.float32
    m = dict(c)
    m["x"] = np.ascontiguousarray(np.asarray(inputs["L0_attn_a_x"], dtype=f32)[core])
    m["wqk"] = np.ascontiguousarray(np.asarray(inputs["L0_attn_a_W_qk"], dtype=f32))
    m["wva"] = np.ascontiguousarray(np.asarray(inputs["L0_attn_a_W_v"], dtype=f32).T)
    m["alibi"] = np.asarray(inputs["L0_attn_a_alibi_slope"], dtype=f32).reshape(1, 1).copy()
    m["wq"] = np.ascontiguousarray(np.asarray(inputs["L0_attn_b_Wq"], dtype=f32).T)
    m["wk"] = np.ascontiguousarray(np.asarray(inputs["L0_attn_b_Wk"], dtype=f32).T)
    m["wv"] = np.ascontiguousarray(np.asarray(inputs["L0_attn_b_Wv"], dtype=f32).T)
    m["wo"] = np.ascontiguousarray(np.asarray(inputs["L0_attn_b_Wo"], dtype=f32).T)
    m["bqF"] = np.ascontiguousarray(np.asarray(inputs["L0_attn_b_bq"], dtype=f32).reshape(8, 128).T)
    m["bkF"] = np.ascontiguousarray(np.asarray(inputs["L0_attn_b_bk"], dtype=f32).reshape(8, 128).T)
    m["bvrow"] = np.asarray(inputs["L0_attn_b_bv"], dtype=f32).reshape(1, D).copy()
    m["borow"] = np.asarray(inputs["L0_attn_b_bo"], dtype=f32).reshape(1, D).copy()
    m["beta"] = np.asarray(inputs["L0_attn_b_beta"], dtype=f32).reshape(1, 1).copy()
    m["w1t"] = np.ascontiguousarray(np.asarray(inputs["L0_ffn_w1"], dtype=f32).T)
    m["w2t"] = np.ascontiguousarray(np.asarray(inputs["L0_ffn_w2"], dtype=f32).T)
    return m


def kernel(**inputs):
    import numpy as np
    from concourse.bass_utils import run_bass_kernel_spmd
    n = 8
    nc = MK(phases=("A", "B", "C", "D")).build()
    in_maps = [stage_inputs(inputs, c) for c in range(n)]
    res = run_bass_kernel_spmd(nc, in_maps, core_ids=list(range(n)))
    return np.stack([np.asarray(res.results[c]["y"]) for c in range(n)], axis=0)
